# version 10
# baseline (speedup 1.0000x reference)
import numpy as np
import ml_dtypes
import concourse.bass as bass
import concourse.mybir as mybir
from concourse.bass_utils import run_bass_kernel_spmd
from concourse.bass_core_rust import IndirectOffsetOnAxis

F32 = mybir.dt.float32
BF16 = mybir.dt.bfloat16
I32 = mybir.dt.int32
U8 = mybir.dt.uint8
AF = mybir.ActivationFunctionType
ALU = mybir.AluOpType
AX = mybir.AxisListType.X

T = 4096
D = 1024
H = 16
HD = 64
NT = T // 128
E = 8
FFN = 4096
CAP = 1024
SLOTS = E * CAP
M0 = 25.0
ROPE_BASE = 10000.0
KB = 1024


class Ctr:
    __slots__ = ("h", "n")

    def __init__(self, h):
        self.h = h
        self.n = 0


class Sched:
    def __init__(self):
        self.recs = []

    def op(self, eng, fn, waits=(), incs=()):
        w = [(c.h, v) for (c, v) in waits if v > 0]
        i = []
        for c, k in incs:
            c.n += k
            i.append((c.h, k))
        self.recs.append((eng, fn, w, i))

    def replay(self, name, eng):
        known = {}
        for e, fn, waits, incs in self.recs:
            if e != name:
                continue
            for sh, v in waits:
                if known.get(id(sh), -1) < v:
                    eng.wait_ge(sh, v)
                    known[id(sh)] = v
            ins = fn(eng)
            for sh, k in incs:
                ins.then_inc(sh, k)


def _perm64():
    return np.concatenate([np.arange(0, 64, 2), np.arange(1, 64, 2)])


def prep_shared(Wqkv, Wproj, router_w, w1, w2):
    p64 = _perm64()
    cols = np.empty((H, 192), dtype=np.int64)
    for h in range(H):
        cols[h] = np.concatenate(
            [64 * h + p64, 1024 + 64 * h + p64, 2048 + 64 * h + np.arange(64)]
        )
    wq = Wqkv.reshape(8, 128, 3072)
    wqkv_r = np.ascontiguousarray(
        wq[:, :, cols].transpose(2, 1, 0, 3).reshape(H, 128, 8 * 192)
    )
    wproj_r = np.ascontiguousarray(
        Wproj.reshape(8, 128, 1024).transpose(1, 0, 2).reshape(128, 8192)
    )
    rw_r = np.ascontiguousarray(
        router_w.reshape(8, 128, 8).transpose(1, 0, 2).reshape(128, 64)
    )
    w1v = w1.reshape(E, 8, 128, 2 * FFN)
    idx = np.empty((32, 256), dtype=np.int64)
    for c in range(32):
        idx[c, :128] = c * 128 + np.arange(128)
        idx[c, 128:] = FFN + c * 128 + np.arange(128)
    w1_r = np.ascontiguousarray(
        w1v[:, :, :, idx].transpose(0, 3, 2, 1, 4).reshape(E, 32, 128, 8 * 256)
    )
    w2_r = np.ascontiguousarray(
        w2.reshape(E, 32, 128, 1024).transpose(0, 2, 1, 3).reshape(E, 128, 32 * 1024)
    )
    return wqkv_r, wproj_r, rw_r, w1_r, w2_r


def prep_tables():
    t = np.arange(T, dtype=np.float32)
    inv_freq = (1.0 / ROPE_BASE ** (np.arange(0, HD, 2, dtype=np.float32) / HD)).astype(
        np.float32
    )
    tabs = np.empty((H, 128, 4096), dtype=np.float32)

    def blk(m):
        return m.reshape(NT, 128, 32).transpose(1, 0, 2).reshape(128, NT * 32)

    for h in range(H):
        fs = np.float32(ROPE_BASE ** (h / H))
        freqs = np.outer(t, (inv_freq * fs).astype(np.float32)).astype(np.float32)
        cos = np.cos(freqs).astype(np.float32)
        sin = np.sin(freqs).astype(np.float32)
        tabs[h, :, 0:1024] = blk(cos * np.float32(0.125))
        tabs[h, :, 1024:2048] = blk(sin * np.float32(0.125))
        tabs[h, :, 2048:3072] = blk(cos)
        tabs[h, :, 3072:4096] = blk(sin)
    return tabs


def prep_consts():
    ident = np.eye(128, dtype=np.float32)
    i = np.arange(128)[:, None]
    c = np.arange(896)[None, :]
    bigmask = (i <= c - 384).astype(np.float32)
    k = np.arange(128)[:, None]
    m = np.arange(128)[None, :]
    ltri = (k < m).astype(np.float32)
    erow = np.tile(np.arange(E, dtype=np.float32), (128, 1))
    return ident, bigmask, ltri, erow


def build_program(debug=(), stages=21, dbghead=0):
    nc = bass.Bass("TRN2", target_bir_lowering=False)

    x_d = nc.dram_tensor("x", [T, D], F32, kind="ExternalInput").ap()
    wqkv_d = nc.dram_tensor("wqkv", [H, 128, 1536], F32, kind="ExternalInput").ap()
    wproj_d = nc.dram_tensor("wproj", [128, 8192], F32, kind="ExternalInput").ap()
    rw_d = nc.dram_tensor("rw", [128, 64], F32, kind="ExternalInput").ap()
    rb_d = nc.dram_tensor("rb", [1, E], F32, kind="ExternalInput").ap()
    cw_d = nc.dram_tensor("cw", [1, H - 1], F32, kind="ExternalInput").ap()
    w1_d = nc.dram_tensor("w1r", [E, 32, 128, 2048], F32, kind="ExternalInput").ap()
    w2_d = nc.dram_tensor("w2r", [E, 128, 32 * 1024], F32, kind="ExternalInput").ap()
    tab_d = nc.dram_tensor("tabs", [H, 128, 4096], F32, kind="ExternalInput").ap()
    ident_d = nc.dram_tensor("identc", [128, 128], F32, kind="ExternalInput").ap()
    mask_d = nc.dram_tensor("bigmask", [128, 896], F32, kind="ExternalInput").ap()
    ltri_d = nc.dram_tensor("ltri", [128, 128], F32, kind="ExternalInput").ap()
    erow_d = nc.dram_tensor("erow", [128, E], F32, kind="ExternalInput").ap()
    out_d = nc.dram_tensor("out", [T, D], F32, kind="ExternalOutput").ap()

    x2acc_d = nc.dram_tensor("x2acc", [T, D], F32, kind="Internal").ap()
    x2_d = nc.dram_tensor("x2s", [T, D], F32, kind="Internal").ap()
    xn2_d = nc.dram_tensor("xn2s", [T, D], BF16, kind="Internal").ap()
    g_d = nc.dram_tensor("gbuf", [SLOTS, D], BF16, kind="Internal").ap()
    y_d = nc.dram_tensor("ybufd", [SLOTS, D], F32, kind="Internal").ap()
    w1b_d = nc.dram_tensor("w1b", [E, 32, 128, 2048], BF16, kind="Internal").ap()
    w2b_d = nc.dram_tensor("w2b", [E, 128, 32 * 1024], BF16, kind="Internal").ap()
    ttot_dram = nc.dram_tensor("ttotd", [1, 256], F32, kind="Internal").ap()
    tpre_dram = nc.dram_tensor("tpred", [1, 256], F32, kind="Internal").ap()
    brow_dram = nc.dram_tensor("browd", [1, 4096], BF16, kind="Internal").ap()

    dbg = {}
    for name, shape, dt in debug:
        dbg[name] = nc.dram_tensor(name, shape, dt, kind="ExternalOutput").ap()

    from contextlib import ExitStack

    es = ExitStack()

    def sbuf(name, shape, dt):
        return es.enter_context(nc.sbuf_tensor(name, shape, dt))

    def psum(name, shape, dt):
        return es.enter_context(nc.psum_tensor(name, shape, dt))

    with es:
        ident_b = sbuf("ident_b", [128, 128], BF16)
        ident_f = sbuf("ident_f", [128, 128], F32)
        mask_b = sbuf("mask_b", [128, 896], BF16)
        ltri_b = sbuf("ltri_b", [128, 128], BF16)
        erow_f = sbuf("erow_f", [128, E], F32)
        rw_b = sbuf("rw_b", [128, 64], BF16)
        rb_row = sbuf("rb_row", [1, E], F32)
        rbrep = sbuf("rbrep", [128, E], F32)
        ones_f = sbuf("ones_f", [1, 128], F32)
        ones_bf = sbuf("ones_bf", [1, 128], BF16)
        ones128 = sbuf("ones128", [128, 1], BF16)
        cws = sbuf("cws", [1, 16], F32)
        cwsc = sbuf("cwsc", [1, 16], F32)
        csts = sbuf("csts", [128, 4], F32)
        stats = sbuf("stats", [128, 24], F32)
        bias_acc = sbuf("bias_acc", [128, 32], F32)
        brow_f = sbuf("brow_f", [32, 128], F32)
        brow_b = sbuf("brow_b", [32, 128], BF16)
        wqkv_b = sbuf("wqkv_b", [128, 2 * 1536], BF16)
        wproj_b = sbuf("wproj_b", [128, 2 * 1024], BF16)
        v_ext = sbuf("v_ext", [128, 2 * NT * 65], BF16)
        invrep = sbuf("invrep", [128, 512], BF16)
        invr_b = sbuf("invr_b", [1, 512], BF16)
        trash = sbuf("trash", [128, 512], BF16)
        silu_f = sbuf("silu_f", [128, 512], F32)
        maxprob = sbuf("maxprob", [128, 32], F32)
        eidx = sbuf("eidx", [128, 32], F32)
        oh_all = sbuf("oh_all", [128, 32 * E], BF16)
        intile = sbuf("intile", [128, 32 * E], F32)
        tprer = sbuf("tprer", [128, 32 * E], F32)
        ttot_row = sbuf("ttot_row", [1, 256], F32)
        ttot_f = sbuf("ttot_f", [32, E], F32)
        ttot_b = sbuf("ttot_b", [32, E], BF16)
        tpre_f = sbuf("tpre_f", [32, E], F32)
        tpre_row = sbuf("tpre_row", [1, 256], F32)
        dest_f = sbuf("dest_f", [128, 32], F32)
        dest_i = sbuf("dest_i", [128, 32], I32)
        xn2t = sbuf("xn2t", [128, 1024], BF16)
        xn2t2 = sbuf("xn2t2", [128, 1024], BF16)
        xn2fmt = sbuf("xn2fmt", [128, 8 * 128], BF16)
        logits = sbuf("logits", [128, 32 * E], F32)
        bparts = logits

        nc.const_aps.register(F32, -M0, csts[:, 0:1])
        nc.const_aps.register(F32, 1e-5, csts[:, 1:2])
        nc.const_aps.register(F32, -0.5, csts[:, 2:3])
        nc.const_aps.register(F32, 0.0, csts[:, 3:4])

        AR = 164 * KB
        arena = es.enter_context(nc.sbuf_tensor("arena", [128, AR], U8))
        ar_base = nc.lookup_mloc(arena).addr

        def at(name, off, shape, dt):
            return nc.alloc_sbuf_tensor_at(name, shape, dt, offset=ar_base + off)

        xn_fm = at("xn_fm", 0, [128, 8 * 4096], BF16)
        costab0 = at("costab", 64 * KB, [128, 4096], F32)
        qk_tm = at("qk_tm", 80 * KB, [128, NT * 128], BF16)
        qrot = at("qrot", 88 * KB, [128, NT * 64], BF16)
        krot = at("krot", 92 * KB, [128, NT * 64], BF16)
        qfm = at("qfm", 96 * KB, [128, 4096], BF16)
        kfm = at("kfm", 104 * KB, [128, 4096], BF16)
        pT = at("pT", 112 * KB, [128, 32 * 512], BF16)
        outcat2 = at("outcat2", 144 * KB, [128, 4096], BF16)
        projpart = at("projpart", 112 * KB, [128, 2 * 2048], F32)
        ln_scr = at("ln_scr", 128 * KB, [128, 1024], F32)
        w2_b = at("w2_b", 0, [128, 32 * 1024], BF16)
        t_b = at("t_b", 64 * KB, [128, 32 * 1024], BF16)
        g_fm = at("g_fm", 128 * KB, [128, 8 * 1024], BF16)
        w1img = at("w1img", 144 * KB, [128, 2 * 2048], BF16)
        ybuf = at("ybuf", 152 * KB, [128, 1024], F32)
        yg = at("yg", 156 * KB, [128, 1024], F32)
        x2t = at("x2t", 160 * KB, [128, 1024], F32)
        outt = at("outt", 152 * KB, [128, 1024], F32)
        x2b = at("x2b", 136 * KB, [128, 1024], F32)
        outt2 = at("outt2", 128 * KB, [128, 1024], F32)
        yg2 = at("yg2", 132 * KB, [128, 1024], F32)
        zrow = trash

        pt_ps = psum("pt_ps", [128, 1024], BF16)
        squad = psum("squad", [128, 2048], F32)
        pf5 = psum("pf5", [128, 512], F32)
        pf6 = psum("pf6", [128, 512], F32)
        pf7 = psum("pf7", [128, 512], F32)

        def newsem(name):
            return Ctr(es.enter_context(nc.semaphore(name)))

        sT = newsem("sT")
        sV = newsem("sV")
        sS = newsem("sS")
        sQ = newsem("sQ")
        syMisc = newsem("syMisc")
        syTab = newsem("syTab")
        syB = newsem("syB")
        syG = newsem("syG")
        syY = [newsem("syY0"), newsem("syY1")]
        syOut = newsem("syOut")
        sx = [newsem("sx0"), newsem("sx1")]
        sa = [newsem("sa0"), newsem("sa1")]
        sn = [newsem("sn0"), newsem("sn1")]
        sw = [newsem("sw0"), newsem("sw1")]
        sc_ = [newsem("sc0"), newsem("sc1")]
        gi = newsem("gi")
        gq = [newsem("gq0"), newsem("gq1")]
        gp = [newsem("gp0"), newsem("gp1")]
        gb = newsem("gb")
        gc = newsem("gc")
        sP = newsem("sP")
        gw = [newsem("gw0"), newsem("gw1")]
        gw2s = newsem("gw2s")
        ga = [newsem("ga0"), newsem("ga1")]
        gs = [newsem("gs0"), newsem("gs1")]
        gy = [newsem("gy0"), newsem("gy1")]

        sch = Sched()
        OP = sch.op

        def dma(queue, out_ap, in_ap, ctr, waits=(), accum=False):
            if accum:
                fn = lambda eng, o=out_ap, i=in_ap: eng.dma_start(
                    out=o, in_=i, accum_op=ALU.add
                )
            else:
                fn = lambda eng, o=out_ap, i=in_ap: eng.dma_start(out=o, in_=i)
            OP(queue, fn, waits=waits, incs=[(ctr, 16)])
            return ctr.n

        def mm(out_ap, lhsT, rhs, start, stop, waits=(), inc=False):
            fn = lambda eng, o=out_ap, l=lhsT, r=rhs, a=start, b=stop: eng.matmul(
                o, l, r, start=a, stop=b
            )
            OP("tensor", fn, waits=waits, incs=[(sT, 1)] if inc else ())
            return sT.n

        def tp(out_ap, in_ap, ident, waits=(), inc=False):
            fn = lambda eng, o=out_ap, i=in_ap, d=ident: eng.transpose(o, i, d)
            OP("tensor", fn, waits=waits, incs=[(sT, 1)] if inc else ())
            return sT.n

        def vop(fn, waits=(), inc=False):
            OP("vector", fn, waits=waits, incs=[(sV, 1)] if inc else ())
            return sV.n

        def sop(fn, waits=(), inc=False):
            OP("scalar", fn, waits=waits, incs=[(sS, 1)] if inc else ())
            return sS.n

        dma("sync", ident_f[:, :], ident_d[:, :], syMisc)
        dma("sync", erow_f[:, :], erow_d[:, :], syMisc)
        dma("sync", rb_row[0:1, :], rb_d[0:1, :], syMisc)
        t_misc4 = dma("sync", cws[0:1, 0:15], cw_d[0:1, :], syMisc)
        dma("gpsimd", ident_b[:, :], ident_d[:, :], gi)
        dma("gpsimd", mask_b[:, :], mask_d[:, :], gi)
        dma("gpsimd", ltri_b[:, :], ltri_d[:, :], gi)
        t_gi = dma("gpsimd", rw_b[:, :], rw_d[:, :], gi)

        vop(lambda v: v.memset(csts[:, 0:1], -M0))
        vop(lambda v: v.memset(csts[:, 1:2], 1e-5))
        vop(lambda v: v.memset(csts[:, 2:3], -0.5))
        vop(lambda v: v.memset(csts[:, 3:4], 0.0))
        vop(lambda v: v.memset(ones_f[:, :], 1.0))
        vop(lambda v: v.memset(ones_bf[:, :], 1.0))
        vop(lambda v: v.memset(ones128[:, :], 1.0))
        vop(lambda v: v.memset(bias_acc[:, :], 0.0))
        vop(lambda v: v.memset(logits[:, :], 0.0))
        vop(lambda v: v.memset(qfm[64:65, :], 0.0))
        vop(lambda v: v.memset(kfm[64:65, :], 0.0))
        vext_v = v_ext[:].rearrange("p (s t c) -> p s t c", c=65, t=NT)
        vop(lambda v: v.memset(vext_v[:, :, :, 64:65], 1.0))
        t_zrow = vop(lambda v: v.memset(zrow[:, :], 0.0), inc=True)
        t_cwsc = vop(
            lambda v: v.tensor_scalar_mul(cwsc[0:1, 0:15], cws[0:1, 0:15], 1.0 / T),
            waits=[(syMisc, t_misc4)],
            inc=True,
        )
        t_initv = t_cwsc

        t_mm_rb = mm(
            pf7[0:128, 0:E], ones_f[0:1, :], rb_row[0:1, :], True, True,
            waits=[(syMisc, t_misc4), (sV, t_initv)], inc=True,
        )
        t_rbrep = vop(
            lambda v: v.tensor_copy(rbrep[:, :], pf7[0:128, 0:E]),
            waits=[(sT, t_mm_rb)], inc=True,
        )

        g_flat = g_d.rearrange("r (a f) -> (r a) f", a=2)
        for j in range(128):
            t_gzero = dma(
                "scalar", g_flat[j * 128:(j + 1) * 128, :], zrow[:, :], sQ,
                waits=[(sV, t_zrow)] if j == 0 else (),
            )

        t_wqkv = [0] * (H + 1)
        t_tab = [0] * (H + 1)
        t_wproj = [0] * 9
        t_wqkv[0] = dma("gpsimd", wqkv_b[:, 0:1536], wqkv_d[0, :, :], gq[0])
        t_tab[0] = dma("sync", costab0[:, :], tab_d[0, :, :], syTab)
        t_wproj[0] = dma("gpsimd", wproj_b[:, 0:1024], wproj_d[:, 0:1024], gp[0])

        xn_fm_v = xn_fm[:].rearrange("p (c t) -> p c t", t=4096)

        def ln_stats(xs, c0, t_load_tok, load_ctr):
            t_sum = vop(lambda v: v.reduce_sum(stats[:, c0:c0 + 1], xs, axis=AX),
                        waits=[(load_ctr, t_load_tok)], inc=True)
            t_sq = sop(lambda s: s.activation(ln_scr[:, :], xs, AF.Square,
                                              accum_out=stats[:, c0 + 1:c0 + 2]),
                       waits=[(load_ctr, t_load_tok)], inc=True)
            t_msq = vop(lambda v: v.scalar_tensor_tensor(
                stats[:, c0 + 4:c0 + 5], stats[:, c0:c0 + 1], 1.0 / (D * D),
                stats[:, c0:c0 + 1], ALU.mult, ALU.mult),
                waits=[(sV, t_sum)], inc=True)
            vop(lambda v: v.tensor_scalar_mul(
                stats[:, c0 + 2:c0 + 3], stats[:, c0:c0 + 1], 1.0 / D))
            t_var = vop(lambda v: v.scalar_tensor_tensor(
                stats[:, c0 + 5:c0 + 6], stats[:, c0 + 1:c0 + 2], 1.0 / D,
                stats[:, c0 + 4:c0 + 5], ALU.mult, ALU.subtract),
                waits=[(sS, t_sq), (sV, t_msq)], inc=True)
            t_ln = sop(lambda s: s.activation(
                stats[:, c0 + 6:c0 + 7], stats[:, c0 + 5:c0 + 6], AF.Ln,
                bias=1e-5), waits=[(sV, t_var)], inc=True)
            t_rstd = sop(lambda s: s.activation(
                stats[:, c0 + 7:c0 + 8], stats[:, c0 + 6:c0 + 7], AF.Exp,
                scale=-0.5), waits=[(sS, t_ln)], inc=True)
            t_negmr = vop(lambda v: v.scalar_tensor_tensor(
                stats[:, c0 + 8:c0 + 9], stats[:, c0 + 2:c0 + 3], -1.0,
                stats[:, c0 + 7:c0 + 8], ALU.mult, ALU.mult),
                waits=[(sS, t_rstd)], inc=True)
            return stats[:, c0 + 7:c0 + 8], stats[:, c0 + 8:c0 + 9], t_negmr, t_rstd

        t_xload = [0] * NT
        t_xuse = [0] * NT
        t_p0evac = [0] * NT
        t_p0tp = [0] * NT
        xslots0 = [x2t, x2b]
        lnrec = at("lnrec", 144 * KB, [128, NT * 4], F32)
        xnslots0 = [xn2t, xn2t2]
        for tt in range(NT):
            slot = tt % 2
            w = [(sS, t_xuse[tt - 2])] if tt >= 2 else ()
            t_xload[tt] = dma(
                "sync", xslots0[slot][:, :],
                x_d[tt * 128:(tt + 1) * 128, :], sx[slot], waits=w,
            )
            xs = xslots0[slot][:, :]
            c0 = slot * 10
            rstd, negmr, t_negmr, t_rstdtok = ln_stats(xs, c0, t_xload[tt], sx[slot])
            if "d_lnrec" in dbg:
                vop(lambda v, c0=c0, tt=tt: v.tensor_copy(
                    lnrec[:, tt * 4:tt * 4 + 4], stats[:, c0 + 2:c0 + 6]))
            xnsl0 = xnslots0[slot]
            w = [(sV, t_negmr), (sS, t_rstdtok)]
            if tt >= 2:
                w.append((sT, t_p0tp[tt - 2]))
            t_xn = sop(
                lambda s, xs=xs, rstd=rstd, negmr=negmr, xnsl0=xnsl0: s.activation(
                    xnsl0[:, :], xs, AF.Identity, scale=rstd, bias=negmr),
                waits=w, inc=True,
            )
            t_xuse[tt] = t_xn
            w = [(sS, t_xn), (gi, t_gi)]
            if tt >= 1:
                w.append((sV, t_p0evac[tt - 1]))
            for fc in range(8):
                tk = tp(
                    pt_ps[:, fc * 128:(fc + 1) * 128],
                    xnsl0[:, fc * 128:(fc + 1) * 128],
                    ident_b[:, :], waits=w if fc == 0 else (), inc=(fc == 7),
                )
            t_p0tp[tt] = tk
            t_p0evac[tt] = vop(
                lambda v, tt=tt: v.tensor_copy(
                    xn_fm_v[:, :, tt * 128:(tt + 1) * 128],
                    pt_ps[:].rearrange("p (c t) -> p c t", t=128),
                ),
                waits=[(sT, tk)], inc=True,
            )
        t_xnfm = t_p0evac[NT - 1]
        if "d_xnfm" in dbg:
            dma("sync", dbg["d_xnfm"][:, :], xn_fm[:, :], syMisc,
                waits=[(sV, t_xnfm)])
        if "d_lnrec" in dbg:
            dma("sync", dbg["d_lnrec"][:, :], lnrec[:, :], syMisc,
                waits=[(sV, t_xnfm)])

        HEADS = 0 if stages == 0 else min(stages, 16)
        DBGHEAD = dbghead
        NT2 = NT if stages >= 20 else 0
        P3PLUS = stages >= 21
        t_desti = 0
        t_ylast = 0
        t_scatall = []
        qk_v = qk_tm[:].rearrange("p (t c) -> p t c", c=128)
        qr_v = qrot[:].rearrange("p (t c) -> p t c", c=64)
        kr_v = krot[:].rearrange("p (t c) -> p t c", c=64)
        tmp_v = xn2t[:].rearrange("p (t c) -> p t c", c=32)

        t_exp_last = 0
        t_brow = [0] * (H + 1)
        t_qrow = [0] * (H + 1)
        t_rope = [0] * H
        t_qkvmm = [0] * H
        t_fmtp = [0] * H
        t_pv_lastnorm = 0
        t_pvlast = [0] * H
        t_subA_evac = 0
        t_lastprojdma = None
        t_projmm = [0] * 8
        t_projevac = [0] * 8
        t_projdma = [[0] * 16 for _ in range(8)]
        pt_lastread = [0] * 32
        t_exp_half = {0: 0, 1: 0}
        t_vext_last = 0

        t_qkve_par = [0, 0]
        t_vext_done = [0] * (H + 1)
        t_fmdone = [0] * (H + 1)
        t_fmevac_roll = 0
        t_exp_prev = 0

        def emit_qkv_group(hh, tt):
            nonlocal t_fmevac_roll
            bank = 2 + tt % 2
            pq = squad[:, bank * 512:bank * 512 + 192]
            wq = wqkv_b[:, (hh % 2) * 1536:(hh % 2 + 1) * 1536]
            sl = hh % 2
            w = [(gq[hh % 2], t_wqkv[hh]), (sV, t_xnfm)]
            if t_qkve_par[tt % 2]:
                w.append((sS, t_qkve_par[tt % 2]))
            if tt == 0 and hh >= 2:
                w.append((sT, t_pvlast[hh - 2]))
            for fc in range(8):
                tk = mm(pq, xn_fm_v[:, fc, tt * 128:(tt + 1) * 128],
                        wq[:, fc * 192:(fc + 1) * 192], fc == 0, fc == 7,
                        waits=w if fc == 0 else (), inc=(fc == 7))
            t_qkvmm[hh] = tk
            sop(lambda s, tt=tt, pq=pq: s.activation(
                qk_v[:, tt, :], pq[:, 0:128], AF.Copy), waits=[(sT, tk)])
            tok = sop(lambda s, tt=tt, pq=pq, sl=sl: s.activation(
                vext_v[:, sl, tt, 0:64], pq[:, 128:192], AF.Copy), inc=True)
            t_qkve_par[tt % 2] = tok
            if tt == NT - 1:
                t_vext_done[hh] = tok
            return tok

        def emit_rope(hh):
            x1q, x2q = qk_v[:, :, 0:32], qk_v[:, :, 32:64]
            x1k, x2k = qk_v[:, :, 64:96], qk_v[:, :, 96:128]
            ct = costab0[:].rearrange("p (b t i) -> p b t i", b=4, i=32)
            cq, sq_, ck, sk = ct[:, 0], ct[:, 1], ct[:, 2], ct[:, 3]
            q1, q2 = qr_v[:, :, 0:32], qr_v[:, :, 32:64]
            k1, k2 = kr_v[:, :, 0:32], kr_v[:, :, 32:64]
            w = [(syTab, t_tab[hh]), (sS, t_qkve_par[(NT - 1) % 2])]
            if hh >= 1:
                w.append((sT, t_fmtp[hh - 1]))
            vop(lambda v: v.tensor_mul(q1, x1q, cq), waits=w)
            vop(lambda v: v.tensor_mul(tmp_v, x2q, sq_))
            vop(lambda v: v.tensor_sub(q1, q1, tmp_v))
            vop(lambda v: v.tensor_mul(q2, x1q, sq_))
            vop(lambda v: v.tensor_mul(tmp_v, x2q, cq))
            vop(lambda v: v.tensor_add(q2, q2, tmp_v))
            vop(lambda v: v.tensor_mul(k1, x1k, ck))
            vop(lambda v: v.tensor_mul(tmp_v, x2k, sk))
            vop(lambda v: v.tensor_sub(k1, k1, tmp_v))
            vop(lambda v: v.tensor_mul(k2, x1k, sk))
            vop(lambda v: v.tensor_mul(tmp_v, x2k, ck))
            t_rope[hh] = vop(lambda v: v.tensor_add(k2, k2, tmp_v), inc=True)
            if hh + 1 < H:
                t_tab[hh + 1] = dma(
                    "sync", costab0[:, :], tab_d[hh + 1, :, :], syTab,
                    waits=[(sV, t_rope[hh])],
                )

        def emit_fmtrans(hh):
            nonlocal t_fmevac_roll
            t_fmevac = t_fmevac_roll
            for grp in range(8):
                srcbuf = qrot if grp < 4 else krot
                dstbuf = qfm if grp < 4 else kfm
                g4 = grp % 4
                w = [(sV, t_rope[hh])]
                if t_fmevac:
                    w.append((sV, t_fmevac))
                for j in range(8):
                    tile = g4 * 8 + j
                    tk = tp(
                        pt_ps[0:64, j * 128:(j + 1) * 128],
                        srcbuf[:, tile * 64:(tile + 1) * 64],
                        ident_b[:, :],
                        waits=w if j == 0 else (), inc=(j == 7),
                    )
                t_fmevac = vop(
                    lambda v, dstbuf=dstbuf, g4=g4: v.tensor_copy(
                        dstbuf[0:64, g4 * 1024:(g4 + 1) * 1024],
                        pt_ps[0:64, :]),
                    waits=[(sT, tk)], inc=True,
                )
            t_fmtp[hh] = tk
            t_fmevac_roll = t_fmevac
            t_fmdone[hh] = t_fmevac

        for h in range(HEADS):
            if h + 1 < H:
                wv = [(sT, t_qkvmm[h - 1])] if h >= 1 else ()
                t_wqkv[h + 1] = dma(
                    "gpsimd",
                    wqkv_b[:, ((h + 1) % 2) * 1536:((h + 1) % 2 + 1) * 1536],
                    wqkv_d[h + 1, :, :], gq[(h + 1) % 2], waits=wv,
                )
            if h % 2 == 0 and h // 2 + 1 < 8:
                p = h // 2 + 1
                wv = [(sT, t_projmm[p - 2])] if p >= 2 else ()
                t_wproj[p] = dma(
                    "gpsimd", wproj_b[:, (p % 2) * 1024:(p % 2 + 1) * 1024],
                    wproj_d[:, p * 1024:(p + 1) * 1024], gp[p % 2], waits=wv,
                )
            for piece in range(3):
                pidx = h * 3 + piece
                if pidx < 32:
                    e_, cg = divmod(pidx, 4)
                    dma("gpsimd",
                        w1b_d[e_, cg * 8:(cg + 1) * 8, :, :],
                        w1_d[e_, cg * 8:(cg + 1) * 8, :, :], gc)
                elif pidx < 48:
                    e_, jg = divmod(pidx - 32, 2)
                    dma("gpsimd",
                        w2b_d[e_, :, jg * 16384:(jg + 1) * 16384],
                        w2_d[e_, :, jg * 16384:(jg + 1) * 16384], gc)

            if h == 0:
                for tt in range(NT):
                    emit_qkv_group(0, tt)
                emit_rope(0)
                emit_fmtrans(0)

            if h == DBGHEAD and "d_qfm" in dbg:
                dma("sync", dbg["d_qfm"][:, :], qfm[:, :], syMisc,
                    waits=[(sV, t_fmdone[h])])
                dma("sync", dbg["d_kfm"][:, :], kfm[:, :], syMisc)
                dma("sync", dbg["d_vext"][:, :], v_ext[:, :], syMisc)
                dma("sync", dbg["d_qk"][:, :], qk_tm[:, :], syMisc)

            pend_tail = None
            qkv_tt = 0 if h + 1 < H else NT

            def emit_tail(qg, nsc, tk_pv, t_maskg_last):
                nonlocal t_pv_lastnorm
                def _recip(v):
                    with nc.allow_low_precision(reason="1/r stored bf16 as matmul operand"):
                        return v.reciprocal(invr_b[0:1, :], pf5[64:65, :])
                t_invb = vop(_recip, waits=[(sT, tk_pv)], inc=True)
                w = [(sV, t_invb)]
                if h % 2 == 0 and h >= 2:
                    w.append((sV, t_projevac[h // 2 - 1]))
                t_repmm = mm(
                    pf6[0:128, :], ones_bf[0:1, :], invr_b[0:1, :], True, True,
                    waits=w, inc=True,
                )
                t_rep = vop(
                    lambda v: v.tensor_copy(invrep[:, :], pf6[0:128, :]),
                    waits=[(sT, t_repmm)], inc=True,
                )
                r0 = 64 * (h % 2)
                t_pv_lastnorm = vop(
                    lambda v, r0=r0, qg=qg: v.tensor_mul(
                        outcat2[r0:r0 + 64, qg * 512:(qg + 1) * 512],
                        pf5[0:64, :], invrep[0:64, :]),
                    waits=[(sV, t_rep)], inc=True,
                )
                for sc in range(nsc):
                    pt_lastread[sc] = vop(
                        lambda v, sc=sc, qg=qg: v.scalar_tensor_tensor(
                            trash[:, :], pT[:, sc * 512:(sc + 1) * 512], 1.0,
                            invrep[:, :], ALU.mult, ALU.mult,
                            accum_out=bparts[:, sc * 8 + qg:sc * 8 + qg + 1]),
                        waits=[(sV, t_rep), (sQ, t_gzero), (sP, t_maskg_last)],
                        inc=True,
                    )

            for qg in range(8):
                nsc = 4 * qg + 4
                qsl = qfm[0:65, qg * 512:(qg + 1) * 512]
                t_expg = {}
                psl = squad[:, 0:1024]
                for scg in range(0, nsc, 2):
                    if qkv_tt < NT:
                        emit_qkv_group(h + 1, qkv_tt)
                        qkv_tt += 1
                        if qkv_tt == NT:
                            emit_rope(h + 1)
                    for j in range(2):
                        sc = scg + j
                        w = [(sV, t_fmdone[h]), (gb, t_brow[h]),
                             (sV, t_qrow[h]), (sS, t_exp_prev)]
                        mm(
                            psl[:, j * 512:(j + 1) * 512],
                            kfm[0:65, sc * 128:(sc + 1) * 128],
                            qsl, True, True,
                            waits=w if j == 0 else (), inc=(j == 1),
                        )
                    tmk = sT.n
                    if scg == 0 and pend_tail is not None:
                        pend_tail()
                        pend_tail = None
                    w = [(sT, tmk)]
                    if t_lastprojdma is not None:
                        w.extend(t_lastprojdma)
                    for sc in (scg, scg + 1):
                        if pt_lastread[sc]:
                            w.append((sV, pt_lastread[sc]))
                    t_expg[scg] = sop(
                        lambda s, scg=scg, psl=psl: s.activation(
                            pT[:, scg * 512:(scg + 2) * 512], psl, AF.Exp,
                            bias=-M0),
                        waits=w, inc=True,
                    )
                    t_exp_prev = t_expg[scg]
                t_exp_last = t_expg[nsc - 2]
                t_maskg = 0
                for kk in range(4):
                    sc = 4 * qg + kk
                    OP("gpsimd",
                       lambda g, sc=sc, kk=kk: g.tensor_mul(
                           pT[:, sc * 512:(sc + 1) * 512],
                           pT[:, sc * 512:(sc + 1) * 512],
                           mask_b[:, 384 - 128 * kk:896 - 128 * kk]),
                       waits=[(sS.h, t_expg[sc - (sc % 2)])] if False else
                       [(sS, t_expg[sc - (sc % 2)])],
                       incs=[(sP, 1)])
                    t_maskg = sP.n
                for sc in range(nsc):
                    w = []
                    if sc == 0:
                        w.append((sV, t_pv_lastnorm))
                        w.append((sS, t_vext_done[h]))
                        if h % 2 == 0 and h >= 2:
                            w.append((sV, t_projevac[h // 2 - 1]))
                    w.append((sS, t_expg[sc - (sc % 2)]))
                    if sc >= 4 * qg:
                        w.append((sP, t_maskg))
                    tk = mm(
                        pf5[0:65, :],
                        v_ext[:, (h % 2) * NT * 65 + sc * 65:
                              (h % 2) * NT * 65 + (sc + 1) * 65],
                        pT[:, sc * 512:(sc + 1) * 512],
                        sc == 0, sc == nsc - 1,
                        waits=w, inc=(sc == nsc - 1),
                    )
                t_pvlast[h] = tk
                pend_tail = (lambda qg=qg, nsc=nsc, tk=tk, mg=t_maskg:
                             emit_tail(qg, nsc, tk, mg))
            pend_tail()
            pend_tail = None
            if h + 1 < H:
                emit_fmtrans(h + 1)

            if h == DBGHEAD and "d_outcat" in dbg:
                t_tmpred = vop(
                    lambda v: v.tensor_reduce(
                        bias_acc[:].rearrange("p (s o) -> p s o", o=1),
                        bparts[:].rearrange("p (s q) -> p s q", q=8),
                        AX, ALU.add),
                    waits=[(sV, pt_lastread[31])], inc=True,
                )
                dma("sync", dbg["d_outcat"][:, :], outcat2[:, :], syMisc,
                    waits=[(sV, t_pv_lastnorm)])
                if "d_bacc" in dbg:
                    dma("sync", dbg["d_bacc"][:, :], bias_acc[:, :], syMisc,
                        waits=[(sV, t_tmpred)])
                if "d_pt" in dbg:
                    dma("sync", dbg["d_pt"][:, :], pT[:, :], syMisc)

            if h + 1 < H:
                t_bred = vop(
                    lambda v: v.tensor_reduce(
                        bias_acc[:].rearrange("p (s o) -> p s o", o=1),
                        bparts[:].rearrange("p (s q) -> p s q", q=8),
                        AX, ALU.add),
                    waits=[(sV, pt_lastread[31])], inc=True,
                )
                t_bz = vop(lambda v: v.memset(bparts[:, :], 0.0),
                           waits=[(sV, t_bred)], inc=True)
                tk = tp(
                    pf7[0:32, 0:128], bias_acc[:, :], ident_f[:, :],
                    waits=[(sV, t_bred)], inc=True,
                )
                t_bev = vop(
                    lambda v: v.tensor_copy(brow_b[:, :], pf7[0:32, 0:128]),
                    waits=[(sT, tk)], inc=True,
                )
                t_bd1 = dma(
                    "sync",
                    brow_dram[0:1, :].rearrange("a (s i) -> (a s) i", i=128),
                    brow_b[:, :], gb, waits=[(sV, t_bev)])
                t_brow[h + 1] = dma(
                    "sync", kfm[64:65, :], brow_dram[0:1, :], gb,
                    waits=[(gb, t_bd1)],
                )
                t_qrow[h + 1] = vop(
                    lambda v, h=h: v.tensor_scalar(
                        qfm[64:65, :], qfm[64:65, :], 0.0,
                        cwsc[0:1, h:h + 1], ALU.mult, ALU.add),
                    waits=[(sV, t_cwsc), (sT, t_pvlast[h])], inc=True,
                )

            if h % 2 == 1:
                p = h // 2
                t_ev_hist = []
                for r in range(16):
                    rslot = r % 2
                    pp = projpart[:, rslot * 2048:(rslot + 1) * 2048]
                    for idx4 in range(4):
                        ti, ofh = divmod(idx4, 2)
                        tt = r * 2 + ti
                        pb = pf5 if idx4 % 2 == 0 else pf6
                        w = [(sV, t_pv_lastnorm), (gp[p % 2], t_wproj[p]),
                             (sV, pt_lastread[31])]
                        if len(t_ev_hist) >= 2:
                            w.append((sV, t_ev_hist[-2]))
                        tk = mm(
                            pb[:, :],
                            outcat2[:, tt * 128:(tt + 1) * 128],
                            wproj_b[:, (p % 2) * 1024 + ofh * 512:
                                    (p % 2) * 1024 + (ofh + 1) * 512],
                            True, True, waits=w, inc=True,
                        )
                        wv = [(sT, tk)]
                        if r >= 2 and idx4 == 0:
                            wv.append((ga[r % 2], t_projdma[p][r - 2]))
                        t_ev = vop(
                            lambda v, pp=pp, ti=ti, ofh=ofh, pb=pb: v.tensor_copy(
                                pp[:, ti * 1024 + ofh * 512:
                                   ti * 1024 + (ofh + 1) * 512], pb[:, :]),
                            waits=wv, inc=True,
                        )
                        t_ev_hist.append(t_ev)
                    w = [(sV, t_ev)]
                    if p >= 1:
                        w.append((ga[r % 2], t_projdma[p - 1][r]))
                    t_projdma[p][r] = dma(
                        "gpsimd",
                        x2acc_d[r * 256:(r + 1) * 256, :].rearrange(
                            "(a p) f -> p a f", p=128),
                        pp[:, :].rearrange("p (a f) -> p a f", f=1024),
                        ga[r % 2], waits=w, accum=(p > 0),
                    )
                t_projmm[p] = tk
                t_projevac[p] = t_ev
                t_lastprojdma = [(ga[0], t_projdma[p][14]),
                                 (ga[1], t_projdma[p][15])]

        t_projall = ([(ga[0], ga[0].n), (ga[1], ga[1].n)]
                     if HEADS == H else [])

        t_w1l = {}
        t_w2l = [0] * E
        t_castall = [(gc, gc.n)]
        if NT2 and P3PLUS:
            for c in range(2):
                t_w1l[(0, c)] = dma(
                    "scalar", w1img[:, (c % 2) * 2048:(c % 2 + 1) * 2048],
                    w1b_d[0, c, :, :], gw[c % 2],
                    waits=[(sT, t_projmm[7])] + t_castall,
                )
            for j in range(8):
                t_w2l[0] = dma(
                    "sync", w2_b[:, j * 4096:(j + 1) * 4096],
                    w2b_d[0, :, j * 4096:(j + 1) * 4096], gw2s,
                    waits=([(sT, t_qkvmm[15])] + t_castall) if j == 0 else (),
                )
        t_x2w = [0] * NT
        t_xn2w = [0] * NT
        t_lgev = [0] * NT
        t_add_v = [0] * NT
        t_xnuse = [0] * NT
        t_p2evac = 0
        t_totmm = 0
        x2slots = [yg, yg2]
        xslots2 = [x2t, x2b]
        xnslots = [xn2t, xn2t2]
        for tt in range(NT2):
            slot = tt % 2
            xsl = xslots2[slot][:, :]
            asl = x2slots[slot]
            w = list(t_projall)
            if tt >= 2:
                w.append((sS, t_xuse[tt - 2]))
                w.append((sc_[slot], t_x2w[tt - 2]))
            t_xl = dma("sync", xsl, x_d[tt * 128:(tt + 1) * 128, :], sx[slot],
                       waits=w)
            w2_ = ([(sV, t_add_v[tt - 2])] if tt >= 2 else list(t_projall))
            t_al = dma("sync", asl[:, :], x2acc_d[tt * 128:(tt + 1) * 128, :],
                       sa[slot], waits=w2_)
            t_add = vop(
                lambda v, xsl=xsl, asl=asl: v.tensor_add(xsl, xsl, asl[:, :]),
                waits=[(sa[slot], t_al), (sx[slot], t_xl)], inc=True,
            )
            t_add_v[tt] = t_add
            t_x2w[tt] = dma("sync", x2_d[tt * 128:(tt + 1) * 128, :], xsl,
                            sc_[slot], waits=[(sV, t_add)])
            c0 = slot * 10
            rstd, negmr, t_negmr, t_rstdtok = ln_stats(xsl, c0, t_add, sV)
            xnsl = xnslots[slot]
            w = [(sV, t_negmr), (sS, t_rstdtok)]
            if tt >= 2:
                w.append((sT, t_xnuse[tt - 2]))
                w.append((sw[slot], t_xn2w[tt - 2]))
            t_xn = sop(
                lambda s, xsl=xsl, rstd=rstd, negmr=negmr, xnsl=xnsl: s.activation(
                    xnsl[:, :], xsl, AF.Identity, scale=rstd, bias=negmr),
                waits=w, inc=True,
            )
            t_xuse[tt] = t_xn
            t_xn2w[tt] = dma("sync", xn2_d[tt * 128:(tt + 1) * 128, :],
                             xnsl[:, :], sw[slot], waits=[(sS, t_xn)])
            w = [(sS, t_xn)]
            if tt == 0:
                w.append((sV, t_subA_evac))
            if t_p2evac:
                w.append((sV, t_p2evac))
            for fc in range(8):
                tk = tp(
                    pt_ps[:, fc * 128:(fc + 1) * 128],
                    xnsl[:, fc * 128:(fc + 1) * 128],
                    ident_b[:, :], waits=w if fc == 0 else (), inc=(fc == 7),
                )
            t_xnuse[tt] = tk
            t_p2evac = vop(
                lambda v: v.tensor_copy(
                    xn2fmt[:].rearrange("p (c t) -> p c t", t=128),
                    pt_ps[:].rearrange("p (c t) -> p c t", t=128)),
                waits=[(sT, tk)], inc=True,
            )
            rslot = tt % 2
            w = [(sV, t_p2evac), (gi, t_gi)]
            if tt >= 1:
                w.append((sV, t_lgev[tt - 1]))
            for fc in range(8):
                tk = mm(
                    pf7[:, rslot * 8:rslot * 8 + 8],
                    xn2fmt[:, fc * 128:(fc + 1) * 128],
                    rw_b[:, fc * 8:(fc + 1) * 8],
                    fc == 0, fc == 7, waits=w if fc == 0 else (), inc=(fc == 7),
                )
            t_lgev[tt] = vop(
                lambda v, tt=tt, rslot=rslot: v.tensor_add(
                    logits[:, tt * 8:(tt + 1) * 8],
                    pf7[:, rslot * 8:rslot * 8 + 8], rbrep[:, :]),
                waits=[(sT, tk), (sV, t_rbrep)], inc=True,
            )
            cnm = 9 if slot == 0 else 19
            cse = 20 if slot == 0 else 21
            t_nm = vop(lambda v, tt=tt, cnm=cnm: v.reduce_max(
                stats[:, cnm:cnm + 1], logits[:, tt * 8:(tt + 1) * 8],
                axis=AX, negate=True), waits=[(sV, t_lgev[tt])], inc=True)
            t_se = sop(
                lambda s, tt=tt, cnm=cnm, cse=cse: s.activation(
                    trash[:, 0:8], logits[:, tt * 8:(tt + 1) * 8], AF.Exp,
                    bias=stats[:, cnm:cnm + 1],
                    accum_out=stats[:, cse:cse + 1]),
                waits=[(sV, t_nm)], inc=True,
            )
            vop(lambda v, tt=tt, cse=cse: v.reciprocal(
                maxprob[:, tt:tt + 1], stats[:, cse:cse + 1]),
                waits=[(sS, t_se)])
            t_oh = vop(
                lambda v, tt=tt, cnm=cnm: v.tensor_scalar(
                    oh_all[:, tt * 8:(tt + 1) * 8],
                    logits[:, tt * 8:(tt + 1) * 8],
                    stats[:, cnm:cnm + 1], 0.0, ALU.add, ALU.is_equal),
                waits=[(sV, t_nm)], inc=True,
            )
            vop(
                lambda v, tt=tt: v.scalar_tensor_tensor(
                    silu_f[:, 0:8], erow_f[:, :], 1.0,
                    oh_all[:, tt * 8:(tt + 1) * 8],
                    ALU.mult, ALU.mult, accum_out=eidx[:, tt:tt + 1]),
                waits=[(sV, t_oh)],
            )
            w = [(sV, t_oh), (gi, t_gi)]
            mm(pf6[:, tt * 8:(tt + 1) * 8], ltri_b[:, :],
               oh_all[:, tt * 8:(tt + 1) * 8], True, True, waits=w, inc=True)
            t_totmm = mm(
                pf5[0:1, tt * 8:(tt + 1) * 8], ones128[:, 0:1],
                oh_all[:, tt * 8:(tt + 1) * 8], True, True, inc=True)

        if P3PLUS:
            t_it = vop(lambda v: v.tensor_copy(intile[:, :], pf6[:, 0:256]),
                       waits=[(sT, t_totmm)], inc=True)
            t_tr = vop(lambda v: v.tensor_copy(ttot_row[0:1, :], pf5[0:1, 0:256]),
                       inc=True)
            t_td1 = dma("sync", ttot_dram[0:1, :], ttot_row[0:1, :], syB,
                        waits=[(sV, t_tr)])
            t_td2 = dma("sync", ttot_f[:, :],
                        ttot_dram[0:1, :].rearrange("a (s e) -> (a s) e", e=8),
                        syB, waits=[(syB, t_td1)])
            t_tb = vop(lambda v: v.tensor_copy(ttot_b[:, :], ttot_f[:, :]),
                       waits=[(syB, t_td2)], inc=True)
            t_tpmm = mm(pf7[0:32, 0:8], ltri_b[0:32, 0:32], ttot_b[:, :], True,
                        True, waits=[(sV, t_tb)], inc=True)
            t_tp_ = vop(lambda v: v.tensor_copy(tpre_f[:, :], pf7[0:32, 0:8]),
                        waits=[(sT, t_tpmm)], inc=True)
            t_tpd1 = dma("sync",
                         tpre_dram[0:1, :].rearrange("a (s e) -> (a s) e", e=8),
                         tpre_f[:, :], syB, waits=[(sV, t_tp_)])
            t_tpd2 = dma("sync", tpre_row[0:1, :], tpre_dram[0:1, :], syB,
                         waits=[(syB, t_tpd1)])
            for tt in range(NT):
                w = [(syB, t_tpd2), (sV, t_it)]
                t_rp = mm(
                    pf6[:, tt * 8:(tt + 1) * 8], ones_f[0:1, :],
                    tpre_row[0:1, tt * 8:(tt + 1) * 8], True, True,
                    waits=w if tt == 0 else (), inc=(tt == NT - 1),
                )
            t_rpe = vop(lambda v: v.tensor_copy(tprer[:, :], pf6[:, 0:256]),
                        waits=[(sT, t_rp)], inc=True)
            for tt in range(NT):
                vop(
                    lambda v, tt=tt: v.tensor_add(
                        intile[:, tt * 8:(tt + 1) * 8],
                        intile[:, tt * 8:(tt + 1) * 8],
                        tprer[:, tt * 8:(tt + 1) * 8]),
                    waits=[(sV, t_rpe)],
                )
            for tt in range(NT):
                vop(
                    lambda v, tt=tt: v.scalar_tensor_tensor(
                        logits[:, tt * 8:(tt + 1) * 8],
                        intile[:, tt * 8:(tt + 1) * 8], 1.0,
                        oh_all[:, tt * 8:(tt + 1) * 8],
                        ALU.mult, ALU.mult,
                        accum_out=tprer[:, tt * 8:tt * 8 + 1]),
                )
            for tt in range(NT):
                t_df = vop(
                    lambda v, tt=tt: v.tensor_scalar(
                        dest_f[:, tt:tt + 1], eidx[:, tt:tt + 1], float(CAP),
                        tprer[:, tt * 8:tt * 8 + 1], ALU.mult, ALU.add),
                    inc=(tt == NT - 1),
                )
            t_desti = vop(lambda v: v.tensor_copy(dest_i[:, :], dest_f[:, :]),
                          waits=[(sV, t_df)], inc=True)
            t_scat_v = [0] * NT
            t_xnl = [0] * NT
            for tt in range(NT):
                slot = tt % 2
                w = [(sw[0], sw[0].n), (sw[1], sw[1].n)]
                if tt >= 2:
                    w.append((gs[slot], t_scat_v[tt - 2]))
                t_xnl[tt] = dma("sync", xnslots[slot][:, :],
                                xn2_d[tt * 128:(tt + 1) * 128, :], sn[slot],
                                waits=w)

                def mk(eng, tt=tt, slot=slot):
                    return eng.indirect_dma_start(
                        out=g_d[0:128, :],
                        out_offset=IndirectOffsetOnAxis(dest_i[:, tt:tt + 1], 0),
                        in_=xnslots[slot][:, :],
                        in_offset=None,
                    )
                OP("gpsimd", mk,
                   waits=[(sn[slot], t_xnl[tt]), (sV, t_desti), (sQ, t_gzero)],
                   incs=[(gs[slot], 16)])
                t_scat_v[tt] = gs[slot].n
            t_scatall = [(gs[0], gs[0].n), (gs[1], gs[1].n)]

            g_fm_v = g_fm[:].rearrange("p (c s) -> p c s", s=1024)
            t_gload = [0] * E
            t_gtrans = [0] * E
            t_mm1_done = [0] * E
            t_mm2_done = [0] * E
            t_ydma = {}
            t_silu_done = 0
            t_sil2 = [0, 0]
            t_yev_last = 0

            def sched_gload(e, waits):
                t_gload[e] = dma(
                    "sync", g_fm[:].rearrange("p (s f) -> p s f", f=1024),
                    g_d[e * CAP:(e + 1) * CAP, :].rearrange("(s p) f -> p s f", p=128),
                    syG, waits=waits,
                )

            def sched_gtrans(e):
                pairs = [(a, b) for a in range(8) for b in range(a, 8)]
                tev = 0
                batch = []
                for pi, (a, b) in enumerate(pairs):
                    batch.append((a, b))
                    if len(batch) == 4 or pi == len(pairs) - 1:
                        w = [(syG, t_gload[e])]
                        if tev:
                            w.append((sV, tev))
                        nops = sum(1 if aa == bb else 2 for aa, bb in batch)
                        cnt = 0
                        for j, (aa, bb) in enumerate(batch):
                            cnt += 1
                            tk = tp(
                                pt_ps[:, (2 * j) * 128:(2 * j + 1) * 128],
                                g_fm[:, aa * 1024 + bb * 128:aa * 1024 + (bb + 1) * 128],
                                ident_b[:, :],
                                waits=w if cnt == 1 else (), inc=(cnt == nops),
                            )
                            if aa != bb:
                                cnt += 1
                                tk = tp(
                                    pt_ps[:, (2 * j + 1) * 128:(2 * j + 2) * 128],
                                    g_fm[:, bb * 1024 + aa * 128:bb * 1024 + (aa + 1) * 128],
                                    ident_b[:, :], inc=(cnt == nops),
                                )
                        for j, (aa, bb) in enumerate(batch):
                            tev = vop(
                                lambda v, j=j, aa=aa, bb=bb: v.tensor_copy(
                                    g_fm[:, bb * 1024 + aa * 128:bb * 1024 + (aa + 1) * 128],
                                    pt_ps[:, (2 * j) * 128:(2 * j + 1) * 128]),
                                waits=[(sT, tk)], inc=True,
                            )
                            if aa != bb:
                                tev = vop(
                                    lambda v, j=j, aa=aa, bb=bb: v.tensor_copy(
                                        g_fm[:, aa * 1024 + bb * 128:aa * 1024 + (bb + 1) * 128],
                                        pt_ps[:, (2 * j + 1) * 128:(2 * j + 2) * 128]),
                                    inc=True,
                                )
                        batch = []
                t_gtrans[e] = tev

            sched_gload(0, list(t_scatall))
            sched_gtrans(0)

            for e in range(E):
                if e > 0:
                    for c in range(2):
                        wv = []
                        if c == 0:
                            wv.append((sT, t_mm1_done[e - 1]))
                        t_w1l[(e, c)] = dma(
                            "scalar", w1img[:, (c % 2) * 2048:(c % 2 + 1) * 2048],
                            w1b_d[e, c, :, :], gw[c % 2], waits=wv,
                        )
                for c in range(32):
                    w1sl = w1img[:, (c % 2) * 2048:(c % 2 + 1) * 2048]
                    t_lastmm_c = 0
                    par = c % 2
                    pa = squad[:, par * 1024:par * 1024 + 512]
                    pg = squad[:, par * 1024 + 512:par * 1024 + 1024]
                    ssl = silu_f if par == 0 else x2t[:, 0:512]
                    for sh in range(2):
                        w = [(gw[c % 2], t_w1l[(e, c)]), (sV, t_gtrans[e])]
                        if t_sil2[par]:
                            w.append((sV, t_sil2[par]))
                        if c == 1 and t_yev_last:
                            w.append((sV, t_yev_last))
                        for fc in range(8):
                            mm(pa, w1sl[:, fc * 256:fc * 256 + 128],
                               g_fm_v[:, fc, sh * 512:(sh + 1) * 512],
                               fc == 0, fc == 7, waits=w if fc == 0 else ())
                        for fc in range(8):
                            tk = mm(pg,
                                    w1sl[:, fc * 256 + 128:fc * 256 + 256],
                                    g_fm_v[:, fc, sh * 512:(sh + 1) * 512],
                                    fc == 0, fc == 7, inc=(fc == 7))
                        t_lastmm_c = tk
                        wsl = [(sT, tk)]
                        if t_sil2[par]:
                            wsl.append((sV, t_sil2[par]))
                        t_sil = sop(
                            lambda s, pa=pa, ssl=ssl: s.activation(
                                ssl[:, :], pa, AF.Silu),
                            waits=wsl, inc=True,
                        )
                        t_silu_done = vop(
                            lambda v, c=c, sh=sh, ssl=ssl, pg=pg: v.tensor_mul(
                                t_b[:, c * 1024 + sh * 512:c * 1024 + (sh + 1) * 512],
                                ssl[:, :], pg),
                            waits=[(sS, t_sil)], inc=True,
                        )
                        t_sil2[par] = t_silu_done
                    if c + 2 < 32:
                        t_w1l[(e, c + 2)] = dma(
                            "scalar", w1img[:, (c % 2) * 2048:(c % 2 + 1) * 2048],
                            w1b_d[e, c + 2, :, :], gw[c % 2],
                            waits=[(sT, t_lastmm_c)],
                        )
                    if e > 0 and 2 <= c < 10:
                        j = c - 2
                        t_w2l[e] = dma(
                            "sync", w2_b[:, j * 4096:(j + 1) * 4096],
                            w2b_d[e, :, j * 4096:(j + 1) * 4096], gw2s,
                            waits=[(sT, t_mm2_done[e - 1])] if j == 0 else (),
                        )
                t_mm1_done[e] = sT.n
                if e + 1 < E:
                    sched_gload(e + 1, [(sT, t_mm1_done[e])])
                t_yev = {}
                for sm in range(8):
                    if sm == 2 and e + 1 < E:
                        sched_gtrans(e + 1)
                    for dh in range(2):
                        k2_ = 2 * sm + dh
                        bank = 2 + k2_ % 2
                        pyp = squad[:, bank * 512:(bank + 1) * 512]
                        w = [(sV, t_silu_done), (gw2s, t_w2l[e])]
                        if k2_ >= 2:
                            w.append((sV, t_yev[k2_ - 2]))
                        elif t_yev_last:
                            w.append((sV, t_yev_last))
                        for c in range(32):
                            tk = mm(
                                pyp,
                                t_b[:, c * 1024 + sm * 128:c * 1024 + (sm + 1) * 128],
                                w2_b[:, c * 1024 + dh * 512:c * 1024 + (dh + 1) * 512],
                                c == 0, c == 31, waits=w if c == 0 else (),
                                inc=(c == 31),
                            )
                        ybsl = ybuf if sm % 2 == 0 else yg
                        wv = [(sT, tk)]
                        if sm > 1:
                            wv.append((syY[sm % 2], t_ydma[(e, sm - 2)]))
                        elif e > 0:
                            wv.append((syY[sm % 2], t_ydma[(e - 1, 6 + sm)]))
                        t_yev[k2_] = vop(
                            lambda v, dh=dh, pyp=pyp, ybsl=ybsl: v.tensor_copy(
                                ybsl[:, dh * 512:(dh + 1) * 512], pyp),
                            waits=wv, inc=True,
                        )
                    t_ydma[(e, sm)] = dma(
                        "sync", y_d[e * CAP + sm * 128:e * CAP + (sm + 1) * 128, :],
                        ybsl[:, :], syY[sm % 2], waits=[(sV, t_yev[2 * sm + 1])],
                    )
                t_yev_last = t_yev[15]
                t_mm2_done[e] = sT.n
            t_ylast = t_ydma[(E - 1, 7)]

            t_o_v = [0] * NT
            t_sc_v = [0] * NT
            t_outw = [0] * NT
            t_x2l_v = [0] * NT
            ygs = [yg, yg2]
            x2s = [x2t, x2b]
            outs_ = [outt, outt2]
            for tt in range(NT):
                sl = tt % 2
                w = [(syY[0], syY[0].n), (syY[1], syY[1].n), (sV, t_desti)]
                if tt >= 2:
                    w.append((sS, t_sc_v[tt - 2]))

                def mkg(eng, tt=tt, sl=sl):
                    return eng.indirect_dma_start(
                        out=ygs[sl][:, :], out_offset=None,
                        in_=y_d[0:128, :],
                        in_offset=IndirectOffsetOnAxis(dest_i[:, tt:tt + 1], 0),
                    )
                OP("gpsimd", mkg, waits=w, incs=[(gy[sl], 16)])
                t_yg = gy[sl].n
                w = [(sc_[0], sc_[0].n), (sc_[1], sc_[1].n)]
                if tt >= 2:
                    w.append((sV, t_o_v[tt - 2]))
                t_x2l_v[tt] = dma("sync", x2s[sl][:, :],
                                  x2_d[tt * 128:(tt + 1) * 128, :],
                                  sx[sl], waits=w)
                w = [(gy[sl], t_yg)]
                if tt >= 2:
                    w.append((syOut, t_outw[tt - 2]))
                t_sc_v[tt] = sop(
                    lambda s, tt=tt, sl=sl: s.activation(
                        outs_[sl][:, :], ygs[sl][:, :], AF.Copy,
                        scale=maxprob[:, tt:tt + 1]),
                    waits=w, inc=True,
                )
                t_o_v[tt] = vop(
                    lambda v, sl=sl: v.tensor_add(
                        outs_[sl][:, :], outs_[sl][:, :], x2s[sl][:, :]),
                    waits=[(sS, t_sc_v[tt]), (sx[sl], t_x2l_v[tt])], inc=True,
                )
                t_outw[tt] = dma("sync", out_d[tt * 128:(tt + 1) * 128, :],
                                 outs_[sl][:, :], syOut,
                                 waits=[(sV, t_o_v[tt])])

        if "d_oh" in dbg:
            dma("sync", dbg["d_oh"][:, :], oh_all[:, :], syMisc,
                waits=[(sV, sV.n)])
            dma("sync", dbg["d_stats"][:, :], stats[:, :], syMisc)
            dma("sync", dbg["d_eidx"][:, :], eidx[:, :], syMisc)
        if "d_logits" in dbg:
            dma("sync", dbg["d_logits"][:, :], logits[:, :], syMisc,
                waits=[(sV, sV.n)])
            dma("sync", dbg["d_rbrep"][:, :], rbrep[:, :], syMisc)
        if "d_x2acc" in dbg:
            dma("sync", dbg["d_x2acc"][:, :], x2acc_d[:, :], syMisc,
                waits=list(t_projall))
        if "d_x2" in dbg:
            dma("sync", dbg["d_x2"][:, :], x2_d[:, :], syMisc,
                waits=[(sc_[0], sc_[0].n), (sc_[1], sc_[1].n)])
        if "d_xn2" in dbg:
            dma("sync", dbg["d_xn2"][:, :], xn2_d[:, :], syMisc,
                waits=[(sw[0], sw[0].n), (sw[1], sw[1].n)])
        if "d_g" in dbg:
            dma("sync", dbg["d_g"][:, :], g_d[:, :], syMisc,
                waits=[(gs[0], gs[0].n), (gs[1], gs[1].n)])
        if "d_y" in dbg:
            dma("sync", dbg["d_y"][:, :], y_d[:, :], syMisc,
                waits=[(syY[0], syY[0].n), (syY[1], syY[1].n)])
        if "d_router" in dbg:
            dma("sync", dbg["d_router"][:, 0:32], eidx[:, :], syMisc,
                waits=[(sV, t_desti)])
            dma("sync", dbg["d_router"][:, 32:64], maxprob[:, :], syMisc)
            dma("sync", dbg["d_router"][:, 64:96], dest_f[:, :], syMisc)

        allsems = [sQ, syMisc, syTab, syB, syG, syY[0], syY[1], syOut, gi, gb, gc, gw2s,
                   sx[0], sx[1], sa[0], sa[1], sn[0], sn[1], sw[0], sw[1],
                   sc_[0], sc_[1], gq[0], gq[1], gp[0], gp[1],
                   gw[0], gw[1], ga[0], ga[1], gs[0], gs[1], gy[0], gy[1]]
        OP("sync", lambda e: e.nop(),
           waits=[(c, c.n) for c in allsems])

        block = es.enter_context(nc.Block())

        @block.sync
        def _(eng):
            sch.replay("sync", eng)

        @block.gpsimd
        def _(eng):
            sch.replay("gpsimd", eng)

        @block.tensor
        def _(eng):
            sch.replay("tensor", eng)

        @block.vector
        def _(eng):
            sch.replay("vector", eng)

        @block.scalar
        def _(eng):
            sch.replay("scalar", eng)

    return nc


def make_inputs(L0_attn_x, L0_attn_Wqkv, L0_attn_Wproj, L0_attn_cascade_w,
                L0_ffn_router_w, L0_ffn_router_b, L0_ffn_w1, L0_ffn_w2):
    x = np.ascontiguousarray(np.asarray(L0_attn_x, dtype=np.float32))
    wqkv_r, wproj_r, rw_r, w1_r, w2_r = prep_shared(
        np.asarray(L0_attn_Wqkv, np.float32),
        np.asarray(L0_attn_Wproj, np.float32),
        np.asarray(L0_ffn_router_w, np.float32),
        np.asarray(L0_ffn_w1, np.float32),
        np.asarray(L0_ffn_w2, np.float32),
    )
    tabs = prep_tables()
    ident, bigmask, ltri, erow = prep_consts()
    rb = np.ascontiguousarray(np.asarray(L0_ffn_router_b, np.float32).reshape(1, E))
    cw = np.ascontiguousarray(np.asarray(L0_attn_cascade_w, np.float32).reshape(1, H - 1))
    shared = dict(
        wqkv=wqkv_r, wproj=wproj_r, rw=rw_r, rb=rb, cw=cw,
        w1r=w1_r, w2r=w2_r, tabs=tabs, identc=ident, bigmask=bigmask,
        ltri=ltri, erow=erow,
    )
    return x, shared


def kernel(L0_attn_x, L0_attn_Wqkv, L0_attn_Wproj, L0_attn_cascade_w,
           L0_ffn_router_w, L0_ffn_router_b, L0_ffn_w1, L0_ffn_w2):
    x, shared = make_inputs(
        L0_attn_x, L0_attn_Wqkv, L0_attn_Wproj, L0_attn_cascade_w,
        L0_ffn_router_w, L0_ffn_router_b, L0_ffn_w1, L0_ffn_w2)
    nc = build_program()
    in_maps = [dict(shared, x=np.ascontiguousarray(x[c])) for c in range(8)]
    res = run_bass_kernel_spmd(nc, in_maps, core_ids=list(range(8)))
    out = np.stack([res.results[c]["out"] for c in range(8)], axis=0)
    return out.astype(np.float32)


# version 12
# speedup vs baseline: 1.0146x; 1.0146x over previous
import numpy as np
import ml_dtypes
import concourse.bass as bass
import concourse.mybir as mybir
from concourse.bass_utils import run_bass_kernel_spmd
from concourse.bass_core_rust import IndirectOffsetOnAxis

F32 = mybir.dt.float32
BF16 = mybir.dt.bfloat16
I32 = mybir.dt.int32
U8 = mybir.dt.uint8
AF = mybir.ActivationFunctionType
ALU = mybir.AluOpType
AX = mybir.AxisListType.X

T = 4096
D = 1024
H = 16
HD = 64
NT = T // 128
E = 8
FFN = 4096
CAP = 1024
SLOTS = E * CAP
M0 = 25.0
ROPE_BASE = 10000.0
KB = 1024


class Ctr:
    __slots__ = ("h", "n")

    def __init__(self, h):
        self.h = h
        self.n = 0


class Sched:
    def __init__(self):
        self.recs = []

    def op(self, eng, fn, waits=(), incs=()):
        w = [(c.h, v) for (c, v) in waits if v > 0]
        i = []
        for c, k in incs:
            c.n += k
            i.append((c.h, k))
        self.recs.append((eng, fn, w, i))

    def replay(self, name, eng):
        known = {}
        for e, fn, waits, incs in self.recs:
            if e != name:
                continue
            for sh, v in waits:
                if known.get(id(sh), -1) < v:
                    eng.wait_ge(sh, v)
                    known[id(sh)] = v
            ins = fn(eng)
            for sh, k in incs:
                ins.then_inc(sh, k)


def _perm64():
    return np.concatenate([np.arange(0, 64, 2), np.arange(1, 64, 2)])


def prep_shared(Wqkv, Wproj, router_w, w1, w2):
    p64 = _perm64()
    cols = np.empty((H, 192), dtype=np.int64)
    for h in range(H):
        cols[h] = np.concatenate(
            [64 * h + p64, 1024 + 64 * h + p64, 2048 + 64 * h + np.arange(64)]
        )
    wq = Wqkv.reshape(8, 128, 3072)
    wqkv_r = np.ascontiguousarray(
        wq[:, :, cols].transpose(2, 1, 0, 3).reshape(H, 128, 8 * 192)
    )
    wproj_r = np.ascontiguousarray(
        Wproj.reshape(8, 128, 1024).transpose(1, 0, 2).reshape(128, 8192)
    )
    rw_r = np.ascontiguousarray(
        router_w.reshape(8, 128, 8).transpose(1, 0, 2).reshape(128, 64)
    )
    w1v = w1.reshape(E, 8, 128, 2 * FFN)
    idx = np.empty((32, 256), dtype=np.int64)
    for c in range(32):
        idx[c, :128] = c * 128 + np.arange(128)
        idx[c, 128:] = FFN + c * 128 + np.arange(128)
    w1_r = np.ascontiguousarray(
        w1v[:, :, :, idx].transpose(0, 3, 2, 1, 4).reshape(E, 32, 128, 8 * 256)
    )
    w2_r = np.ascontiguousarray(
        w2.reshape(E, 32, 128, 1024).transpose(0, 2, 1, 3).reshape(E, 128, 32 * 1024)
    )
    return wqkv_r, wproj_r, rw_r, w1_r, w2_r


def prep_tables():
    t = np.arange(T, dtype=np.float32)
    inv_freq = (1.0 / ROPE_BASE ** (np.arange(0, HD, 2, dtype=np.float32) / HD)).astype(
        np.float32
    )
    tabs = np.empty((H, 128, 4096), dtype=np.float32)

    def blk(m):
        return m.reshape(NT, 128, 32).transpose(1, 0, 2).reshape(128, NT * 32)

    for h in range(H):
        fs = np.float32(ROPE_BASE ** (h / H))
        freqs = np.outer(t, (inv_freq * fs).astype(np.float32)).astype(np.float32)
        cos = np.cos(freqs).astype(np.float32)
        sin = np.sin(freqs).astype(np.float32)
        tabs[h, :, 0:1024] = blk(cos * np.float32(0.125))
        tabs[h, :, 1024:2048] = blk(sin * np.float32(0.125))
        tabs[h, :, 2048:3072] = blk(cos)
        tabs[h, :, 3072:4096] = blk(sin)
    return tabs


def prep_consts():
    ident = np.eye(128, dtype=np.float32)
    i = np.arange(128)[:, None]
    c = np.arange(896)[None, :]
    bigmask = (i <= c - 384).astype(np.float32)
    k = np.arange(128)[:, None]
    m = np.arange(128)[None, :]
    ltri = (k < m).astype(np.float32)
    erow = np.tile(np.arange(E, dtype=np.float32), (128, 1))
    return ident, bigmask, ltri, erow


def build_program(debug=(), stages=21, dbghead=0):
    nc = bass.Bass("TRN2", target_bir_lowering=False)

    x_d = nc.dram_tensor("x", [T, D], F32, kind="ExternalInput").ap()
    wqkv_d = nc.dram_tensor("wqkv", [H, 128, 1536], F32, kind="ExternalInput").ap()
    wproj_d = nc.dram_tensor("wproj", [128, 8192], F32, kind="ExternalInput").ap()
    rw_d = nc.dram_tensor("rw", [128, 64], F32, kind="ExternalInput").ap()
    rb_d = nc.dram_tensor("rb", [1, E], F32, kind="ExternalInput").ap()
    cw_d = nc.dram_tensor("cw", [1, H - 1], F32, kind="ExternalInput").ap()
    w1_d = nc.dram_tensor("w1r", [E, 32, 128, 2048], F32, kind="ExternalInput").ap()
    w2_d = nc.dram_tensor("w2r", [E, 128, 32 * 1024], F32, kind="ExternalInput").ap()
    tab_d = nc.dram_tensor("tabs", [H, 128, 4096], F32, kind="ExternalInput").ap()
    ident_d = nc.dram_tensor("identc", [128, 128], F32, kind="ExternalInput").ap()
    mask_d = nc.dram_tensor("bigmask", [128, 896], F32, kind="ExternalInput").ap()
    ltri_d = nc.dram_tensor("ltri", [128, 128], F32, kind="ExternalInput").ap()
    erow_d = nc.dram_tensor("erow", [128, E], F32, kind="ExternalInput").ap()
    out_d = nc.dram_tensor("out", [T, D], F32, kind="ExternalOutput").ap()

    x2acc_d = nc.dram_tensor("x2acc", [T, D], F32, kind="Internal").ap()
    x2_d = nc.dram_tensor("x2s", [T, D], F32, kind="Internal").ap()
    xn2_d = nc.dram_tensor("xn2s", [T, D], BF16, kind="Internal").ap()
    g_d = nc.dram_tensor("gbuf", [SLOTS, D], BF16, kind="Internal").ap()
    y_d = nc.dram_tensor("ybufd", [SLOTS, D], F32, kind="Internal").ap()
    w1b_d = nc.dram_tensor("w1b", [E, 32, 128, 2048], BF16, kind="Internal").ap()
    w2b_d = nc.dram_tensor("w2b", [E, 128, 32 * 1024], BF16, kind="Internal").ap()
    ttot_dram = nc.dram_tensor("ttotd", [1, 256], F32, kind="Internal").ap()
    tpre_dram = nc.dram_tensor("tpred", [1, 256], F32, kind="Internal").ap()
    brow_dram = nc.dram_tensor("browd", [1, 4096], BF16, kind="Internal").ap()

    dbg = {}
    for name, shape, dt in debug:
        dbg[name] = nc.dram_tensor(name, shape, dt, kind="ExternalOutput").ap()

    from contextlib import ExitStack

    es = ExitStack()

    def sbuf(name, shape, dt):
        return es.enter_context(nc.sbuf_tensor(name, shape, dt))

    def psum(name, shape, dt):
        return es.enter_context(nc.psum_tensor(name, shape, dt))

    with es:
        ident_b = sbuf("ident_b", [128, 128], BF16)
        ident_f = sbuf("ident_f", [128, 128], F32)
        mask_b = sbuf("mask_b", [128, 896], BF16)
        ltri_b = sbuf("ltri_b", [128, 128], BF16)
        erow_f = sbuf("erow_f", [128, E], F32)
        rw_b = sbuf("rw_b", [128, 64], BF16)
        rb_row = sbuf("rb_row", [1, E], F32)
        rbrep = sbuf("rbrep", [128, E], F32)
        ones_f = sbuf("ones_f", [1, 128], F32)
        ones_bf = sbuf("ones_bf", [1, 128], BF16)
        ones128 = sbuf("ones128", [128, 1], BF16)
        cws = sbuf("cws", [1, 16], F32)
        cwsc = sbuf("cwsc", [1, 16], F32)
        csts = sbuf("csts", [128, 4], F32)
        csts2 = sbuf("csts2", [128, 1], F32)
        stats = sbuf("stats", [128, 24], F32)
        bias_acc = sbuf("bias_acc", [128, 32], F32)
        brow_f = sbuf("brow_f", [32, 128], F32)
        brow_b = sbuf("brow_b", [32, 128], BF16)
        wqkv_b = sbuf("wqkv_b", [128, 2 * 1536], BF16)
        wproj_b = sbuf("wproj_b", [128, 2 * 1024], BF16)
        v_ext = sbuf("v_ext", [128, 2 * NT * 65], BF16)
        invrep = sbuf("invrep", [128, 512], BF16)
        invr_b = sbuf("invr_b", [1, 512], BF16)
        trash = sbuf("trash", [128, 512], BF16)
        trash2 = sbuf("trash2", [128, 512], BF16)
        silu_f = sbuf("silu_f", [128, 512], F32)
        invr_f2 = silu_f
        maxprob = sbuf("maxprob", [128, 32], F32)
        eidx = sbuf("eidx", [128, 32], F32)
        oh_all = sbuf("oh_all", [128, 32 * E], BF16)
        intile = sbuf("intile", [128, 32 * E], F32)
        tprer = sbuf("tprer", [128, 32 * E], F32)
        ttot_row = sbuf("ttot_row", [1, 256], F32)
        ttot_f = sbuf("ttot_f", [32, E], F32)
        ttot_b = sbuf("ttot_b", [32, E], BF16)
        tpre_f = sbuf("tpre_f", [32, E], F32)
        tpre_row = sbuf("tpre_row", [1, 256], F32)
        dest_f = sbuf("dest_f", [128, 32], F32)
        dest_i = sbuf("dest_i", [128, 32], I32)
        xn2t = sbuf("xn2t", [128, 1024], BF16)
        xn2t2 = sbuf("xn2t2", [128, 1024], BF16)
        xn2fmt = sbuf("xn2fmt", [128, 8 * 128], BF16)
        logits = sbuf("logits", [128, 32 * E], F32)
        bparts = logits

        nc.const_aps.register(F32, -M0, csts[:, 0:1])
        nc.const_aps.register(F32, 1e-5, csts[:, 1:2])
        nc.const_aps.register(F32, -0.5, csts[:, 2:3])
        nc.const_aps.register(F32, 0.0, csts[:, 3:4])
        nc.const_aps.register(F32, -1.0, csts2[:, 0:1])

        AR = 164 * KB
        arena = es.enter_context(nc.sbuf_tensor("arena", [128, AR], U8))
        ar_base = nc.lookup_mloc(arena).addr

        def at(name, off, shape, dt):
            return nc.alloc_sbuf_tensor_at(name, shape, dt, offset=ar_base + off)

        xn_fm = at("xn_fm", 0, [128, 8 * 4096], BF16)
        costab0 = at("costab", 64 * KB, [128, 4096], F32)
        qk_tm = at("qk_tm", 80 * KB, [128, NT * 128], BF16)
        qrot = at("qrot", 88 * KB, [128, NT * 64], BF16)
        krot = at("krot", 92 * KB, [128, NT * 64], BF16)
        qfm = at("qfm", 96 * KB, [128, 4096], BF16)
        kfm = at("kfm", 104 * KB, [128, 4096], BF16)
        pT = at("pT", 112 * KB, [128, 32 * 512], BF16)
        outcat2 = at("outcat2", 144 * KB, [128, 4096], BF16)
        projpart = at("projpart", 112 * KB, [128, 2 * 2048], F32)
        ln_scr = at("ln_scr", 128 * KB, [128, 1024], F32)
        w2_b = at("w2_b", 0, [128, 32 * 1024], BF16)
        t_b = at("t_b", 64 * KB, [128, 32 * 1024], BF16)
        g_fm = at("g_fm", 128 * KB, [128, 8 * 1024], BF16)
        w1img = at("w1img", 144 * KB, [128, 2 * 2048], BF16)
        ybuf = at("ybuf", 152 * KB, [128, 1024], F32)
        yg = at("yg", 156 * KB, [128, 1024], F32)
        x2t = at("x2t", 160 * KB, [128, 1024], F32)
        outt = at("outt", 152 * KB, [128, 1024], F32)
        x2b = at("x2b", 136 * KB, [128, 1024], F32)
        outt2 = at("outt2", 128 * KB, [128, 1024], F32)
        yg2 = at("yg2", 132 * KB, [128, 1024], F32)
        zrow = trash

        pt_ps = psum("pt_ps", [128, 1024], BF16)
        squad = psum("squad", [128, 2048], F32)
        pf5 = psum("pf5", [128, 512], F32)
        pf6 = psum("pf6", [128, 512], F32)
        pf7 = psum("pf7", [128, 512], F32)

        def newsem(name):
            return Ctr(es.enter_context(nc.semaphore(name)))

        sT = newsem("sT")
        sV = newsem("sV")
        sS = newsem("sS")
        sQ = newsem("sQ")
        syMisc = newsem("syMisc")
        syTab = newsem("syTab")
        syB = newsem("syB")
        syG = newsem("syG")
        syY = [newsem("syY0"), newsem("syY1")]
        syOut = newsem("syOut")
        sx = [newsem("sx0"), newsem("sx1")]
        sa = [newsem("sa0"), newsem("sa1")]
        sn = [newsem("sn0"), newsem("sn1")]
        sw = [newsem("sw0"), newsem("sw1")]
        sc_ = [newsem("sc0"), newsem("sc1")]
        gi = newsem("gi")
        gq = [newsem("gq0"), newsem("gq1")]
        gp = [newsem("gp0"), newsem("gp1")]
        gb = newsem("gb")
        gc = newsem("gc")
        sP = newsem("sP")
        gw = [newsem("gw0"), newsem("gw1")]
        gw2s = newsem("gw2s")
        ga = [newsem("ga0"), newsem("ga1")]
        gs = [newsem("gs0"), newsem("gs1")]
        gy = [newsem("gy0"), newsem("gy1")]

        sch = Sched()
        OP = sch.op

        def dma(queue, out_ap, in_ap, ctr, waits=(), accum=False):
            if accum:
                fn = lambda eng, o=out_ap, i=in_ap: eng.dma_start(
                    out=o, in_=i, accum_op=ALU.add
                )
            else:
                fn = lambda eng, o=out_ap, i=in_ap: eng.dma_start(out=o, in_=i)
            OP(queue, fn, waits=waits, incs=[(ctr, 16)])
            return ctr.n

        def mm(out_ap, lhsT, rhs, start, stop, waits=(), inc=False):
            fn = lambda eng, o=out_ap, l=lhsT, r=rhs, a=start, b=stop: eng.matmul(
                o, l, r, start=a, stop=b
            )
            OP("tensor", fn, waits=waits, incs=[(sT, 1)] if inc else ())
            return sT.n

        def tp(out_ap, in_ap, ident, waits=(), inc=False):
            fn = lambda eng, o=out_ap, i=in_ap, d=ident: eng.transpose(o, i, d)
            OP("tensor", fn, waits=waits, incs=[(sT, 1)] if inc else ())
            return sT.n

        def vop(fn, waits=(), inc=False):
            OP("vector", fn, waits=waits, incs=[(sV, 1)] if inc else ())
            return sV.n

        def sop(fn, waits=(), inc=False):
            OP("scalar", fn, waits=waits, incs=[(sS, 1)] if inc else ())
            return sS.n

        dma("sync", ident_f[:, :], ident_d[:, :], syMisc)
        dma("sync", erow_f[:, :], erow_d[:, :], syMisc)
        dma("sync", rb_row[0:1, :], rb_d[0:1, :], syMisc)
        t_misc4 = dma("sync", cws[0:1, 0:15], cw_d[0:1, :], syMisc)
        dma("gpsimd", ident_b[:, :], ident_d[:, :], gi)
        dma("gpsimd", mask_b[:, :], mask_d[:, :], gi)
        dma("gpsimd", ltri_b[:, :], ltri_d[:, :], gi)
        t_gi = dma("gpsimd", rw_b[:, :], rw_d[:, :], gi)

        vop(lambda v: v.memset(csts[:, 0:1], -M0))
        vop(lambda v: v.memset(csts[:, 1:2], 1e-5))
        vop(lambda v: v.memset(csts[:, 2:3], -0.5))
        vop(lambda v: v.memset(csts[:, 3:4], 0.0))
        vop(lambda v: v.memset(csts2[:, 0:1], -1.0))
        vop(lambda v: v.memset(ones_f[:, :], 1.0))
        vop(lambda v: v.memset(ones_bf[:, :], 1.0))
        vop(lambda v: v.memset(ones128[:, :], 1.0))
        vop(lambda v: v.memset(bias_acc[:, :], 0.0))
        vop(lambda v: v.memset(logits[:, :], 0.0))
        vop(lambda v: v.memset(qfm[64:65, :], 0.0))
        vop(lambda v: v.memset(kfm[64:65, :], 0.0))
        vext_v = v_ext[:].rearrange("p (s t c) -> p s t c", c=65, t=NT)
        vop(lambda v: v.memset(vext_v[:, :, :, 64:65], 1.0))
        t_zrow = vop(lambda v: v.memset(zrow[:, :], 0.0), inc=True)
        t_cwsc = vop(
            lambda v: v.tensor_scalar_mul(cwsc[0:1, 0:15], cws[0:1, 0:15], 1.0 / T),
            waits=[(syMisc, t_misc4)],
            inc=True,
        )
        t_initv = t_cwsc

        t_mm_rb = mm(
            pf7[0:128, 0:E], ones_f[0:1, :], rb_row[0:1, :], True, True,
            waits=[(syMisc, t_misc4), (sV, t_initv)], inc=True,
        )
        t_rbrep = vop(
            lambda v: v.tensor_copy(rbrep[:, :], pf7[0:128, 0:E]),
            waits=[(sT, t_mm_rb)], inc=True,
        )

        g_flat = g_d.rearrange("r (a f) -> (r a) f", a=2)
        for j in range(128):
            t_gzero = dma(
                "scalar", g_flat[j * 128:(j + 1) * 128, :], zrow[:, :], sQ,
                waits=[(sV, t_zrow)] if j == 0 else (),
            )

        t_wqkv = [0] * (H + 1)
        t_tab = [0] * (H + 1)
        t_wproj = [0] * 9
        t_wqkv[0] = dma("gpsimd", wqkv_b[:, 0:1536], wqkv_d[0, :, :], gq[0])
        t_tab[0] = dma("sync", costab0[:, :], tab_d[0, :, :], syTab)
        t_wproj[0] = dma("gpsimd", wproj_b[:, 0:1024], wproj_d[:, 0:1024], gp[0])

        xn_fm_v = xn_fm[:].rearrange("p (c t) -> p c t", t=4096)

        def ln_stats(xs, c0, t_load_tok, load_ctr):
            t_sum = vop(lambda v: v.reduce_sum(stats[:, c0:c0 + 1], xs, axis=AX),
                        waits=[(load_ctr, t_load_tok)], inc=True)
            t_sq = sop(lambda s: s.activation(ln_scr[:, :], xs, AF.Square,
                                              accum_out=stats[:, c0 + 1:c0 + 2]),
                       waits=[(load_ctr, t_load_tok)], inc=True)
            t_msq = vop(lambda v: v.scalar_tensor_tensor(
                stats[:, c0 + 4:c0 + 5], stats[:, c0:c0 + 1], 1.0 / (D * D),
                stats[:, c0:c0 + 1], ALU.mult, ALU.mult),
                waits=[(sV, t_sum)], inc=True)
            vop(lambda v: v.tensor_scalar_mul(
                stats[:, c0 + 2:c0 + 3], stats[:, c0:c0 + 1], 1.0 / D))
            t_var = vop(lambda v: v.scalar_tensor_tensor(
                stats[:, c0 + 5:c0 + 6], stats[:, c0 + 1:c0 + 2], 1.0 / D,
                stats[:, c0 + 4:c0 + 5], ALU.mult, ALU.subtract),
                waits=[(sS, t_sq), (sV, t_msq)], inc=True)
            t_ln = sop(lambda s: s.activation(
                stats[:, c0 + 6:c0 + 7], stats[:, c0 + 5:c0 + 6], AF.Ln,
                bias=1e-5), waits=[(sV, t_var)], inc=True)
            t_rstd = sop(lambda s: s.activation(
                stats[:, c0 + 7:c0 + 8], stats[:, c0 + 6:c0 + 7], AF.Exp,
                scale=-0.5), waits=[(sS, t_ln)], inc=True)
            t_negmr = vop(lambda v: v.scalar_tensor_tensor(
                stats[:, c0 + 8:c0 + 9], stats[:, c0 + 2:c0 + 3], -1.0,
                stats[:, c0 + 7:c0 + 8], ALU.mult, ALU.mult),
                waits=[(sS, t_rstd)], inc=True)
            return stats[:, c0 + 7:c0 + 8], stats[:, c0 + 8:c0 + 9], t_negmr, t_rstd

        t_xload = [0] * NT
        t_xuse = [0] * NT
        t_p0evac = [0] * NT
        t_p0tp = [0] * NT
        xslots0 = [x2t, x2b]
        lnrec = at("lnrec", 144 * KB, [128, NT * 4], F32)
        xnslots0 = [xn2t, xn2t2]
        for tt in range(NT):
            slot = tt % 2
            w = [(sS, t_xuse[tt - 2])] if tt >= 2 else ()
            t_xload[tt] = dma(
                "sync", xslots0[slot][:, :],
                x_d[tt * 128:(tt + 1) * 128, :], sx[slot], waits=w,
            )
            xs = xslots0[slot][:, :]
            c0 = slot * 10
            rstd, negmr, t_negmr, t_rstdtok = ln_stats(xs, c0, t_xload[tt], sx[slot])
            if "d_lnrec" in dbg:
                vop(lambda v, c0=c0, tt=tt: v.tensor_copy(
                    lnrec[:, tt * 4:tt * 4 + 4], stats[:, c0 + 2:c0 + 6]))
            xnsl0 = xnslots0[slot]
            w = [(sV, t_negmr), (sS, t_rstdtok)]
            if tt >= 2:
                w.append((sT, t_p0tp[tt - 2]))
            t_xn = sop(
                lambda s, xs=xs, rstd=rstd, negmr=negmr, xnsl0=xnsl0: s.activation(
                    xnsl0[:, :], xs, AF.Identity, scale=rstd, bias=negmr),
                waits=w, inc=True,
            )
            t_xuse[tt] = t_xn
            w = [(sS, t_xn), (gi, t_gi)]
            if tt >= 1:
                w.append((sV, t_p0evac[tt - 1]))
            for fc in range(8):
                tk = tp(
                    pt_ps[:, fc * 128:(fc + 1) * 128],
                    xnsl0[:, fc * 128:(fc + 1) * 128],
                    ident_b[:, :], waits=w if fc == 0 else (), inc=(fc == 7),
                )
            t_p0tp[tt] = tk
            t_p0evac[tt] = vop(
                lambda v, tt=tt: v.tensor_copy(
                    xn_fm_v[:, :, tt * 128:(tt + 1) * 128],
                    pt_ps[:].rearrange("p (c t) -> p c t", t=128),
                ),
                waits=[(sT, tk)], inc=True,
            )
        t_xnfm = t_p0evac[NT - 1]
        if "d_xnfm" in dbg:
            dma("sync", dbg["d_xnfm"][:, :], xn_fm[:, :], syMisc,
                waits=[(sV, t_xnfm)])
        if "d_lnrec" in dbg:
            dma("sync", dbg["d_lnrec"][:, :], lnrec[:, :], syMisc,
                waits=[(sV, t_xnfm)])

        HEADS = 0 if stages == 0 else min(stages, 16)
        DBGHEAD = dbghead
        NT2 = NT if stages >= 20 else 0
        P3PLUS = stages >= 21
        t_desti = 0
        t_ylast = 0
        t_scatall = []
        qk_v = qk_tm[:].rearrange("p (t c) -> p t c", c=128)
        qr_v = qrot[:].rearrange("p (t c) -> p t c", c=64)
        kr_v = krot[:].rearrange("p (t c) -> p t c", c=64)
        tmp_v = xn2t[:].rearrange("p (t c) -> p t c", c=32)

        t_exp_last = 0
        t_brow = [0] * (H + 1)
        t_qrow = [0] * (H + 1)
        t_rope = [0] * H
        t_qkvmm = [0] * H
        t_fmtp = [0] * H
        t_pv_lastnorm = 0
        t_pvlast = [0] * H
        t_subA_evac = 0
        t_lastprojdma = None
        t_projmm = [0] * 8
        t_projevac = [0] * 8
        t_projdma = [[0] * 16 for _ in range(8)]
        pt_lastread = [0] * 32
        t_exp_half = {0: 0, 1: 0}
        t_vext_last = 0

        def pl31(v):
            if isinstance(v, tuple):
                return (sV if v[0] == "V" else sP, v[1])
            return (sV, v)

        t_qkve_par = [0, 0]
        t_vext_done = [0] * (H + 1)
        t_fmdone = [0] * (H + 1)
        t_fmevac_roll = 0
        t_exp_prev = 0

        def emit_qkv_group(hh, tt):
            nonlocal t_fmevac_roll
            bank = 2 + tt % 2
            pq = squad[:, bank * 512:bank * 512 + 192]
            wq = wqkv_b[:, (hh % 2) * 1536:(hh % 2 + 1) * 1536]
            sl = hh % 2
            w = [(gq[hh % 2], t_wqkv[hh]), (sV, t_xnfm)]
            if t_qkve_par[tt % 2]:
                w.append((sS, t_qkve_par[tt % 2]))
            if tt == 0 and hh >= 2:
                w.append((sT, t_pvlast[hh - 2]))
            for fc in range(8):
                tk = mm(pq, xn_fm_v[:, fc, tt * 128:(tt + 1) * 128],
                        wq[:, fc * 192:(fc + 1) * 192], fc == 0, fc == 7,
                        waits=w if fc == 0 else (), inc=(fc == 7))
            t_qkvmm[hh] = tk
            sop(lambda s, tt=tt, pq=pq: s.activation(
                qk_v[:, tt, :], pq[:, 0:128], AF.Copy), waits=[(sT, tk)])
            tok = sop(lambda s, tt=tt, pq=pq, sl=sl: s.activation(
                vext_v[:, sl, tt, 0:64], pq[:, 128:192], AF.Copy), inc=True)
            t_qkve_par[tt % 2] = tok
            if tt == NT - 1:
                t_vext_done[hh] = tok
            return tok

        def emit_rope(hh):
            x1q, x2q = qk_v[:, :, 0:32], qk_v[:, :, 32:64]
            x1k, x2k = qk_v[:, :, 64:96], qk_v[:, :, 96:128]
            ct = costab0[:].rearrange("p (b t i) -> p b t i", b=4, i=32)
            cq, sq_, ck, sk = ct[:, 0], ct[:, 1], ct[:, 2], ct[:, 3]
            q1, q2 = qr_v[:, :, 0:32], qr_v[:, :, 32:64]
            k1, k2 = kr_v[:, :, 0:32], kr_v[:, :, 32:64]
            w = [(syTab, t_tab[hh]), (sS, t_qkve_par[(NT - 1) % 2])]
            if hh >= 1:
                w.append((sT, t_fmtp[hh - 1]))
            vop(lambda v: v.tensor_mul(q1, x1q, cq), waits=w)
            vop(lambda v: v.tensor_mul(tmp_v, x2q, sq_))
            vop(lambda v: v.tensor_sub(q1, q1, tmp_v))
            vop(lambda v: v.tensor_mul(q2, x1q, sq_))
            vop(lambda v: v.tensor_mul(tmp_v, x2q, cq))
            vop(lambda v: v.tensor_add(q2, q2, tmp_v))
            vop(lambda v: v.tensor_mul(k1, x1k, ck))
            vop(lambda v: v.tensor_mul(tmp_v, x2k, sk))
            vop(lambda v: v.tensor_sub(k1, k1, tmp_v))
            vop(lambda v: v.tensor_mul(k2, x1k, sk))
            vop(lambda v: v.tensor_mul(tmp_v, x2k, ck))
            t_rope[hh] = vop(lambda v: v.tensor_add(k2, k2, tmp_v), inc=True)
            if hh + 1 < H:
                t_tab[hh + 1] = dma(
                    "sync", costab0[:, :], tab_d[hh + 1, :, :], syTab,
                    waits=[(sV, t_rope[hh])],
                )

        def emit_fmtrans(hh):
            nonlocal t_fmevac_roll
            t_fmevac = t_fmevac_roll
            for grp in range(8):
                srcbuf = qrot if grp < 4 else krot
                dstbuf = qfm if grp < 4 else kfm
                g4 = grp % 4
                w = [(sV, t_rope[hh])]
                if t_fmevac:
                    w.append((sV, t_fmevac))
                for j in range(8):
                    tile = g4 * 8 + j
                    tk = tp(
                        pt_ps[0:64, j * 128:(j + 1) * 128],
                        srcbuf[:, tile * 64:(tile + 1) * 64],
                        ident_b[:, :],
                        waits=w if j == 0 else (), inc=(j == 7),
                    )
                t_fmevac = vop(
                    lambda v, dstbuf=dstbuf, g4=g4: v.tensor_copy(
                        dstbuf[0:64, g4 * 1024:(g4 + 1) * 1024],
                        pt_ps[0:64, :]),
                    waits=[(sT, tk)], inc=True,
                )
            t_fmtp[hh] = tk
            t_fmevac_roll = t_fmevac
            t_fmdone[hh] = t_fmevac

        for h in range(HEADS):
            if h + 1 < H:
                wv = [(sT, t_qkvmm[h - 1])] if h >= 1 else ()
                t_wqkv[h + 1] = dma(
                    "gpsimd",
                    wqkv_b[:, ((h + 1) % 2) * 1536:((h + 1) % 2 + 1) * 1536],
                    wqkv_d[h + 1, :, :], gq[(h + 1) % 2], waits=wv,
                )
            if h % 2 == 0 and h // 2 + 1 < 8:
                p = h // 2 + 1
                wv = [(sT, t_projmm[p - 2])] if p >= 2 else ()
                t_wproj[p] = dma(
                    "gpsimd", wproj_b[:, (p % 2) * 1024:(p % 2 + 1) * 1024],
                    wproj_d[:, p * 1024:(p + 1) * 1024], gp[p % 2], waits=wv,
                )
            for piece in range(3):
                pidx = h * 3 + piece
                if pidx < 32:
                    e_, cg = divmod(pidx, 4)
                    dma("gpsimd",
                        w1b_d[e_, cg * 8:(cg + 1) * 8, :, :],
                        w1_d[e_, cg * 8:(cg + 1) * 8, :, :], gc)
                elif pidx < 48:
                    e_, jg = divmod(pidx - 32, 2)
                    dma("gpsimd",
                        w2b_d[e_, :, jg * 16384:(jg + 1) * 16384],
                        w2_d[e_, :, jg * 16384:(jg + 1) * 16384], gc)

            if h == 0:
                for tt in range(NT):
                    emit_qkv_group(0, tt)
                emit_rope(0)
                emit_fmtrans(0)

            if h == DBGHEAD and "d_qfm" in dbg:
                dma("sync", dbg["d_qfm"][:, :], qfm[:, :], syMisc,
                    waits=[(sV, t_fmdone[h])])
                dma("sync", dbg["d_kfm"][:, :], kfm[:, :], syMisc)
                dma("sync", dbg["d_vext"][:, :], v_ext[:, :], syMisc)
                dma("sync", dbg["d_qk"][:, :], qk_tm[:, :], syMisc)

            pend_tail = None
            qkv_tt = 0 if h + 1 < H else NT

            def emit_tail(qg, nsc, tk_pv, t_maskg_last):
                nonlocal t_pv_lastnorm
                t_lnr = sop(lambda s: s.activation(
                    invr_f2[0:1, :], pf5[64:65, :], AF.Ln),
                    waits=[(sT, tk_pv)], inc=True)
                def _iexp(s):
                    with nc.allow_low_precision(reason="1/r stored bf16 as matmul operand"):
                        return s.activation(invr_b[0:1, :], invr_f2[0:1, :],
                                            AF.Exp, scale=-1.0)
                t_invb = sop(_iexp, waits=[(sS, t_lnr)], inc=True)
                w = [(sS, t_invb)]
                if h % 2 == 0 and h >= 2:
                    w.append((sV, t_projevac[h // 2 - 1]))
                t_repmm = mm(
                    pf6[0:128, :], ones_bf[0:1, :], invr_b[0:1, :], True, True,
                    waits=w, inc=True,
                )
                t_rep = vop(
                    lambda v: v.tensor_copy(invrep[:, :], pf6[0:128, :]),
                    waits=[(sT, t_repmm)], inc=True,
                )
                r0 = 64 * (h % 2)
                t_pv_lastnorm = vop(
                    lambda v, r0=r0, qg=qg: v.tensor_mul(
                        outcat2[r0:r0 + 64, qg * 512:(qg + 1) * 512],
                        pf5[0:64, :], invrep[0:64, :]),
                    waits=[(sV, t_rep)], inc=True,
                )
                for sc in range(nsc):
                    pt_lastread[sc] = ("V", vop(
                        lambda v, sc=sc, qg=qg: v.scalar_tensor_tensor(
                            trash[:, :], pT[:, sc * 512:(sc + 1) * 512],
                            1.0, invrep[:, :], ALU.mult, ALU.mult,
                            accum_out=bparts[:, sc * 8 + qg:sc * 8 + qg + 1]),
                        waits=[(sV, t_rep), (sQ, t_gzero),
                               (sP, t_maskg_last)],
                        inc=True,
                    ))

            for qg in range(8):
                nsc = 4 * qg + 4
                qsl = qfm[0:65, qg * 512:(qg + 1) * 512]
                t_expg = {}
                psl = squad[:, 0:1024]
                for scg in range(0, nsc, 2):
                    if qkv_tt < NT:
                        emit_qkv_group(h + 1, qkv_tt)
                        qkv_tt += 1
                        if qkv_tt == NT:
                            emit_rope(h + 1)
                    for j in range(2):
                        sc = scg + j
                        w = [(sV, t_fmdone[h]), (gb, t_brow[h]),
                             (sV, t_qrow[h]), (sS, t_exp_prev)]
                        mm(
                            psl[:, j * 512:(j + 1) * 512],
                            kfm[0:65, sc * 128:(sc + 1) * 128],
                            qsl, True, True,
                            waits=w if j == 0 else (), inc=(j == 1),
                        )
                    tmk = sT.n
                    if scg == 0 and pend_tail is not None:
                        pend_tail()
                        pend_tail = None
                    w = [(sT, tmk)]
                    if t_lastprojdma is not None:
                        w.extend(t_lastprojdma)
                    for sc in (scg, scg + 1):
                        if pt_lastread[sc]:
                            eng_, tok_ = pt_lastread[sc]
                            w.append((sV if eng_ == "V" else sP, tok_))
                    t_expg[scg] = sop(
                        lambda s, scg=scg, psl=psl: s.activation(
                            pT[:, scg * 512:(scg + 2) * 512], psl, AF.Exp,
                            bias=-M0),
                        waits=w, inc=True,
                    )
                    t_exp_prev = t_expg[scg]
                t_exp_last = t_expg[nsc - 2]
                t_maskg = 0
                for kk in range(4):
                    sc = 4 * qg + kk
                    OP("gpsimd",
                       lambda g, sc=sc, kk=kk: g.tensor_mul(
                           pT[:, sc * 512:(sc + 1) * 512],
                           pT[:, sc * 512:(sc + 1) * 512],
                           mask_b[:, 384 - 128 * kk:896 - 128 * kk]),
                       waits=[(sS.h, t_expg[sc - (sc % 2)])] if False else
                       [(sS, t_expg[sc - (sc % 2)])],
                       incs=[(sP, 1)])
                    t_maskg = sP.n
                for sc in range(nsc):
                    w = []
                    if sc == 0:
                        w.append((sV, t_pv_lastnorm))
                        w.append((sS, t_vext_done[h]))
                        if h % 2 == 0 and h >= 2:
                            w.append((sV, t_projevac[h // 2 - 1]))
                    w.append((sS, t_expg[sc - (sc % 2)]))
                    if sc >= 4 * qg:
                        w.append((sP, t_maskg))
                    tk = mm(
                        pf5[0:65, :],
                        v_ext[:, (h % 2) * NT * 65 + sc * 65:
                              (h % 2) * NT * 65 + (sc + 1) * 65],
                        pT[:, sc * 512:(sc + 1) * 512],
                        sc == 0, sc == nsc - 1,
                        waits=w, inc=(sc == nsc - 1),
                    )
                t_pvlast[h] = tk
                pend_tail = (lambda qg=qg, nsc=nsc, tk=tk, mg=t_maskg:
                             emit_tail(qg, nsc, tk, mg))
            pend_tail()
            pend_tail = None
            if h + 1 < H:
                emit_fmtrans(h + 1)

            if h == DBGHEAD and "d_outcat" in dbg:
                t_tmpred = vop(
                    lambda v: v.tensor_reduce(
                        bias_acc[:].rearrange("p (s o) -> p s o", o=1),
                        bparts[:].rearrange("p (s q) -> p s q", q=8),
                        AX, ALU.add),
                    waits=[pl31(pt_lastread[31]), (sP, sP.n)], inc=True,
                )
                dma("sync", dbg["d_outcat"][:, :], outcat2[:, :], syMisc,
                    waits=[(sV, t_pv_lastnorm)])
                if "d_bacc" in dbg:
                    dma("sync", dbg["d_bacc"][:, :], bias_acc[:, :], syMisc,
                        waits=[(sV, t_tmpred)])
                if "d_pt" in dbg:
                    dma("sync", dbg["d_pt"][:, :], pT[:, :], syMisc)

            if h + 1 < H:
                t_bred = vop(
                    lambda v: v.tensor_reduce(
                        bias_acc[:].rearrange("p (s o) -> p s o", o=1),
                        bparts[:].rearrange("p (s q) -> p s q", q=8),
                        AX, ALU.add),
                    waits=[pl31(pt_lastread[31]), (sP, sP.n)], inc=True,
                )
                t_bz = vop(lambda v: v.memset(bparts[:, :], 0.0),
                           waits=[(sV, t_bred)], inc=True)
                tk = tp(
                    pf7[0:32, 0:128], bias_acc[:, :], ident_f[:, :],
                    waits=[(sV, t_bred)], inc=True,
                )
                t_bev = vop(
                    lambda v: v.tensor_copy(brow_b[:, :], pf7[0:32, 0:128]),
                    waits=[(sT, tk)], inc=True,
                )
                t_bd1 = dma(
                    "sync",
                    brow_dram[0:1, :].rearrange("a (s i) -> (a s) i", i=128),
                    brow_b[:, :], gb, waits=[(sV, t_bev)])
                t_brow[h + 1] = dma(
                    "sync", kfm[64:65, :], brow_dram[0:1, :], gb,
                    waits=[(gb, t_bd1)],
                )
                t_qrow[h + 1] = vop(
                    lambda v, h=h: v.tensor_scalar(
                        qfm[64:65, :], qfm[64:65, :], 0.0,
                        cwsc[0:1, h:h + 1], ALU.mult, ALU.add),
                    waits=[(sV, t_cwsc), (sT, t_pvlast[h])], inc=True,
                )

            if h % 2 == 1:
                p = h // 2
                t_ev_hist = []
                for r in range(16):
                    rslot = r % 2
                    pp = projpart[:, rslot * 2048:(rslot + 1) * 2048]
                    for idx4 in range(4):
                        ti, ofh = divmod(idx4, 2)
                        tt = r * 2 + ti
                        pb = pf5 if idx4 % 2 == 0 else pf6
                        w = [(sV, t_pv_lastnorm), (gp[p % 2], t_wproj[p]),
                             pl31(pt_lastread[31]), (sP, sP.n)]
                        if len(t_ev_hist) >= 2:
                            w.append((sV, t_ev_hist[-2]))
                        tk = mm(
                            pb[:, :],
                            outcat2[:, tt * 128:(tt + 1) * 128],
                            wproj_b[:, (p % 2) * 1024 + ofh * 512:
                                    (p % 2) * 1024 + (ofh + 1) * 512],
                            True, True, waits=w, inc=True,
                        )
                        wv = [(sT, tk)]
                        if r >= 2 and idx4 == 0:
                            wv.append((ga[r % 2], t_projdma[p][r - 2]))
                        t_ev = vop(
                            lambda v, pp=pp, ti=ti, ofh=ofh, pb=pb: v.tensor_copy(
                                pp[:, ti * 1024 + ofh * 512:
                                   ti * 1024 + (ofh + 1) * 512], pb[:, :]),
                            waits=wv, inc=True,
                        )
                        t_ev_hist.append(t_ev)
                    w = [(sV, t_ev)]
                    if p >= 1:
                        w.append((ga[r % 2], t_projdma[p - 1][r]))
                    t_projdma[p][r] = dma(
                        "gpsimd",
                        x2acc_d[r * 256:(r + 1) * 256, :].rearrange(
                            "(a p) f -> p a f", p=128),
                        pp[:, :].rearrange("p (a f) -> p a f", f=1024),
                        ga[r % 2], waits=w, accum=(p > 0),
                    )
                t_projmm[p] = tk
                t_projevac[p] = t_ev
                t_lastprojdma = [(ga[0], t_projdma[p][14]),
                                 (ga[1], t_projdma[p][15])]

        t_projall = ([(ga[0], ga[0].n), (ga[1], ga[1].n)]
                     if HEADS == H else [])

        t_w1l = {}
        t_w2l = [0] * E
        t_castall = [(gc, gc.n)]
        if NT2 and P3PLUS:
            for c in range(2):
                t_w1l[(0, c)] = dma(
                    "scalar", w1img[:, (c % 2) * 2048:(c % 2 + 1) * 2048],
                    w1b_d[0, c, :, :], gw[c % 2],
                    waits=[(sT, t_projmm[7])] + t_castall,
                )
            for j in range(8):
                t_w2l[0] = dma(
                    "sync", w2_b[:, j * 4096:(j + 1) * 4096],
                    w2b_d[0, :, j * 4096:(j + 1) * 4096], gw2s,
                    waits=([(sT, t_qkvmm[15])] + t_castall) if j == 0 else (),
                )
        t_x2w = [0] * NT
        t_xn2w = [0] * NT
        t_lgev = [0] * NT
        t_add_v = [0] * NT
        t_xnuse = [0] * NT
        t_p2evac = 0
        t_totmm = 0
        x2slots = [yg, yg2]
        xslots2 = [x2t, x2b]
        xnslots = [xn2t, xn2t2]
        for tt in range(NT2):
            slot = tt % 2
            xsl = xslots2[slot][:, :]
            asl = x2slots[slot]
            w = list(t_projall)
            if tt >= 2:
                w.append((sS, t_xuse[tt - 2]))
                w.append((sc_[slot], t_x2w[tt - 2]))
            t_xl = dma("sync", xsl, x_d[tt * 128:(tt + 1) * 128, :], sx[slot],
                       waits=w)
            w2_ = ([(sV, t_add_v[tt - 2])] if tt >= 2 else list(t_projall))
            t_al = dma("sync", asl[:, :], x2acc_d[tt * 128:(tt + 1) * 128, :],
                       sa[slot], waits=w2_)
            t_add = vop(
                lambda v, xsl=xsl, asl=asl: v.tensor_add(xsl, xsl, asl[:, :]),
                waits=[(sa[slot], t_al), (sx[slot], t_xl)], inc=True,
            )
            t_add_v[tt] = t_add
            t_x2w[tt] = dma("sync", x2_d[tt * 128:(tt + 1) * 128, :], xsl,
                            sc_[slot], waits=[(sV, t_add)])
            c0 = slot * 10
            rstd, negmr, t_negmr, t_rstdtok = ln_stats(xsl, c0, t_add, sV)
            xnsl = xnslots[slot]
            w = [(sV, t_negmr), (sS, t_rstdtok)]
            if tt >= 2:
                w.append((sT, t_xnuse[tt - 2]))
                w.append((sw[slot], t_xn2w[tt - 2]))
            t_xn = sop(
                lambda s, xsl=xsl, rstd=rstd, negmr=negmr, xnsl=xnsl: s.activation(
                    xnsl[:, :], xsl, AF.Identity, scale=rstd, bias=negmr),
                waits=w, inc=True,
            )
            t_xuse[tt] = t_xn
            t_xn2w[tt] = dma("sync", xn2_d[tt * 128:(tt + 1) * 128, :],
                             xnsl[:, :], sw[slot], waits=[(sS, t_xn)])
            w = [(sS, t_xn)]
            if tt == 0:
                w.append((sV, t_subA_evac))
            if t_p2evac:
                w.append((sV, t_p2evac))
            for fc in range(8):
                tk = tp(
                    pt_ps[:, fc * 128:(fc + 1) * 128],
                    xnsl[:, fc * 128:(fc + 1) * 128],
                    ident_b[:, :], waits=w if fc == 0 else (), inc=(fc == 7),
                )
            t_xnuse[tt] = tk
            t_p2evac = vop(
                lambda v: v.tensor_copy(
                    xn2fmt[:].rearrange("p (c t) -> p c t", t=128),
                    pt_ps[:].rearrange("p (c t) -> p c t", t=128)),
                waits=[(sT, tk)], inc=True,
            )
            rslot = tt % 2
            w = [(sV, t_p2evac), (gi, t_gi)]
            if tt >= 1:
                w.append((sV, t_lgev[tt - 1]))
            for fc in range(8):
                tk = mm(
                    pf7[:, rslot * 8:rslot * 8 + 8],
                    xn2fmt[:, fc * 128:(fc + 1) * 128],
                    rw_b[:, fc * 8:(fc + 1) * 8],
                    fc == 0, fc == 7, waits=w if fc == 0 else (), inc=(fc == 7),
                )
            t_lgev[tt] = vop(
                lambda v, tt=tt, rslot=rslot: v.tensor_add(
                    logits[:, tt * 8:(tt + 1) * 8],
                    pf7[:, rslot * 8:rslot * 8 + 8], rbrep[:, :]),
                waits=[(sT, tk), (sV, t_rbrep)], inc=True,
            )
            cnm = 9 if slot == 0 else 19
            cse = 20 if slot == 0 else 21
            t_nm = vop(lambda v, tt=tt, cnm=cnm: v.reduce_max(
                stats[:, cnm:cnm + 1], logits[:, tt * 8:(tt + 1) * 8],
                axis=AX, negate=True), waits=[(sV, t_lgev[tt])], inc=True)
            t_se = sop(
                lambda s, tt=tt, cnm=cnm, cse=cse: s.activation(
                    trash[:, 0:8], logits[:, tt * 8:(tt + 1) * 8], AF.Exp,
                    bias=stats[:, cnm:cnm + 1],
                    accum_out=stats[:, cse:cse + 1]),
                waits=[(sV, t_nm)], inc=True,
            )
            vop(lambda v, tt=tt, cse=cse: v.reciprocal(
                maxprob[:, tt:tt + 1], stats[:, cse:cse + 1]),
                waits=[(sS, t_se)])
            t_oh = vop(
                lambda v, tt=tt, cnm=cnm: v.tensor_scalar(
                    oh_all[:, tt * 8:(tt + 1) * 8],
                    logits[:, tt * 8:(tt + 1) * 8],
                    stats[:, cnm:cnm + 1], 0.0, ALU.add, ALU.is_equal),
                waits=[(sV, t_nm)], inc=True,
            )
            vop(
                lambda v, tt=tt: v.scalar_tensor_tensor(
                    silu_f[:, 0:8], erow_f[:, :], 1.0,
                    oh_all[:, tt * 8:(tt + 1) * 8],
                    ALU.mult, ALU.mult, accum_out=eidx[:, tt:tt + 1]),
                waits=[(sV, t_oh)],
            )
            w = [(sV, t_oh), (gi, t_gi)]
            mm(pf6[:, tt * 8:(tt + 1) * 8], ltri_b[:, :],
               oh_all[:, tt * 8:(tt + 1) * 8], True, True, waits=w, inc=True)
            t_totmm = mm(
                pf5[0:1, tt * 8:(tt + 1) * 8], ones128[:, 0:1],
                oh_all[:, tt * 8:(tt + 1) * 8], True, True, inc=True)

        if P3PLUS:
            t_it = vop(lambda v: v.tensor_copy(intile[:, :], pf6[:, 0:256]),
                       waits=[(sT, t_totmm)], inc=True)
            t_tr = vop(lambda v: v.tensor_copy(ttot_row[0:1, :], pf5[0:1, 0:256]),
                       inc=True)
            t_td1 = dma("sync", ttot_dram[0:1, :], ttot_row[0:1, :], syB,
                        waits=[(sV, t_tr)])
            t_td2 = dma("sync", ttot_f[:, :],
                        ttot_dram[0:1, :].rearrange("a (s e) -> (a s) e", e=8),
                        syB, waits=[(syB, t_td1)])
            t_tb = vop(lambda v: v.tensor_copy(ttot_b[:, :], ttot_f[:, :]),
                       waits=[(syB, t_td2)], inc=True)
            t_tpmm = mm(pf7[0:32, 0:8], ltri_b[0:32, 0:32], ttot_b[:, :], True,
                        True, waits=[(sV, t_tb)], inc=True)
            t_tp_ = vop(lambda v: v.tensor_copy(tpre_f[:, :], pf7[0:32, 0:8]),
                        waits=[(sT, t_tpmm)], inc=True)
            t_tpd1 = dma("sync",
                         tpre_dram[0:1, :].rearrange("a (s e) -> (a s) e", e=8),
                         tpre_f[:, :], syB, waits=[(sV, t_tp_)])
            t_tpd2 = dma("sync", tpre_row[0:1, :], tpre_dram[0:1, :], syB,
                         waits=[(syB, t_tpd1)])
            for tt in range(NT):
                w = [(syB, t_tpd2), (sV, t_it)]
                t_rp = mm(
                    pf6[:, tt * 8:(tt + 1) * 8], ones_f[0:1, :],
                    tpre_row[0:1, tt * 8:(tt + 1) * 8], True, True,
                    waits=w if tt == 0 else (), inc=(tt == NT - 1),
                )
            t_rpe = vop(lambda v: v.tensor_copy(tprer[:, :], pf6[:, 0:256]),
                        waits=[(sT, t_rp)], inc=True)
            for tt in range(NT):
                vop(
                    lambda v, tt=tt: v.tensor_add(
                        intile[:, tt * 8:(tt + 1) * 8],
                        intile[:, tt * 8:(tt + 1) * 8],
                        tprer[:, tt * 8:(tt + 1) * 8]),
                    waits=[(sV, t_rpe)],
                )
            for tt in range(NT):
                vop(
                    lambda v, tt=tt: v.scalar_tensor_tensor(
                        logits[:, tt * 8:(tt + 1) * 8],
                        intile[:, tt * 8:(tt + 1) * 8], 1.0,
                        oh_all[:, tt * 8:(tt + 1) * 8],
                        ALU.mult, ALU.mult,
                        accum_out=tprer[:, tt * 8:tt * 8 + 1]),
                )
            for tt in range(NT):
                t_df = vop(
                    lambda v, tt=tt: v.tensor_scalar(
                        dest_f[:, tt:tt + 1], eidx[:, tt:tt + 1], float(CAP),
                        tprer[:, tt * 8:tt * 8 + 1], ALU.mult, ALU.add),
                    inc=(tt == NT - 1),
                )
            t_desti = vop(lambda v: v.tensor_copy(dest_i[:, :], dest_f[:, :]),
                          waits=[(sV, t_df)], inc=True)
            t_scat_v = [0] * NT
            t_xnl = [0] * NT
            for tt in range(NT):
                slot = tt % 2
                w = [(sw[0], sw[0].n), (sw[1], sw[1].n)]
                if tt >= 2:
                    w.append((gs[slot], t_scat_v[tt - 2]))
                t_xnl[tt] = dma("sync", xnslots[slot][:, :],
                                xn2_d[tt * 128:(tt + 1) * 128, :], sn[slot],
                                waits=w)

                def mk(eng, tt=tt, slot=slot):
                    return eng.indirect_dma_start(
                        out=g_d[0:128, :],
                        out_offset=IndirectOffsetOnAxis(dest_i[:, tt:tt + 1], 0),
                        in_=xnslots[slot][:, :],
                        in_offset=None,
                    )
                OP("gpsimd", mk,
                   waits=[(sn[slot], t_xnl[tt]), (sV, t_desti), (sQ, t_gzero)],
                   incs=[(gs[slot], 16)])
                t_scat_v[tt] = gs[slot].n
            t_scatall = [(gs[0], gs[0].n), (gs[1], gs[1].n)]

            g_fm_v = g_fm[:].rearrange("p (c s) -> p c s", s=1024)
            t_gload = [0] * E
            t_gtrans = [0] * E
            t_mm1_done = [0] * E
            t_mm2_done = [0] * E
            t_ydma = {}
            t_silu_done = 0
            t_sil2 = [0, 0]
            t_yev_last = 0

            def sched_gload(e, waits):
                t_gload[e] = dma(
                    "sync", g_fm[:].rearrange("p (s f) -> p s f", f=1024),
                    g_d[e * CAP:(e + 1) * CAP, :].rearrange("(s p) f -> p s f", p=128),
                    syG, waits=waits,
                )

            def sched_gtrans(e):
                pairs = [(a, b) for a in range(8) for b in range(a, 8)]
                tev = 0
                batch = []
                for pi, (a, b) in enumerate(pairs):
                    batch.append((a, b))
                    if len(batch) == 4 or pi == len(pairs) - 1:
                        w = [(syG, t_gload[e])]
                        if tev:
                            w.append((sV, tev))
                        nops = sum(1 if aa == bb else 2 for aa, bb in batch)
                        cnt = 0
                        for j, (aa, bb) in enumerate(batch):
                            cnt += 1
                            tk = tp(
                                pt_ps[:, (2 * j) * 128:(2 * j + 1) * 128],
                                g_fm[:, aa * 1024 + bb * 128:aa * 1024 + (bb + 1) * 128],
                                ident_b[:, :],
                                waits=w if cnt == 1 else (), inc=(cnt == nops),
                            )
                            if aa != bb:
                                cnt += 1
                                tk = tp(
                                    pt_ps[:, (2 * j + 1) * 128:(2 * j + 2) * 128],
                                    g_fm[:, bb * 1024 + aa * 128:bb * 1024 + (aa + 1) * 128],
                                    ident_b[:, :], inc=(cnt == nops),
                                )
                        for j, (aa, bb) in enumerate(batch):
                            tev = vop(
                                lambda v, j=j, aa=aa, bb=bb: v.tensor_copy(
                                    g_fm[:, bb * 1024 + aa * 128:bb * 1024 + (aa + 1) * 128],
                                    pt_ps[:, (2 * j) * 128:(2 * j + 1) * 128]),
                                waits=[(sT, tk)], inc=True,
                            )
                            if aa != bb:
                                tev = vop(
                                    lambda v, j=j, aa=aa, bb=bb: v.tensor_copy(
                                        g_fm[:, aa * 1024 + bb * 128:aa * 1024 + (bb + 1) * 128],
                                        pt_ps[:, (2 * j + 1) * 128:(2 * j + 2) * 128]),
                                    inc=True,
                                )
                        batch = []
                t_gtrans[e] = tev

            sched_gload(0, list(t_scatall))
            sched_gtrans(0)

            for e in range(E):
                if e > 0:
                    for c in range(2):
                        wv = []
                        if c == 0:
                            wv.append((sT, t_mm1_done[e - 1]))
                        t_w1l[(e, c)] = dma(
                            "scalar", w1img[:, (c % 2) * 2048:(c % 2 + 1) * 2048],
                            w1b_d[e, c, :, :], gw[c % 2], waits=wv,
                        )
                for c in range(32):
                    w1sl = w1img[:, (c % 2) * 2048:(c % 2 + 1) * 2048]
                    t_lastmm_c = 0
                    par = c % 2
                    pa = squad[:, par * 1024:par * 1024 + 512]
                    pg = squad[:, par * 1024 + 512:par * 1024 + 1024]
                    ssl = silu_f if par == 0 else x2t[:, 0:512]
                    for sh in range(2):
                        w = [(gw[c % 2], t_w1l[(e, c)]), (sV, t_gtrans[e])]
                        if t_sil2[par]:
                            w.append((sV, t_sil2[par]))
                        if c == 1 and t_yev_last:
                            w.append((sV, t_yev_last))
                        for fc in range(8):
                            mm(pa, w1sl[:, fc * 256:fc * 256 + 128],
                               g_fm_v[:, fc, sh * 512:(sh + 1) * 512],
                               fc == 0, fc == 7, waits=w if fc == 0 else ())
                        for fc in range(8):
                            tk = mm(pg,
                                    w1sl[:, fc * 256 + 128:fc * 256 + 256],
                                    g_fm_v[:, fc, sh * 512:(sh + 1) * 512],
                                    fc == 0, fc == 7, inc=(fc == 7))
                        t_lastmm_c = tk
                        wsl = [(sT, tk)]
                        if t_sil2[par]:
                            wsl.append((sV, t_sil2[par]))
                        t_sil = sop(
                            lambda s, pa=pa, ssl=ssl: s.activation(
                                ssl[:, :], pa, AF.Silu),
                            waits=wsl, inc=True,
                        )
                        t_silu_done = vop(
                            lambda v, c=c, sh=sh, ssl=ssl, pg=pg: v.tensor_mul(
                                t_b[:, c * 1024 + sh * 512:c * 1024 + (sh + 1) * 512],
                                ssl[:, :], pg),
                            waits=[(sS, t_sil)], inc=True,
                        )
                        t_sil2[par] = t_silu_done
                    if c + 2 < 32:
                        t_w1l[(e, c + 2)] = dma(
                            "scalar", w1img[:, (c % 2) * 2048:(c % 2 + 1) * 2048],
                            w1b_d[e, c + 2, :, :], gw[c % 2],
                            waits=[(sT, t_lastmm_c)],
                        )
                    if e > 0 and 2 <= c < 10:
                        j = c - 2
                        t_w2l[e] = dma(
                            "sync", w2_b[:, j * 4096:(j + 1) * 4096],
                            w2b_d[e, :, j * 4096:(j + 1) * 4096], gw2s,
                            waits=[(sT, t_mm2_done[e - 1])] if j == 0 else (),
                        )
                t_mm1_done[e] = sT.n
                if e + 1 < E:
                    sched_gload(e + 1, [(sT, t_mm1_done[e])])
                t_yev = {}
                for sm in range(8):
                    if sm == 2 and e + 1 < E:
                        sched_gtrans(e + 1)
                    for dh in range(2):
                        k2_ = 2 * sm + dh
                        bank = 2 + k2_ % 2
                        pyp = squad[:, bank * 512:(bank + 1) * 512]
                        w = [(sV, t_silu_done), (gw2s, t_w2l[e])]
                        if k2_ >= 2:
                            w.append((sV, t_yev[k2_ - 2]))
                        elif t_yev_last:
                            w.append((sV, t_yev_last))
                        for c in range(32):
                            tk = mm(
                                pyp,
                                t_b[:, c * 1024 + sm * 128:c * 1024 + (sm + 1) * 128],
                                w2_b[:, c * 1024 + dh * 512:c * 1024 + (dh + 1) * 512],
                                c == 0, c == 31, waits=w if c == 0 else (),
                                inc=(c == 31),
                            )
                        ybsl = ybuf if sm % 2 == 0 else yg
                        wv = [(sT, tk)]
                        if sm > 1:
                            wv.append((syY[sm % 2], t_ydma[(e, sm - 2)]))
                        elif e > 0:
                            wv.append((syY[sm % 2], t_ydma[(e - 1, 6 + sm)]))
                        t_yev[k2_] = vop(
                            lambda v, dh=dh, pyp=pyp, ybsl=ybsl: v.tensor_copy(
                                ybsl[:, dh * 512:(dh + 1) * 512], pyp),
                            waits=wv, inc=True,
                        )
                    t_ydma[(e, sm)] = dma(
                        "sync", y_d[e * CAP + sm * 128:e * CAP + (sm + 1) * 128, :],
                        ybsl[:, :], syY[sm % 2], waits=[(sV, t_yev[2 * sm + 1])],
                    )
                t_yev_last = t_yev[15]
                t_mm2_done[e] = sT.n
            t_ylast = t_ydma[(E - 1, 7)]

            t_o_v = [0] * NT
            t_sc_v = [0] * NT
            t_outw = [0] * NT
            t_x2l_v = [0] * NT
            ygs = [yg, yg2]
            x2s = [x2t, x2b]
            outs_ = [outt, outt2]
            for tt in range(NT):
                sl = tt % 2
                w = [(syY[0], syY[0].n), (syY[1], syY[1].n), (sV, t_desti)]
                if tt >= 2:
                    w.append((sS, t_sc_v[tt - 2]))

                def mkg(eng, tt=tt, sl=sl):
                    return eng.indirect_dma_start(
                        out=ygs[sl][:, :], out_offset=None,
                        in_=y_d[0:128, :],
                        in_offset=IndirectOffsetOnAxis(dest_i[:, tt:tt + 1], 0),
                    )
                OP("gpsimd", mkg, waits=w, incs=[(gy[sl], 16)])
                t_yg = gy[sl].n
                w = [(sc_[0], sc_[0].n), (sc_[1], sc_[1].n)]
                if tt >= 2:
                    w.append((sV, t_o_v[tt - 2]))
                t_x2l_v[tt] = dma("sync", x2s[sl][:, :],
                                  x2_d[tt * 128:(tt + 1) * 128, :],
                                  sx[sl], waits=w)
                w = [(gy[sl], t_yg)]
                if tt >= 2:
                    w.append((syOut, t_outw[tt - 2]))
                t_sc_v[tt] = sop(
                    lambda s, tt=tt, sl=sl: s.activation(
                        outs_[sl][:, :], ygs[sl][:, :], AF.Copy,
                        scale=maxprob[:, tt:tt + 1]),
                    waits=w, inc=True,
                )
                t_o_v[tt] = vop(
                    lambda v, sl=sl: v.tensor_add(
                        outs_[sl][:, :], outs_[sl][:, :], x2s[sl][:, :]),
                    waits=[(sS, t_sc_v[tt]), (sx[sl], t_x2l_v[tt])], inc=True,
                )
                t_outw[tt] = dma("sync", out_d[tt * 128:(tt + 1) * 128, :],
                                 outs_[sl][:, :], syOut,
                                 waits=[(sV, t_o_v[tt])])

        if "d_oh" in dbg:
            dma("sync", dbg["d_oh"][:, :], oh_all[:, :], syMisc,
                waits=[(sV, sV.n)])
            dma("sync", dbg["d_stats"][:, :], stats[:, :], syMisc)
            dma("sync", dbg["d_eidx"][:, :], eidx[:, :], syMisc)
        if "d_logits" in dbg:
            dma("sync", dbg["d_logits"][:, :], logits[:, :], syMisc,
                waits=[(sV, sV.n)])
            dma("sync", dbg["d_rbrep"][:, :], rbrep[:, :], syMisc)
        if "d_x2acc" in dbg:
            dma("sync", dbg["d_x2acc"][:, :], x2acc_d[:, :], syMisc,
                waits=list(t_projall))
        if "d_x2" in dbg:
            dma("sync", dbg["d_x2"][:, :], x2_d[:, :], syMisc,
                waits=[(sc_[0], sc_[0].n), (sc_[1], sc_[1].n)])
        if "d_xn2" in dbg:
            dma("sync", dbg["d_xn2"][:, :], xn2_d[:, :], syMisc,
                waits=[(sw[0], sw[0].n), (sw[1], sw[1].n)])
        if "d_g" in dbg:
            dma("sync", dbg["d_g"][:, :], g_d[:, :], syMisc,
                waits=[(gs[0], gs[0].n), (gs[1], gs[1].n)])
        if "d_y" in dbg:
            dma("sync", dbg["d_y"][:, :], y_d[:, :], syMisc,
                waits=[(syY[0], syY[0].n), (syY[1], syY[1].n)])
        if "d_router" in dbg:
            dma("sync", dbg["d_router"][:, 0:32], eidx[:, :], syMisc,
                waits=[(sV, t_desti)])
            dma("sync", dbg["d_router"][:, 32:64], maxprob[:, :], syMisc)
            dma("sync", dbg["d_router"][:, 64:96], dest_f[:, :], syMisc)

        allsems = [sQ, syMisc, syTab, syB, syG, syY[0], syY[1], syOut, gi, gb, gc, gw2s,
                   sx[0], sx[1], sa[0], sa[1], sn[0], sn[1], sw[0], sw[1],
                   sc_[0], sc_[1], gq[0], gq[1], gp[0], gp[1],
                   gw[0], gw[1], ga[0], ga[1], gs[0], gs[1], gy[0], gy[1]]
        OP("sync", lambda e: e.nop(),
           waits=[(c, c.n) for c in allsems])

        block = es.enter_context(nc.Block())

        @block.sync
        def _(eng):
            sch.replay("sync", eng)

        @block.gpsimd
        def _(eng):
            sch.replay("gpsimd", eng)

        @block.tensor
        def _(eng):
            sch.replay("tensor", eng)

        @block.vector
        def _(eng):
            sch.replay("vector", eng)

        @block.scalar
        def _(eng):
            sch.replay("scalar", eng)

    return nc


def make_inputs(L0_attn_x, L0_attn_Wqkv, L0_attn_Wproj, L0_attn_cascade_w,
                L0_ffn_router_w, L0_ffn_router_b, L0_ffn_w1, L0_ffn_w2):
    x = np.ascontiguousarray(np.asarray(L0_attn_x, dtype=np.float32))
    wqkv_r, wproj_r, rw_r, w1_r, w2_r = prep_shared(
        np.asarray(L0_attn_Wqkv, np.float32),
        np.asarray(L0_attn_Wproj, np.float32),
        np.asarray(L0_ffn_router_w, np.float32),
        np.asarray(L0_ffn_w1, np.float32),
        np.asarray(L0_ffn_w2, np.float32),
    )
    tabs = prep_tables()
    ident, bigmask, ltri, erow = prep_consts()
    rb = np.ascontiguousarray(np.asarray(L0_ffn_router_b, np.float32).reshape(1, E))
    cw = np.ascontiguousarray(np.asarray(L0_attn_cascade_w, np.float32).reshape(1, H - 1))
    shared = dict(
        wqkv=wqkv_r, wproj=wproj_r, rw=rw_r, rb=rb, cw=cw,
        w1r=w1_r, w2r=w2_r, tabs=tabs, identc=ident, bigmask=bigmask,
        ltri=ltri, erow=erow,
    )
    return x, shared


def kernel(L0_attn_x, L0_attn_Wqkv, L0_attn_Wproj, L0_attn_cascade_w,
           L0_ffn_router_w, L0_ffn_router_b, L0_ffn_w1, L0_ffn_w2):
    x, shared = make_inputs(
        L0_attn_x, L0_attn_Wqkv, L0_attn_Wproj, L0_attn_cascade_w,
        L0_ffn_router_w, L0_ffn_router_b, L0_ffn_w1, L0_ffn_w2)
    nc = build_program()
    in_maps = [dict(shared, x=np.ascontiguousarray(x[c])) for c in range(8)]
    res = run_bass_kernel_spmd(nc, in_maps, core_ids=list(range(8)))
    out = np.stack([res.results[c]["out"] for c in range(8)], axis=0)
    return out.astype(np.float32)


# version 13
# speedup vs baseline: 1.1124x; 1.0964x over previous
import numpy as np
import ml_dtypes
import concourse.bass as bass
import concourse.mybir as mybir
from concourse.bass_utils import run_bass_kernel_spmd
from concourse.bass_core_rust import IndirectOffsetOnAxis

F32 = mybir.dt.float32
BF16 = mybir.dt.bfloat16
I32 = mybir.dt.int32
U8 = mybir.dt.uint8
AF = mybir.ActivationFunctionType
ALU = mybir.AluOpType
AX = mybir.AxisListType.X

T = 4096
D = 1024
H = 16
HD = 64
NT = T // 128
E = 8
FFN = 4096
CAP = 1024
SLOTS = E * CAP
M0 = 25.0
ROPE_BASE = 10000.0
KB = 1024


class Ctr:
    __slots__ = ("h", "n")

    def __init__(self, h):
        self.h = h
        self.n = 0


class Sched:
    def __init__(self):
        self.recs = []

    def op(self, eng, fn, waits=(), incs=()):
        w = [(c.h, v) for (c, v) in waits if v > 0]
        i = []
        for c, k in incs:
            c.n += k
            i.append((c.h, k))
        self.recs.append((eng, fn, w, i))

    def replay(self, name, eng):
        known = {}
        for e, fn, waits, incs in self.recs:
            if e != name:
                continue
            for sh, v in waits:
                if known.get(id(sh), -1) < v:
                    eng.wait_ge(sh, v)
                    known[id(sh)] = v
            ins = fn(eng)
            for sh, k in incs:
                ins.then_inc(sh, k)


def _perm64():
    return np.concatenate([np.arange(0, 64, 2), np.arange(1, 64, 2)])


def prep_shared(Wqkv, Wproj, router_w, w1, w2):
    p64 = _perm64()
    cols = np.empty((H, 192), dtype=np.int64)
    for h in range(H):
        cols[h] = np.concatenate(
            [64 * h + p64, 1024 + 64 * h + p64, 2048 + 64 * h + np.arange(64)]
        )
    wq = Wqkv.reshape(8, 128, 3072)
    wqkv_r = np.ascontiguousarray(
        wq[:, :, cols].transpose(2, 1, 0, 3).reshape(H, 128, 8 * 192)
    )
    wproj_r = np.ascontiguousarray(
        Wproj.reshape(8, 128, 1024).transpose(1, 0, 2).reshape(128, 8192)
    )
    rw_r = np.ascontiguousarray(
        router_w.reshape(8, 128, 8).transpose(1, 0, 2).reshape(128, 64)
    )
    w1v = w1.reshape(E, 8, 128, 2 * FFN)
    idx = np.empty((32, 256), dtype=np.int64)
    for c in range(32):
        idx[c, :128] = c * 128 + np.arange(128)
        idx[c, 128:] = FFN + c * 128 + np.arange(128)
    w1_r = np.ascontiguousarray(
        w1v[:, :, :, idx].transpose(0, 3, 2, 1, 4).reshape(E, 32, 128, 8 * 256)
    )
    w2_r = np.ascontiguousarray(
        w2.reshape(E, 32, 128, 1024).transpose(0, 2, 1, 3).reshape(E, 128, 32 * 1024)
    )
    return wqkv_r, wproj_r, rw_r, w1_r, w2_r


def prep_tables():
    t = np.arange(T, dtype=np.float32)
    inv_freq = (1.0 / ROPE_BASE ** (np.arange(0, HD, 2, dtype=np.float32) / HD)).astype(
        np.float32
    )
    tabs = np.empty((H, 128, 4096), dtype=np.float32)

    def blk(m):
        return m.reshape(NT, 128, 32).transpose(1, 0, 2).reshape(128, NT * 32)

    for h in range(H):
        fs = np.float32(ROPE_BASE ** (h / H))
        freqs = np.outer(t, (inv_freq * fs).astype(np.float32)).astype(np.float32)
        cos = np.cos(freqs).astype(np.float32)
        sin = np.sin(freqs).astype(np.float32)
        tabs[h, :, 0:1024] = blk(cos * np.float32(0.125))
        tabs[h, :, 1024:2048] = blk(sin * np.float32(0.125))
        tabs[h, :, 2048:3072] = blk(cos)
        tabs[h, :, 3072:4096] = blk(sin)
    return tabs


def prep_consts():
    ident = np.eye(128, dtype=np.float32)
    i = np.arange(128)[:, None]
    c = np.arange(896)[None, :]
    bigmask = (i <= c - 384).astype(np.float32)
    k = np.arange(128)[:, None]
    m = np.arange(128)[None, :]
    ltri = (k < m).astype(np.float32)
    erow = np.tile(np.arange(E, dtype=np.float32), (128, 1))
    return ident, bigmask, ltri, erow


def build_program(debug=(), stages=21, dbghead=0):
    nc = bass.Bass("TRN2", target_bir_lowering=False)

    x_d = nc.dram_tensor("x", [T, D], F32, kind="ExternalInput").ap()
    wqkv_d = nc.dram_tensor("wqkv", [H, 128, 1536], F32, kind="ExternalInput").ap()
    wproj_d = nc.dram_tensor("wproj", [128, 8192], F32, kind="ExternalInput").ap()
    rw_d = nc.dram_tensor("rw", [128, 64], F32, kind="ExternalInput").ap()
    rb_d = nc.dram_tensor("rb", [1, E], F32, kind="ExternalInput").ap()
    cw_d = nc.dram_tensor("cw", [1, H - 1], F32, kind="ExternalInput").ap()
    w1_d = nc.dram_tensor("w1r", [E, 32, 128, 2048], F32, kind="ExternalInput").ap()
    w2_d = nc.dram_tensor("w2r", [E, 128, 32 * 1024], F32, kind="ExternalInput").ap()
    tab_d = nc.dram_tensor("tabs", [H, 128, 4096], F32, kind="ExternalInput").ap()
    ident_d = nc.dram_tensor("identc", [128, 128], F32, kind="ExternalInput").ap()
    mask_d = nc.dram_tensor("bigmask", [128, 896], F32, kind="ExternalInput").ap()
    ltri_d = nc.dram_tensor("ltri", [128, 128], F32, kind="ExternalInput").ap()
    erow_d = nc.dram_tensor("erow", [128, E], F32, kind="ExternalInput").ap()
    out_d = nc.dram_tensor("out", [T, D], F32, kind="ExternalOutput").ap()

    x2acc_d = nc.dram_tensor("x2acc", [T, D], F32, kind="Internal").ap()
    x2_d = nc.dram_tensor("x2s", [T, D], F32, kind="Internal").ap()
    xn2_d = nc.dram_tensor("xn2s", [T, D], BF16, kind="Internal").ap()
    g_d = nc.dram_tensor("gbuf", [SLOTS, D], BF16, kind="Internal").ap()
    y_d = nc.dram_tensor("ybufd", [SLOTS, D], F32, kind="Internal").ap()
    w1b_d = nc.dram_tensor("w1b", [E, 32, 128, 2048], BF16, kind="Internal").ap()
    w2b_d = nc.dram_tensor("w2b", [E, 128, 32 * 1024], BF16, kind="Internal").ap()
    ttot_dram = nc.dram_tensor("ttotd", [1, 256], F32, kind="Internal").ap()
    tpre_dram = nc.dram_tensor("tpred", [1, 256], F32, kind="Internal").ap()
    brow_dram = nc.dram_tensor("browd", [1, 4096], BF16, kind="Internal").ap()

    dbg = {}
    for name, shape, dt in debug:
        dbg[name] = nc.dram_tensor(name, shape, dt, kind="ExternalOutput").ap()

    from contextlib import ExitStack

    es = ExitStack()

    def sbuf(name, shape, dt):
        return es.enter_context(nc.sbuf_tensor(name, shape, dt))

    def psum(name, shape, dt):
        return es.enter_context(nc.psum_tensor(name, shape, dt))

    with es:
        ident_b = sbuf("ident_b", [128, 128], BF16)
        ident_f = sbuf("ident_f", [128, 128], F32)
        mask_b = sbuf("mask_b", [128, 896], BF16)
        ltri_b = sbuf("ltri_b", [128, 128], BF16)
        erow_f = sbuf("erow_f", [128, E], F32)
        rw_b = sbuf("rw_b", [128, 64], BF16)
        rb_row = sbuf("rb_row", [1, E], F32)
        rbrep = sbuf("rbrep", [128, E], F32)
        ones_f = sbuf("ones_f", [1, 128], F32)
        ones_bf = sbuf("ones_bf", [1, 128], BF16)
        ones128 = sbuf("ones128", [128, 1], BF16)
        cws = sbuf("cws", [1, 16], F32)
        cwsc = sbuf("cwsc", [1, 16], F32)
        csts = sbuf("csts", [128, 4], F32)
        csts2 = sbuf("csts2", [128, 1], F32)
        stats = sbuf("stats", [128, 24], F32)
        bias_acc = sbuf("bias_acc", [128, 32], F32)
        brow_f = sbuf("brow_f", [32, 128], F32)
        brow_b = sbuf("brow_b", [32, 128], BF16)
        wqkv_b = sbuf("wqkv_b", [128, 2 * 1536], BF16)
        wproj_b = sbuf("wproj_b", [128, 2 * 1024], BF16)
        v_ext = sbuf("v_ext", [128, 2 * NT * 65], BF16)
        invrep = sbuf("invrep", [128, 512], BF16)
        invr_b = sbuf("invr_b", [1, 512], BF16)
        trash = sbuf("trash", [128, 512], BF16)
        trash2 = sbuf("trash2", [128, 512], BF16)
        silu_f = sbuf("silu_f", [128, 512], F32)
        invr_f2 = silu_f
        maxprob = sbuf("maxprob", [128, 32], F32)
        eidx = sbuf("eidx", [128, 32], F32)
        oh_all = sbuf("oh_all", [128, 32 * E], BF16)
        intile = sbuf("intile", [128, 32 * E], F32)
        tprer = sbuf("tprer", [128, 32 * E], F32)
        ttot_row = sbuf("ttot_row", [1, 256], F32)
        ttot_f = sbuf("ttot_f", [32, E], F32)
        ttot_b = sbuf("ttot_b", [32, E], BF16)
        tpre_f = sbuf("tpre_f", [32, E], F32)
        tpre_row = sbuf("tpre_row", [1, 256], F32)
        dest_f = sbuf("dest_f", [128, 32], F32)
        dest_i = sbuf("dest_i", [128, 32], I32)
        xn2t = sbuf("xn2t", [128, 1024], BF16)
        xn2t2 = sbuf("xn2t2", [128, 1024], BF16)
        xn2fmt = sbuf("xn2fmt", [128, 8 * 128], BF16)
        logits = sbuf("logits", [128, 32 * E], F32)
        bparts = logits

        nc.const_aps.register(F32, -M0, csts[:, 0:1])
        nc.const_aps.register(F32, 1e-5, csts[:, 1:2])
        nc.const_aps.register(F32, -0.5, csts[:, 2:3])
        nc.const_aps.register(F32, 0.0, csts[:, 3:4])
        nc.const_aps.register(F32, -1.0, csts2[:, 0:1])

        AR = 164 * KB
        arena = es.enter_context(nc.sbuf_tensor("arena", [128, AR], U8))
        ar_base = nc.lookup_mloc(arena).addr

        def at(name, off, shape, dt):
            return nc.alloc_sbuf_tensor_at(name, shape, dt, offset=ar_base + off)

        xn_fm = at("xn_fm", 0, [128, 8 * 4096], BF16)
        costab0 = at("costab", 64 * KB, [128, 4096], F32)
        qk_tm = at("qk_tm", 80 * KB, [128, NT * 128], BF16)
        qrot = at("qrot", 88 * KB, [128, NT * 64], BF16)
        krot = at("krot", 92 * KB, [128, NT * 64], BF16)
        qfm = at("qfm", 96 * KB, [128, 4096], BF16)
        kfm = at("kfm", 104 * KB, [128, 4096], BF16)
        pT = at("pT", 112 * KB, [128, 32 * 512], BF16)
        outcat2 = at("outcat2", 144 * KB, [128, 4096], BF16)
        projpart = at("projpart", 112 * KB, [128, 2 * 2048], F32)
        ln_scr = at("ln_scr", 128 * KB, [128, 1024], F32)
        w2_b = at("w2_b", 0, [128, 32 * 1024], BF16)
        t_b = at("t_b", 64 * KB, [128, 32 * 1024], BF16)
        g_fm = at("g_fm", 128 * KB, [128, 8 * 1024], BF16)
        w1img = at("w1img", 144 * KB, [128, 2 * 2048], BF16)
        ybuf = at("ybuf", 152 * KB, [128, 1024], F32)
        yg = at("yg", 156 * KB, [128, 1024], F32)
        x2t = at("x2t", 160 * KB, [128, 1024], F32)
        outt = at("outt", 152 * KB, [128, 1024], F32)
        x2b = at("x2b", 136 * KB, [128, 1024], F32)
        outt2 = at("outt2", 128 * KB, [128, 1024], F32)
        yg2 = at("yg2", 132 * KB, [128, 1024], F32)
        zrow = trash

        pt_ps = psum("pt_ps", [128, 1024], BF16)
        squad = psum("squad", [128, 2048], F32)
        pf5 = psum("pf5", [128, 512], F32)
        pf6 = psum("pf6", [128, 512], F32)
        pf7 = psum("pf7", [128, 512], F32)

        def newsem(name):
            return Ctr(es.enter_context(nc.semaphore(name)))

        sT = newsem("sT")
        sV = newsem("sV")
        sS = newsem("sS")
        sQ = newsem("sQ")
        syMisc = newsem("syMisc")
        syTab = newsem("syTab")
        syB = newsem("syB")
        syG = newsem("syG")
        syY = [newsem("syY0"), newsem("syY1")]
        syOut = newsem("syOut")
        sx = [newsem("sx0"), newsem("sx1")]
        sa = [newsem("sa0"), newsem("sa1")]
        sn = [newsem("sn0"), newsem("sn1")]
        sw = [newsem("sw0"), newsem("sw1")]
        sc_ = [newsem("sc0"), newsem("sc1")]
        gi = newsem("gi")
        gq = [newsem("gq0"), newsem("gq1")]
        gp = [newsem("gp0"), newsem("gp1")]
        gb = newsem("gb")
        gc = newsem("gc")
        sP = newsem("sP")
        gw = [newsem("gw0"), newsem("gw1")]
        gw2s = newsem("gw2s")
        ga = [newsem("ga0"), newsem("ga1")]
        gs = [newsem("gs0"), newsem("gs1")]
        gy = [newsem("gy0"), newsem("gy1")]

        sch = Sched()
        OP = sch.op

        def dma(queue, out_ap, in_ap, ctr, waits=(), accum=False):
            if accum:
                fn = lambda eng, o=out_ap, i=in_ap: eng.dma_start(
                    out=o, in_=i, accum_op=ALU.add
                )
            else:
                fn = lambda eng, o=out_ap, i=in_ap: eng.dma_start(out=o, in_=i)
            OP(queue, fn, waits=waits, incs=[(ctr, 16)])
            return ctr.n

        def mm(out_ap, lhsT, rhs, start, stop, waits=(), inc=False):
            fn = lambda eng, o=out_ap, l=lhsT, r=rhs, a=start, b=stop: eng.matmul(
                o, l, r, start=a, stop=b
            )
            OP("tensor", fn, waits=waits, incs=[(sT, 1)] if inc else ())
            return sT.n

        def tp(out_ap, in_ap, ident, waits=(), inc=False):
            fn = lambda eng, o=out_ap, i=in_ap, d=ident: eng.transpose(o, i, d)
            OP("tensor", fn, waits=waits, incs=[(sT, 1)] if inc else ())
            return sT.n

        def vop(fn, waits=(), inc=False):
            OP("vector", fn, waits=waits, incs=[(sV, 1)] if inc else ())
            return sV.n

        def sop(fn, waits=(), inc=False):
            OP("scalar", fn, waits=waits, incs=[(sS, 1)] if inc else ())
            return sS.n

        dma("sync", ident_f[:, :], ident_d[:, :], syMisc)
        dma("sync", erow_f[:, :], erow_d[:, :], syMisc)
        dma("sync", rb_row[0:1, :], rb_d[0:1, :], syMisc)
        t_misc4 = dma("sync", cws[0:1, 0:15], cw_d[0:1, :], syMisc)
        dma("gpsimd", ident_b[:, :], ident_d[:, :], gi)
        dma("gpsimd", mask_b[:, :], mask_d[:, :], gi)
        dma("gpsimd", ltri_b[:, :], ltri_d[:, :], gi)
        t_gi = dma("gpsimd", rw_b[:, :], rw_d[:, :], gi)

        vop(lambda v: v.memset(csts[:, 0:1], -M0))
        vop(lambda v: v.memset(csts[:, 1:2], 1e-5))
        vop(lambda v: v.memset(csts[:, 2:3], -0.5))
        vop(lambda v: v.memset(csts[:, 3:4], 0.0))
        vop(lambda v: v.memset(csts2[:, 0:1], -1.0))
        vop(lambda v: v.memset(ones_f[:, :], 1.0))
        vop(lambda v: v.memset(ones_bf[:, :], 1.0))
        vop(lambda v: v.memset(ones128[:, :], 1.0))
        vop(lambda v: v.memset(bias_acc[:, :], 0.0))
        vop(lambda v: v.memset(logits[:, :], 0.0))
        vop(lambda v: v.memset(qfm[64:65, :], 0.0))
        vop(lambda v: v.memset(kfm[64:65, :], 0.0))
        vext_v = v_ext[:].rearrange("p (s t c) -> p s t c", c=65, t=NT)
        vop(lambda v: v.memset(vext_v[:, :, :, 64:65], 1.0))
        t_zrow = vop(lambda v: v.memset(zrow[:, :], 0.0), inc=True)
        t_cwsc = vop(
            lambda v: v.tensor_scalar_mul(cwsc[0:1, 0:15], cws[0:1, 0:15], 1.0 / T),
            waits=[(syMisc, t_misc4)],
            inc=True,
        )
        t_initv = t_cwsc

        t_mm_rb = mm(
            pf7[0:128, 0:E], ones_f[0:1, :], rb_row[0:1, :], True, True,
            waits=[(syMisc, t_misc4), (sV, t_initv)], inc=True,
        )
        t_rbrep = vop(
            lambda v: v.tensor_copy(rbrep[:, :], pf7[0:128, 0:E]),
            waits=[(sT, t_mm_rb)], inc=True,
        )

        g_flat = g_d.rearrange("r (a f) -> (r a) f", a=2)
        for j in range(128):
            t_gzero = dma(
                "scalar", g_flat[j * 128:(j + 1) * 128, :], zrow[:, :], sQ,
                waits=[(sV, t_zrow)] if j == 0 else (),
            )

        t_wqkv = [0] * (H + 1)
        t_tab = [0] * (H + 1)
        t_wproj = [0] * 9
        t_wqkv[0] = dma("gpsimd", wqkv_b[:, 0:1536], wqkv_d[0, :, :], gq[0])
        t_tab[0] = dma("sync", costab0[:, :], tab_d[0, :, :], syTab)
        t_wproj[0] = dma("gpsimd", wproj_b[:, 0:1024], wproj_d[:, 0:1024], gp[0])

        xn_fm_v = xn_fm[:].rearrange("p (c t) -> p c t", t=4096)

        def ln_stats(xs, c0, t_load_tok, load_ctr):
            t_sum = vop(lambda v: v.reduce_sum(stats[:, c0:c0 + 1], xs, axis=AX),
                        waits=[(load_ctr, t_load_tok)], inc=True)
            t_sq = sop(lambda s: s.activation(ln_scr[:, :], xs, AF.Square,
                                              accum_out=stats[:, c0 + 1:c0 + 2]),
                       waits=[(load_ctr, t_load_tok)], inc=True)
            t_msq = vop(lambda v: v.scalar_tensor_tensor(
                stats[:, c0 + 4:c0 + 5], stats[:, c0:c0 + 1], 1.0 / (D * D),
                stats[:, c0:c0 + 1], ALU.mult, ALU.mult),
                waits=[(sV, t_sum)], inc=True)
            vop(lambda v: v.tensor_scalar_mul(
                stats[:, c0 + 2:c0 + 3], stats[:, c0:c0 + 1], 1.0 / D))
            t_var = vop(lambda v: v.scalar_tensor_tensor(
                stats[:, c0 + 5:c0 + 6], stats[:, c0 + 1:c0 + 2], 1.0 / D,
                stats[:, c0 + 4:c0 + 5], ALU.mult, ALU.subtract),
                waits=[(sS, t_sq), (sV, t_msq)], inc=True)
            t_ln = sop(lambda s: s.activation(
                stats[:, c0 + 6:c0 + 7], stats[:, c0 + 5:c0 + 6], AF.Ln,
                bias=1e-5), waits=[(sV, t_var)], inc=True)
            t_rstd = sop(lambda s: s.activation(
                stats[:, c0 + 7:c0 + 8], stats[:, c0 + 6:c0 + 7], AF.Exp,
                scale=-0.5), waits=[(sS, t_ln)], inc=True)
            t_negmr = vop(lambda v: v.scalar_tensor_tensor(
                stats[:, c0 + 8:c0 + 9], stats[:, c0 + 2:c0 + 3], -1.0,
                stats[:, c0 + 7:c0 + 8], ALU.mult, ALU.mult),
                waits=[(sS, t_rstd)], inc=True)
            return stats[:, c0 + 7:c0 + 8], stats[:, c0 + 8:c0 + 9], t_negmr, t_rstd

        t_xload = [0] * NT
        t_xuse = [0] * NT
        t_p0evac = [0] * NT
        t_p0tp = [0] * NT
        xslots0 = [x2t, x2b]
        lnrec = at("lnrec", 144 * KB, [128, NT * 4], F32)
        xnslots0 = [xn2t, xn2t2]
        for tt in range(NT):
            slot = tt % 2
            w = [(sS, t_xuse[tt - 2])] if tt >= 2 else ()
            t_xload[tt] = dma(
                "sync", xslots0[slot][:, :],
                x_d[tt * 128:(tt + 1) * 128, :], sx[slot], waits=w,
            )
            xs = xslots0[slot][:, :]
            c0 = slot * 10
            rstd, negmr, t_negmr, t_rstdtok = ln_stats(xs, c0, t_xload[tt], sx[slot])
            if "d_lnrec" in dbg:
                vop(lambda v, c0=c0, tt=tt: v.tensor_copy(
                    lnrec[:, tt * 4:tt * 4 + 4], stats[:, c0 + 2:c0 + 6]))
            xnsl0 = xnslots0[slot]
            w = [(sV, t_negmr), (sS, t_rstdtok)]
            if tt >= 2:
                w.append((sT, t_p0tp[tt - 2]))
            t_xn = sop(
                lambda s, xs=xs, rstd=rstd, negmr=negmr, xnsl0=xnsl0: s.activation(
                    xnsl0[:, :], xs, AF.Identity, scale=rstd, bias=negmr),
                waits=w, inc=True,
            )
            t_xuse[tt] = t_xn
            w = [(sS, t_xn), (gi, t_gi)]
            if tt >= 1:
                w.append((sV, t_p0evac[tt - 1]))
            for fc in range(8):
                tk = tp(
                    pt_ps[:, fc * 128:(fc + 1) * 128],
                    xnsl0[:, fc * 128:(fc + 1) * 128],
                    ident_b[:, :], waits=w if fc == 0 else (), inc=(fc == 7),
                )
            t_p0tp[tt] = tk
            t_p0evac[tt] = vop(
                lambda v, tt=tt: v.tensor_copy(
                    xn_fm_v[:, :, tt * 128:(tt + 1) * 128],
                    pt_ps[:].rearrange("p (c t) -> p c t", t=128),
                ),
                waits=[(sT, tk)], inc=True,
            )
        t_xnfm = t_p0evac[NT - 1]
        if "d_xnfm" in dbg:
            dma("sync", dbg["d_xnfm"][:, :], xn_fm[:, :], syMisc,
                waits=[(sV, t_xnfm)])
        if "d_lnrec" in dbg:
            dma("sync", dbg["d_lnrec"][:, :], lnrec[:, :], syMisc,
                waits=[(sV, t_xnfm)])

        HEADS = 0 if stages == 0 else min(stages, 16)
        DBGHEAD = dbghead
        NT2 = NT if stages >= 20 else 0
        P3PLUS = stages >= 21
        t_desti = 0
        t_ylast = 0
        t_scatall = []
        qk_v = qk_tm[:].rearrange("p (t c) -> p t c", c=128)
        qr_v = qrot[:].rearrange("p (t c) -> p t c", c=64)
        kr_v = krot[:].rearrange("p (t c) -> p t c", c=64)
        tmp_v = xn2t[:].rearrange("p (t c) -> p t c", c=32)

        t_exp_last = 0
        t_brow = [0] * (H + 1)
        t_qrow = [0] * (H + 1)
        t_rope = [0] * H
        t_qkvmm = [0] * H
        t_fmtp = [0] * H
        t_pv_lastnorm = 0
        t_pvlast = [0] * H
        t_subA_evac = 0
        t_lastprojdma = None
        t_projmm = [0] * 8
        t_projevac = [0] * 8
        t_projdma = [[0] * 16 for _ in range(8)]
        pt_lastread = [0] * 32
        t_exp_half = {0: 0, 1: 0}
        t_vext_last = 0

        def pl31(v):
            if isinstance(v, tuple):
                return (sV if v[0] == "V" else sP, v[1])
            return (sV, v)

        t_qkve_par = [0, 0]
        t_vext_done = [0] * (H + 1)
        t_fmdone = [0] * (H + 1)
        t_fmevac_roll = 0
        t_exp_bank = [0, 0]
        pair_ctr = [0]
        t_bev_last = [0]

        def emit_qkv_group(hh, tt):
            nonlocal t_fmevac_roll
            if hh == 0:
                bank = 2 + tt % 2
                pq = squad[:, bank * 512:bank * 512 + 192]
            else:
                pq = pf7[:, 0:192]
            wq = wqkv_b[:, (hh % 2) * 1536:(hh % 2 + 1) * 1536]
            sl = hh % 2
            w = [(gq[hh % 2], t_wqkv[hh]), (sV, t_xnfm)]
            if hh == 0:
                if t_qkve_par[tt % 2]:
                    w.append((sS, t_qkve_par[tt % 2]))
            else:
                for par in range(2):
                    if t_qkve_par[par]:
                        w.append((sS, t_qkve_par[par]))
                w.append((sV, t_rbrep))
                if t_bev_last[0]:
                    w.append((sV, t_bev_last[0]))
            if tt == 0 and hh >= 2:
                w.append((sT, t_pvlast[hh - 2]))
            for fc in range(8):
                tk = mm(pq, xn_fm_v[:, fc, tt * 128:(tt + 1) * 128],
                        wq[:, fc * 192:(fc + 1) * 192], fc == 0, fc == 7,
                        waits=w if fc == 0 else (), inc=(fc == 7))
            t_qkvmm[hh] = tk
            sop(lambda s, tt=tt, pq=pq: s.activation(
                qk_v[:, tt, :], pq[:, 0:128], AF.Copy), waits=[(sT, tk)])
            tok = sop(lambda s, tt=tt, pq=pq, sl=sl: s.activation(
                vext_v[:, sl, tt, 0:64], pq[:, 128:192], AF.Copy), inc=True)
            t_qkve_par[tt % 2] = tok
            if tt == NT - 1:
                t_vext_done[hh] = tok
            return tok

        def emit_rope(hh):
            x1q, x2q = qk_v[:, :, 0:32], qk_v[:, :, 32:64]
            x1k, x2k = qk_v[:, :, 64:96], qk_v[:, :, 96:128]
            ct = costab0[:].rearrange("p (b t i) -> p b t i", b=4, i=32)
            cq, sq_, ck, sk = ct[:, 0], ct[:, 1], ct[:, 2], ct[:, 3]
            q1, q2 = qr_v[:, :, 0:32], qr_v[:, :, 32:64]
            k1, k2 = kr_v[:, :, 0:32], kr_v[:, :, 32:64]
            w = [(syTab, t_tab[hh]), (sS, t_qkve_par[(NT - 1) % 2])]
            if hh >= 1:
                w.append((sT, t_fmtp[hh - 1]))
            vop(lambda v: v.tensor_mul(q1, x1q, cq), waits=w)
            vop(lambda v: v.tensor_mul(tmp_v, x2q, sq_))
            vop(lambda v: v.tensor_sub(q1, q1, tmp_v))
            vop(lambda v: v.tensor_mul(q2, x1q, sq_))
            vop(lambda v: v.tensor_mul(tmp_v, x2q, cq))
            vop(lambda v: v.tensor_add(q2, q2, tmp_v))
            vop(lambda v: v.tensor_mul(k1, x1k, ck))
            vop(lambda v: v.tensor_mul(tmp_v, x2k, sk))
            vop(lambda v: v.tensor_sub(k1, k1, tmp_v))
            vop(lambda v: v.tensor_mul(k2, x1k, sk))
            vop(lambda v: v.tensor_mul(tmp_v, x2k, ck))
            t_rope[hh] = vop(lambda v: v.tensor_add(k2, k2, tmp_v), inc=True)
            if hh + 1 < H:
                t_tab[hh + 1] = dma(
                    "sync", costab0[:, :], tab_d[hh + 1, :, :], syTab,
                    waits=[(sV, t_rope[hh])],
                )

        def emit_fmtrans(hh):
            nonlocal t_fmevac_roll
            t_fmevac = t_fmevac_roll
            for grp in range(8):
                srcbuf = qrot if grp < 4 else krot
                dstbuf = qfm if grp < 4 else kfm
                g4 = grp % 4
                w = [(sV, t_rope[hh])]
                if t_fmevac:
                    w.append((sV, t_fmevac))
                for j in range(8):
                    tile = g4 * 8 + j
                    tk = tp(
                        pt_ps[0:64, j * 128:(j + 1) * 128],
                        srcbuf[:, tile * 64:(tile + 1) * 64],
                        ident_b[:, :],
                        waits=w if j == 0 else (), inc=(j == 7),
                    )
                t_fmevac = vop(
                    lambda v, dstbuf=dstbuf, g4=g4: v.tensor_copy(
                        dstbuf[0:64, g4 * 1024:(g4 + 1) * 1024],
                        pt_ps[0:64, :]),
                    waits=[(sT, tk)], inc=True,
                )
            t_fmtp[hh] = tk
            t_fmevac_roll = t_fmevac
            t_fmdone[hh] = t_fmevac

        for h in range(HEADS):
            if h + 1 < H:
                wv = [(sT, t_qkvmm[h - 1])] if h >= 1 else ()
                t_wqkv[h + 1] = dma(
                    "gpsimd",
                    wqkv_b[:, ((h + 1) % 2) * 1536:((h + 1) % 2 + 1) * 1536],
                    wqkv_d[h + 1, :, :], gq[(h + 1) % 2], waits=wv,
                )
            if h % 2 == 0 and h // 2 + 1 < 8:
                p = h // 2 + 1
                wv = [(sT, t_projmm[p - 2])] if p >= 2 else ()
                t_wproj[p] = dma(
                    "gpsimd", wproj_b[:, (p % 2) * 1024:(p % 2 + 1) * 1024],
                    wproj_d[:, p * 1024:(p + 1) * 1024], gp[p % 2], waits=wv,
                )
            for piece in range(3):
                pidx = h * 3 + piece
                if pidx < 32:
                    e_, cg = divmod(pidx, 4)
                    dma("gpsimd",
                        w1b_d[e_, cg * 8:(cg + 1) * 8, :, :],
                        w1_d[e_, cg * 8:(cg + 1) * 8, :, :], gc)
                elif pidx < 48:
                    e_, jg = divmod(pidx - 32, 2)
                    dma("gpsimd",
                        w2b_d[e_, :, jg * 16384:(jg + 1) * 16384],
                        w2_d[e_, :, jg * 16384:(jg + 1) * 16384], gc)

            if h == 0:
                for tt in range(NT):
                    emit_qkv_group(0, tt)
                emit_rope(0)
                emit_fmtrans(0)
                t_exp_bank[1] = t_vext_done[0]

            if h == DBGHEAD and "d_qfm" in dbg:
                dma("sync", dbg["d_qfm"][:, :], qfm[:, :], syMisc,
                    waits=[(sV, t_fmdone[h])])
                dma("sync", dbg["d_kfm"][:, :], kfm[:, :], syMisc)
                dma("sync", dbg["d_vext"][:, :], v_ext[:, :], syMisc)
                dma("sync", dbg["d_qk"][:, :], qk_tm[:, :], syMisc)

            pend_tail = None
            qkv_tt = 0 if h + 1 < H else NT

            def emit_tail(qg, nsc, tk_pv, t_maskg_last):
                nonlocal t_pv_lastnorm
                t_lnr = sop(lambda s: s.activation(
                    invr_f2[0:1, :], pf5[64:65, :], AF.Ln),
                    waits=[(sT, tk_pv)], inc=True)
                def _iexp(s):
                    with nc.allow_low_precision(reason="1/r stored bf16 as matmul operand"):
                        return s.activation(invr_b[0:1, :], invr_f2[0:1, :],
                                            AF.Exp, scale=-1.0)
                t_invb = sop(_iexp, waits=[(sS, t_lnr)], inc=True)
                w = [(sS, t_invb)]
                if h % 2 == 0 and h >= 2:
                    w.append((sV, t_projevac[h // 2 - 1]))
                t_repmm = mm(
                    pf6[0:128, :], ones_bf[0:1, :], invr_b[0:1, :], True, True,
                    waits=w, inc=True,
                )
                t_rep = vop(
                    lambda v: v.tensor_copy(invrep[:, :], pf6[0:128, :]),
                    waits=[(sT, t_repmm)], inc=True,
                )
                r0 = 64 * (h % 2)
                t_pv_lastnorm = vop(
                    lambda v, r0=r0, qg=qg: v.tensor_mul(
                        outcat2[r0:r0 + 64, qg * 512:(qg + 1) * 512],
                        pf5[0:64, :], invrep[0:64, :]),
                    waits=[(sV, t_rep)], inc=True,
                )
                for sc in range(nsc):
                    pt_lastread[sc] = ("V", vop(
                        lambda v, sc=sc, qg=qg: v.scalar_tensor_tensor(
                            trash[:, :], pT[:, sc * 512:(sc + 1) * 512],
                            1.0, invrep[:, :], ALU.mult, ALU.mult,
                            accum_out=bparts[:, sc * 8 + qg:sc * 8 + qg + 1]),
                        waits=[(sV, t_rep), (sQ, t_gzero),
                               (sP, t_maskg_last)],
                        inc=True,
                    ))

            for qg in range(8):
                nsc = 4 * qg + 4
                qsl = qfm[0:65, qg * 512:(qg + 1) * 512]
                t_expg = {}
                for scg in range(0, nsc, 2):
                    if qkv_tt < NT:
                        emit_qkv_group(h + 1, qkv_tt)
                        qkv_tt += 1
                        if qkv_tt == NT:
                            emit_rope(h + 1)
                    bp = pair_ctr[0] & 1
                    pair_ctr[0] += 1
                    psl = squad[:, bp * 1024:(bp + 1) * 1024]
                    for j in range(2):
                        sc = scg + j
                        w = [(sV, t_fmdone[h]), (gb, t_brow[h]),
                             (sV, t_qrow[h]), (sS, t_exp_bank[bp])]
                        mm(
                            psl[:, j * 512:(j + 1) * 512],
                            kfm[0:65, sc * 128:(sc + 1) * 128],
                            qsl, True, True,
                            waits=w if j == 0 else (), inc=(j == 1),
                        )
                    tmk = sT.n
                    if scg == 0 and pend_tail is not None:
                        pend_tail()
                        pend_tail = None
                    w = [(sT, tmk)]
                    if t_lastprojdma is not None:
                        w.extend(t_lastprojdma)
                    for sc in (scg, scg + 1):
                        if pt_lastread[sc]:
                            eng_, tok_ = pt_lastread[sc]
                            w.append((sV if eng_ == "V" else sP, tok_))
                    t_expg[scg] = sop(
                        lambda s, scg=scg, psl=psl: s.activation(
                            pT[:, scg * 512:(scg + 2) * 512], psl, AF.Exp,
                            bias=-M0),
                        waits=w, inc=True,
                    )
                    t_exp_bank[bp] = t_expg[scg]
                t_exp_last = t_expg[nsc - 2]
                t_maskg = 0
                for kk in range(4):
                    sc = 4 * qg + kk
                    OP("gpsimd",
                       lambda g, sc=sc, kk=kk: g.tensor_mul(
                           pT[:, sc * 512:(sc + 1) * 512],
                           pT[:, sc * 512:(sc + 1) * 512],
                           mask_b[:, 384 - 128 * kk:896 - 128 * kk]),
                       waits=[(sS.h, t_expg[sc - (sc % 2)])] if False else
                       [(sS, t_expg[sc - (sc % 2)])],
                       incs=[(sP, 1)])
                    t_maskg = sP.n
                for sc in range(nsc):
                    w = []
                    if sc == 0:
                        w.append((sV, t_pv_lastnorm))
                        w.append((sS, t_vext_done[h]))
                        if h % 2 == 0 and h >= 2:
                            w.append((sV, t_projevac[h // 2 - 1]))
                    w.append((sS, t_expg[sc - (sc % 2)]))
                    if sc >= 4 * qg:
                        w.append((sP, t_maskg))
                    tk = mm(
                        pf5[0:65, :],
                        v_ext[:, (h % 2) * NT * 65 + sc * 65:
                              (h % 2) * NT * 65 + (sc + 1) * 65],
                        pT[:, sc * 512:(sc + 1) * 512],
                        sc == 0, sc == nsc - 1,
                        waits=w, inc=(sc == nsc - 1),
                    )
                t_pvlast[h] = tk
                pend_tail = (lambda qg=qg, nsc=nsc, tk=tk, mg=t_maskg:
                             emit_tail(qg, nsc, tk, mg))
            pend_tail()
            pend_tail = None
            if h + 1 < H:
                emit_fmtrans(h + 1)

            if h == DBGHEAD and "d_outcat" in dbg:
                t_tmpred = vop(
                    lambda v: v.tensor_reduce(
                        bias_acc[:].rearrange("p (s o) -> p s o", o=1),
                        bparts[:].rearrange("p (s q) -> p s q", q=8),
                        AX, ALU.add),
                    waits=[pl31(pt_lastread[31]), (sP, sP.n)], inc=True,
                )
                dma("sync", dbg["d_outcat"][:, :], outcat2[:, :], syMisc,
                    waits=[(sV, t_pv_lastnorm)])
                if "d_bacc" in dbg:
                    dma("sync", dbg["d_bacc"][:, :], bias_acc[:, :], syMisc,
                        waits=[(sV, t_tmpred)])
                if "d_pt" in dbg:
                    dma("sync", dbg["d_pt"][:, :], pT[:, :], syMisc)

            if h + 1 < H:
                t_bred = vop(
                    lambda v: v.tensor_reduce(
                        bias_acc[:].rearrange("p (s o) -> p s o", o=1),
                        bparts[:].rearrange("p (s q) -> p s q", q=8),
                        AX, ALU.add),
                    waits=[pl31(pt_lastread[31]), (sP, sP.n)], inc=True,
                )
                t_bz = vop(lambda v: v.memset(bparts[:, :], 0.0),
                           waits=[(sV, t_bred)], inc=True)
                tk = tp(
                    pf7[0:32, 0:128], bias_acc[:, :], ident_f[:, :],
                    waits=[(sV, t_bred), (sS, t_vext_done[h + 1])], inc=True,
                )
                t_bev = vop(
                    lambda v: v.tensor_copy(brow_b[:, :], pf7[0:32, 0:128]),
                    waits=[(sT, tk)], inc=True,
                )
                t_bev_last[0] = t_bev
                t_bd1 = dma(
                    "sync",
                    brow_dram[0:1, :].rearrange("a (s i) -> (a s) i", i=128),
                    brow_b[:, :], gb, waits=[(sV, t_bev)])
                t_brow[h + 1] = dma(
                    "sync", kfm[64:65, :], brow_dram[0:1, :], gb,
                    waits=[(gb, t_bd1)],
                )
                t_qrow[h + 1] = vop(
                    lambda v, h=h: v.tensor_scalar(
                        qfm[64:65, :], qfm[64:65, :], 0.0,
                        cwsc[0:1, h:h + 1], ALU.mult, ALU.add),
                    waits=[(sV, t_cwsc), (sT, t_pvlast[h])], inc=True,
                )

            if h % 2 == 1:
                p = h // 2
                t_ev_hist = []
                for r in range(16):
                    rslot = r % 2
                    pp = projpart[:, rslot * 2048:(rslot + 1) * 2048]
                    for idx4 in range(4):
                        ti, ofh = divmod(idx4, 2)
                        tt = r * 2 + ti
                        pb = pf5 if idx4 % 2 == 0 else pf6
                        w = [(sV, t_pv_lastnorm), (gp[p % 2], t_wproj[p]),
                             pl31(pt_lastread[31]), (sP, sP.n)]
                        if len(t_ev_hist) >= 2:
                            w.append((sV, t_ev_hist[-2]))
                        tk = mm(
                            pb[:, :],
                            outcat2[:, tt * 128:(tt + 1) * 128],
                            wproj_b[:, (p % 2) * 1024 + ofh * 512:
                                    (p % 2) * 1024 + (ofh + 1) * 512],
                            True, True, waits=w, inc=True,
                        )
                        wv = [(sT, tk)]
                        if r >= 2 and idx4 == 0:
                            wv.append((ga[r % 2], t_projdma[p][r - 2]))
                        t_ev = vop(
                            lambda v, pp=pp, ti=ti, ofh=ofh, pb=pb: v.tensor_copy(
                                pp[:, ti * 1024 + ofh * 512:
                                   ti * 1024 + (ofh + 1) * 512], pb[:, :]),
                            waits=wv, inc=True,
                        )
                        t_ev_hist.append(t_ev)
                    w = [(sV, t_ev)]
                    if p >= 1:
                        w.append((ga[r % 2], t_projdma[p - 1][r]))
                    t_projdma[p][r] = dma(
                        "gpsimd",
                        x2acc_d[r * 256:(r + 1) * 256, :].rearrange(
                            "(a p) f -> p a f", p=128),
                        pp[:, :].rearrange("p (a f) -> p a f", f=1024),
                        ga[r % 2], waits=w, accum=(p > 0),
                    )
                t_projmm[p] = tk
                t_projevac[p] = t_ev
                t_lastprojdma = [(ga[0], t_projdma[p][14]),
                                 (ga[1], t_projdma[p][15])]

        t_projall = ([(ga[0], ga[0].n), (ga[1], ga[1].n)]
                     if HEADS == H else [])

        t_w1l = {}
        t_w2l = [0] * E
        t_castall = [(gc, gc.n)]
        if NT2 and P3PLUS:
            for c in range(2):
                t_w1l[(0, c)] = dma(
                    "scalar", w1img[:, (c % 2) * 2048:(c % 2 + 1) * 2048],
                    w1b_d[0, c, :, :], gw[c % 2],
                    waits=[(sT, t_projmm[7])] + t_castall,
                )
            for j in range(8):
                t_w2l[0] = dma(
                    "sync", w2_b[:, j * 4096:(j + 1) * 4096],
                    w2b_d[0, :, j * 4096:(j + 1) * 4096], gw2s,
                    waits=([(sT, t_qkvmm[15])] + t_castall) if j == 0 else (),
                )
        t_x2w = [0] * NT
        t_xn2w = [0] * NT
        t_lgev = [0] * NT
        t_add_v = [0] * NT
        t_xnuse = [0] * NT
        t_p2evac = 0
        t_totmm = 0
        x2slots = [yg, yg2]
        xslots2 = [x2t, x2b]
        xnslots = [xn2t, xn2t2]
        for tt in range(NT2):
            slot = tt % 2
            xsl = xslots2[slot][:, :]
            asl = x2slots[slot]
            w = list(t_projall)
            if tt >= 2:
                w.append((sS, t_xuse[tt - 2]))
                w.append((sc_[slot], t_x2w[tt - 2]))
            t_xl = dma("sync", xsl, x_d[tt * 128:(tt + 1) * 128, :], sx[slot],
                       waits=w)
            w2_ = ([(sV, t_add_v[tt - 2])] if tt >= 2 else list(t_projall))
            t_al = dma("sync", asl[:, :], x2acc_d[tt * 128:(tt + 1) * 128, :],
                       sa[slot], waits=w2_)
            t_add = vop(
                lambda v, xsl=xsl, asl=asl: v.tensor_add(xsl, xsl, asl[:, :]),
                waits=[(sa[slot], t_al), (sx[slot], t_xl)], inc=True,
            )
            t_add_v[tt] = t_add
            t_x2w[tt] = dma("sync", x2_d[tt * 128:(tt + 1) * 128, :], xsl,
                            sc_[slot], waits=[(sV, t_add)])
            c0 = slot * 10
            rstd, negmr, t_negmr, t_rstdtok = ln_stats(xsl, c0, t_add, sV)
            xnsl = xnslots[slot]
            w = [(sV, t_negmr), (sS, t_rstdtok)]
            if tt >= 2:
                w.append((sT, t_xnuse[tt - 2]))
                w.append((sw[slot], t_xn2w[tt - 2]))
            t_xn = sop(
                lambda s, xsl=xsl, rstd=rstd, negmr=negmr, xnsl=xnsl: s.activation(
                    xnsl[:, :], xsl, AF.Identity, scale=rstd, bias=negmr),
                waits=w, inc=True,
            )
            t_xuse[tt] = t_xn
            t_xn2w[tt] = dma("sync", xn2_d[tt * 128:(tt + 1) * 128, :],
                             xnsl[:, :], sw[slot], waits=[(sS, t_xn)])
            w = [(sS, t_xn)]
            if tt == 0:
                w.append((sV, t_subA_evac))
            if t_p2evac:
                w.append((sV, t_p2evac))
            for fc in range(8):
                tk = tp(
                    pt_ps[:, fc * 128:(fc + 1) * 128],
                    xnsl[:, fc * 128:(fc + 1) * 128],
                    ident_b[:, :], waits=w if fc == 0 else (), inc=(fc == 7),
                )
            t_xnuse[tt] = tk
            t_p2evac = vop(
                lambda v: v.tensor_copy(
                    xn2fmt[:].rearrange("p (c t) -> p c t", t=128),
                    pt_ps[:].rearrange("p (c t) -> p c t", t=128)),
                waits=[(sT, tk)], inc=True,
            )
            rslot = tt % 2
            w = [(sV, t_p2evac), (gi, t_gi)]
            if tt == 0 and t_bev_last[0]:
                w.append((sV, t_bev_last[0]))
            if tt >= 1:
                w.append((sV, t_lgev[tt - 1]))
            for fc in range(8):
                tk = mm(
                    pf7[:, rslot * 8:rslot * 8 + 8],
                    xn2fmt[:, fc * 128:(fc + 1) * 128],
                    rw_b[:, fc * 8:(fc + 1) * 8],
                    fc == 0, fc == 7, waits=w if fc == 0 else (), inc=(fc == 7),
                )
            t_lgev[tt] = vop(
                lambda v, tt=tt, rslot=rslot: v.tensor_add(
                    logits[:, tt * 8:(tt + 1) * 8],
                    pf7[:, rslot * 8:rslot * 8 + 8], rbrep[:, :]),
                waits=[(sT, tk), (sV, t_rbrep)], inc=True,
            )
            cnm = 9 if slot == 0 else 19
            cse = 20 if slot == 0 else 21
            t_nm = vop(lambda v, tt=tt, cnm=cnm: v.reduce_max(
                stats[:, cnm:cnm + 1], logits[:, tt * 8:(tt + 1) * 8],
                axis=AX, negate=True), waits=[(sV, t_lgev[tt])], inc=True)
            t_se = sop(
                lambda s, tt=tt, cnm=cnm, cse=cse: s.activation(
                    trash[:, 0:8], logits[:, tt * 8:(tt + 1) * 8], AF.Exp,
                    bias=stats[:, cnm:cnm + 1],
                    accum_out=stats[:, cse:cse + 1]),
                waits=[(sV, t_nm)], inc=True,
            )
            vop(lambda v, tt=tt, cse=cse: v.reciprocal(
                maxprob[:, tt:tt + 1], stats[:, cse:cse + 1]),
                waits=[(sS, t_se)])
            t_oh = vop(
                lambda v, tt=tt, cnm=cnm: v.tensor_scalar(
                    oh_all[:, tt * 8:(tt + 1) * 8],
                    logits[:, tt * 8:(tt + 1) * 8],
                    stats[:, cnm:cnm + 1], 0.0, ALU.add, ALU.is_equal),
                waits=[(sV, t_nm)], inc=True,
            )
            vop(
                lambda v, tt=tt: v.scalar_tensor_tensor(
                    silu_f[:, 0:8], erow_f[:, :], 1.0,
                    oh_all[:, tt * 8:(tt + 1) * 8],
                    ALU.mult, ALU.mult, accum_out=eidx[:, tt:tt + 1]),
                waits=[(sV, t_oh)],
            )
            w = [(sV, t_oh), (gi, t_gi)]
            mm(pf6[:, tt * 8:(tt + 1) * 8], ltri_b[:, :],
               oh_all[:, tt * 8:(tt + 1) * 8], True, True, waits=w, inc=True)
            t_totmm = mm(
                pf5[0:1, tt * 8:(tt + 1) * 8], ones128[:, 0:1],
                oh_all[:, tt * 8:(tt + 1) * 8], True, True, inc=True)

        if P3PLUS:
            t_it = vop(lambda v: v.tensor_copy(intile[:, :], pf6[:, 0:256]),
                       waits=[(sT, t_totmm)], inc=True)
            t_tr = vop(lambda v: v.tensor_copy(ttot_row[0:1, :], pf5[0:1, 0:256]),
                       inc=True)
            t_td1 = dma("sync", ttot_dram[0:1, :], ttot_row[0:1, :], syB,
                        waits=[(sV, t_tr)])
            t_td2 = dma("sync", ttot_f[:, :],
                        ttot_dram[0:1, :].rearrange("a (s e) -> (a s) e", e=8),
                        syB, waits=[(syB, t_td1)])
            t_tb = vop(lambda v: v.tensor_copy(ttot_b[:, :], ttot_f[:, :]),
                       waits=[(syB, t_td2)], inc=True)
            t_tpmm = mm(pf7[0:32, 0:8], ltri_b[0:32, 0:32], ttot_b[:, :], True,
                        True, waits=[(sV, t_tb)], inc=True)
            t_tp_ = vop(lambda v: v.tensor_copy(tpre_f[:, :], pf7[0:32, 0:8]),
                        waits=[(sT, t_tpmm)], inc=True)
            t_tpd1 = dma("sync",
                         tpre_dram[0:1, :].rearrange("a (s e) -> (a s) e", e=8),
                         tpre_f[:, :], syB, waits=[(sV, t_tp_)])
            t_tpd2 = dma("sync", tpre_row[0:1, :], tpre_dram[0:1, :], syB,
                         waits=[(syB, t_tpd1)])
            for tt in range(NT):
                w = [(syB, t_tpd2), (sV, t_it)]
                t_rp = mm(
                    pf6[:, tt * 8:(tt + 1) * 8], ones_f[0:1, :],
                    tpre_row[0:1, tt * 8:(tt + 1) * 8], True, True,
                    waits=w if tt == 0 else (), inc=(tt == NT - 1),
                )
            t_rpe = vop(lambda v: v.tensor_copy(tprer[:, :], pf6[:, 0:256]),
                        waits=[(sT, t_rp)], inc=True)
            for tt in range(NT):
                vop(
                    lambda v, tt=tt: v.tensor_add(
                        intile[:, tt * 8:(tt + 1) * 8],
                        intile[:, tt * 8:(tt + 1) * 8],
                        tprer[:, tt * 8:(tt + 1) * 8]),
                    waits=[(sV, t_rpe)],
                )
            for tt in range(NT):
                vop(
                    lambda v, tt=tt: v.scalar_tensor_tensor(
                        logits[:, tt * 8:(tt + 1) * 8],
                        intile[:, tt * 8:(tt + 1) * 8], 1.0,
                        oh_all[:, tt * 8:(tt + 1) * 8],
                        ALU.mult, ALU.mult,
                        accum_out=tprer[:, tt * 8:tt * 8 + 1]),
                )
            for tt in range(NT):
                t_df = vop(
                    lambda v, tt=tt: v.tensor_scalar(
                        dest_f[:, tt:tt + 1], eidx[:, tt:tt + 1], float(CAP),
                        tprer[:, tt * 8:tt * 8 + 1], ALU.mult, ALU.add),
                    inc=(tt == NT - 1),
                )
            t_desti = vop(lambda v: v.tensor_copy(dest_i[:, :], dest_f[:, :]),
                          waits=[(sV, t_df)], inc=True)
            t_scat_v = [0] * NT
            t_xnl = [0] * NT
            for tt in range(NT):
                slot = tt % 2
                w = [(sw[0], sw[0].n), (sw[1], sw[1].n)]
                if tt >= 2:
                    w.append((gs[slot], t_scat_v[tt - 2]))
                t_xnl[tt] = dma("sync", xnslots[slot][:, :],
                                xn2_d[tt * 128:(tt + 1) * 128, :], sn[slot],
                                waits=w)

                def mk(eng, tt=tt, slot=slot):
                    return eng.indirect_dma_start(
                        out=g_d[0:128, :],
                        out_offset=IndirectOffsetOnAxis(dest_i[:, tt:tt + 1], 0),
                        in_=xnslots[slot][:, :],
                        in_offset=None,
                    )
                OP("gpsimd", mk,
                   waits=[(sn[slot], t_xnl[tt]), (sV, t_desti), (sQ, t_gzero)],
                   incs=[(gs[slot], 16)])
                t_scat_v[tt] = gs[slot].n
            t_scatall = [(gs[0], gs[0].n), (gs[1], gs[1].n)]

            g_fm_v = g_fm[:].rearrange("p (c s) -> p c s", s=1024)
            t_gload = [0] * E
            t_gtrans = [0] * E
            t_mm1_done = [0] * E
            t_mm2_done = [0] * E
            t_ydma = {}
            t_silu_done = 0
            t_sil2 = [0, 0]
            t_yev_last = 0

            def sched_gload(e, waits):
                t_gload[e] = dma(
                    "sync", g_fm[:].rearrange("p (s f) -> p s f", f=1024),
                    g_d[e * CAP:(e + 1) * CAP, :].rearrange("(s p) f -> p s f", p=128),
                    syG, waits=waits,
                )

            def sched_gtrans(e):
                pairs = [(a, b) for a in range(8) for b in range(a, 8)]
                tev = 0
                batch = []
                for pi, (a, b) in enumerate(pairs):
                    batch.append((a, b))
                    if len(batch) == 4 or pi == len(pairs) - 1:
                        w = [(syG, t_gload[e])]
                        if tev:
                            w.append((sV, tev))
                        nops = sum(1 if aa == bb else 2 for aa, bb in batch)
                        cnt = 0
                        for j, (aa, bb) in enumerate(batch):
                            cnt += 1
                            tk = tp(
                                pt_ps[:, (2 * j) * 128:(2 * j + 1) * 128],
                                g_fm[:, aa * 1024 + bb * 128:aa * 1024 + (bb + 1) * 128],
                                ident_b[:, :],
                                waits=w if cnt == 1 else (), inc=(cnt == nops),
                            )
                            if aa != bb:
                                cnt += 1
                                tk = tp(
                                    pt_ps[:, (2 * j + 1) * 128:(2 * j + 2) * 128],
                                    g_fm[:, bb * 1024 + aa * 128:bb * 1024 + (aa + 1) * 128],
                                    ident_b[:, :], inc=(cnt == nops),
                                )
                        for j, (aa, bb) in enumerate(batch):
                            tev = vop(
                                lambda v, j=j, aa=aa, bb=bb: v.tensor_copy(
                                    g_fm[:, bb * 1024 + aa * 128:bb * 1024 + (aa + 1) * 128],
                                    pt_ps[:, (2 * j) * 128:(2 * j + 1) * 128]),
                                waits=[(sT, tk)], inc=True,
                            )
                            if aa != bb:
                                tev = vop(
                                    lambda v, j=j, aa=aa, bb=bb: v.tensor_copy(
                                        g_fm[:, aa * 1024 + bb * 128:aa * 1024 + (bb + 1) * 128],
                                        pt_ps[:, (2 * j + 1) * 128:(2 * j + 2) * 128]),
                                    inc=True,
                                )
                        batch = []
                t_gtrans[e] = tev

            sched_gload(0, list(t_scatall))
            sched_gtrans(0)

            for e in range(E):
                if e > 0:
                    for c in range(2):
                        wv = []
                        if c == 0:
                            wv.append((sT, t_mm1_done[e - 1]))
                        t_w1l[(e, c)] = dma(
                            "scalar", w1img[:, (c % 2) * 2048:(c % 2 + 1) * 2048],
                            w1b_d[e, c, :, :], gw[c % 2], waits=wv,
                        )
                for c in range(32):
                    w1sl = w1img[:, (c % 2) * 2048:(c % 2 + 1) * 2048]
                    t_lastmm_c = 0
                    par = c % 2
                    pa = squad[:, par * 1024:par * 1024 + 512]
                    pg = squad[:, par * 1024 + 512:par * 1024 + 1024]
                    ssl = silu_f if par == 0 else x2t[:, 0:512]
                    for sh in range(2):
                        w = [(gw[c % 2], t_w1l[(e, c)]), (sV, t_gtrans[e])]
                        if t_sil2[par]:
                            w.append((sV, t_sil2[par]))
                        if c == 1 and t_yev_last:
                            w.append((sV, t_yev_last))
                        for fc in range(8):
                            mm(pa, w1sl[:, fc * 256:fc * 256 + 128],
                               g_fm_v[:, fc, sh * 512:(sh + 1) * 512],
                               fc == 0, fc == 7, waits=w if fc == 0 else ())
                        for fc in range(8):
                            tk = mm(pg,
                                    w1sl[:, fc * 256 + 128:fc * 256 + 256],
                                    g_fm_v[:, fc, sh * 512:(sh + 1) * 512],
                                    fc == 0, fc == 7, inc=(fc == 7))
                        t_lastmm_c = tk
                        wsl = [(sT, tk)]
                        if t_sil2[par]:
                            wsl.append((sV, t_sil2[par]))
                        t_sil = sop(
                            lambda s, pa=pa, ssl=ssl: s.activation(
                                ssl[:, :], pa, AF.Silu),
                            waits=wsl, inc=True,
                        )
                        t_silu_done = vop(
                            lambda v, c=c, sh=sh, ssl=ssl, pg=pg: v.tensor_mul(
                                t_b[:, c * 1024 + sh * 512:c * 1024 + (sh + 1) * 512],
                                ssl[:, :], pg),
                            waits=[(sS, t_sil)], inc=True,
                        )
                        t_sil2[par] = t_silu_done
                    if c + 2 < 32:
                        t_w1l[(e, c + 2)] = dma(
                            "scalar", w1img[:, (c % 2) * 2048:(c % 2 + 1) * 2048],
                            w1b_d[e, c + 2, :, :], gw[c % 2],
                            waits=[(sT, t_lastmm_c)],
                        )
                    if e > 0 and 2 <= c < 10:
                        j = c - 2
                        t_w2l[e] = dma(
                            "sync", w2_b[:, j * 4096:(j + 1) * 4096],
                            w2b_d[e, :, j * 4096:(j + 1) * 4096], gw2s,
                            waits=[(sT, t_mm2_done[e - 1])] if j == 0 else (),
                        )
                t_mm1_done[e] = sT.n
                if e + 1 < E:
                    sched_gload(e + 1, [(sT, t_mm1_done[e])])
                t_yev = {}
                for sm in range(8):
                    if sm == 2 and e + 1 < E:
                        sched_gtrans(e + 1)
                    for dh in range(2):
                        k2_ = 2 * sm + dh
                        bank = 2 + k2_ % 2
                        pyp = squad[:, bank * 512:(bank + 1) * 512]
                        w = [(sV, t_silu_done), (gw2s, t_w2l[e])]
                        if k2_ >= 2:
                            w.append((sV, t_yev[k2_ - 2]))
                        elif t_yev_last:
                            w.append((sV, t_yev_last))
                        for c in range(32):
                            tk = mm(
                                pyp,
                                t_b[:, c * 1024 + sm * 128:c * 1024 + (sm + 1) * 128],
                                w2_b[:, c * 1024 + dh * 512:c * 1024 + (dh + 1) * 512],
                                c == 0, c == 31, waits=w if c == 0 else (),
                                inc=(c == 31),
                            )
                        ybsl = ybuf if sm % 2 == 0 else yg
                        wv = [(sT, tk)]
                        if sm > 1:
                            wv.append((syY[sm % 2], t_ydma[(e, sm - 2)]))
                        elif e > 0:
                            wv.append((syY[sm % 2], t_ydma[(e - 1, 6 + sm)]))
                        t_yev[k2_] = vop(
                            lambda v, dh=dh, pyp=pyp, ybsl=ybsl: v.tensor_copy(
                                ybsl[:, dh * 512:(dh + 1) * 512], pyp),
                            waits=wv, inc=True,
                        )
                    t_ydma[(e, sm)] = dma(
                        "sync", y_d[e * CAP + sm * 128:e * CAP + (sm + 1) * 128, :],
                        ybsl[:, :], syY[sm % 2], waits=[(sV, t_yev[2 * sm + 1])],
                    )
                t_yev_last = t_yev[15]
                t_mm2_done[e] = sT.n
            t_ylast = t_ydma[(E - 1, 7)]

            t_o_v = [0] * NT
            t_sc_v = [0] * NT
            t_outw = [0] * NT
            t_x2l_v = [0] * NT
            ygs = [yg, yg2]
            x2s = [x2t, x2b]
            outs_ = [outt, outt2]
            for tt in range(NT):
                sl = tt % 2
                w = [(syY[0], syY[0].n), (syY[1], syY[1].n), (sV, t_desti)]
                if tt >= 2:
                    w.append((sS, t_sc_v[tt - 2]))

                def mkg(eng, tt=tt, sl=sl):
                    return eng.indirect_dma_start(
                        out=ygs[sl][:, :], out_offset=None,
                        in_=y_d[0:128, :],
                        in_offset=IndirectOffsetOnAxis(dest_i[:, tt:tt + 1], 0),
                    )
                OP("gpsimd", mkg, waits=w, incs=[(gy[sl], 16)])
                t_yg = gy[sl].n
                w = [(sc_[0], sc_[0].n), (sc_[1], sc_[1].n)]
                if tt >= 2:
                    w.append((sV, t_o_v[tt - 2]))
                t_x2l_v[tt] = dma("sync", x2s[sl][:, :],
                                  x2_d[tt * 128:(tt + 1) * 128, :],
                                  sx[sl], waits=w)
                w = [(gy[sl], t_yg)]
                if tt >= 2:
                    w.append((syOut, t_outw[tt - 2]))
                t_sc_v[tt] = sop(
                    lambda s, tt=tt, sl=sl: s.activation(
                        outs_[sl][:, :], ygs[sl][:, :], AF.Copy,
                        scale=maxprob[:, tt:tt + 1]),
                    waits=w, inc=True,
                )
                t_o_v[tt] = vop(
                    lambda v, sl=sl: v.tensor_add(
                        outs_[sl][:, :], outs_[sl][:, :], x2s[sl][:, :]),
                    waits=[(sS, t_sc_v[tt]), (sx[sl], t_x2l_v[tt])], inc=True,
                )
                t_outw[tt] = dma("sync", out_d[tt * 128:(tt + 1) * 128, :],
                                 outs_[sl][:, :], syOut,
                                 waits=[(sV, t_o_v[tt])])

        if "d_oh" in dbg:
            dma("sync", dbg["d_oh"][:, :], oh_all[:, :], syMisc,
                waits=[(sV, sV.n)])
            dma("sync", dbg["d_stats"][:, :], stats[:, :], syMisc)
            dma("sync", dbg["d_eidx"][:, :], eidx[:, :], syMisc)
        if "d_logits" in dbg:
            dma("sync", dbg["d_logits"][:, :], logits[:, :], syMisc,
                waits=[(sV, sV.n)])
            dma("sync", dbg["d_rbrep"][:, :], rbrep[:, :], syMisc)
        if "d_x2acc" in dbg:
            dma("sync", dbg["d_x2acc"][:, :], x2acc_d[:, :], syMisc,
                waits=list(t_projall))
        if "d_x2" in dbg:
            dma("sync", dbg["d_x2"][:, :], x2_d[:, :], syMisc,
                waits=[(sc_[0], sc_[0].n), (sc_[1], sc_[1].n)])
        if "d_xn2" in dbg:
            dma("sync", dbg["d_xn2"][:, :], xn2_d[:, :], syMisc,
                waits=[(sw[0], sw[0].n), (sw[1], sw[1].n)])
        if "d_g" in dbg:
            dma("sync", dbg["d_g"][:, :], g_d[:, :], syMisc,
                waits=[(gs[0], gs[0].n), (gs[1], gs[1].n)])
        if "d_y" in dbg:
            dma("sync", dbg["d_y"][:, :], y_d[:, :], syMisc,
                waits=[(syY[0], syY[0].n), (syY[1], syY[1].n)])
        if "d_router" in dbg:
            dma("sync", dbg["d_router"][:, 0:32], eidx[:, :], syMisc,
                waits=[(sV, t_desti)])
            dma("sync", dbg["d_router"][:, 32:64], maxprob[:, :], syMisc)
            dma("sync", dbg["d_router"][:, 64:96], dest_f[:, :], syMisc)

        allsems = [sQ, syMisc, syTab, syB, syG, syY[0], syY[1], syOut, gi, gb, gc, gw2s,
                   sx[0], sx[1], sa[0], sa[1], sn[0], sn[1], sw[0], sw[1],
                   sc_[0], sc_[1], gq[0], gq[1], gp[0], gp[1],
                   gw[0], gw[1], ga[0], ga[1], gs[0], gs[1], gy[0], gy[1]]
        OP("sync", lambda e: e.nop(),
           waits=[(c, c.n) for c in allsems])

        block = es.enter_context(nc.Block())

        @block.sync
        def _(eng):
            sch.replay("sync", eng)

        @block.gpsimd
        def _(eng):
            sch.replay("gpsimd", eng)

        @block.tensor
        def _(eng):
            sch.replay("tensor", eng)

        @block.vector
        def _(eng):
            sch.replay("vector", eng)

        @block.scalar
        def _(eng):
            sch.replay("scalar", eng)

    return nc


def make_inputs(L0_attn_x, L0_attn_Wqkv, L0_attn_Wproj, L0_attn_cascade_w,
                L0_ffn_router_w, L0_ffn_router_b, L0_ffn_w1, L0_ffn_w2):
    x = np.ascontiguousarray(np.asarray(L0_attn_x, dtype=np.float32))
    wqkv_r, wproj_r, rw_r, w1_r, w2_r = prep_shared(
        np.asarray(L0_attn_Wqkv, np.float32),
        np.asarray(L0_attn_Wproj, np.float32),
        np.asarray(L0_ffn_router_w, np.float32),
        np.asarray(L0_ffn_w1, np.float32),
        np.asarray(L0_ffn_w2, np.float32),
    )
    tabs = prep_tables()
    ident, bigmask, ltri, erow = prep_consts()
    rb = np.ascontiguousarray(np.asarray(L0_ffn_router_b, np.float32).reshape(1, E))
    cw = np.ascontiguousarray(np.asarray(L0_attn_cascade_w, np.float32).reshape(1, H - 1))
    shared = dict(
        wqkv=wqkv_r, wproj=wproj_r, rw=rw_r, rb=rb, cw=cw,
        w1r=w1_r, w2r=w2_r, tabs=tabs, identc=ident, bigmask=bigmask,
        ltri=ltri, erow=erow,
    )
    return x, shared


def kernel(L0_attn_x, L0_attn_Wqkv, L0_attn_Wproj, L0_attn_cascade_w,
           L0_ffn_router_w, L0_ffn_router_b, L0_ffn_w1, L0_ffn_w2):
    x, shared = make_inputs(
        L0_attn_x, L0_attn_Wqkv, L0_attn_Wproj, L0_attn_cascade_w,
        L0_ffn_router_w, L0_ffn_router_b, L0_ffn_w1, L0_ffn_w2)
    nc = build_program()
    in_maps = [dict(shared, x=np.ascontiguousarray(x[c])) for c in range(8)]
    res = run_bass_kernel_spmd(nc, in_maps, core_ids=list(range(8)))
    out = np.stack([res.results[c]["out"] for c in range(8)], axis=0)
    return out.astype(np.float32)


# version 14 (final)
# speedup vs baseline: 1.1649x; 1.0472x over previous
import numpy as np
import ml_dtypes
import concourse.bass as bass
import concourse.mybir as mybir
from concourse.bass_utils import run_bass_kernel_spmd
from concourse.bass_core_rust import IndirectOffsetOnAxis

F32 = mybir.dt.float32
BF16 = mybir.dt.bfloat16
I32 = mybir.dt.int32
U8 = mybir.dt.uint8
AF = mybir.ActivationFunctionType
ALU = mybir.AluOpType
AX = mybir.AxisListType.X

T = 4096
D = 1024
H = 16
HD = 64
NT = T // 128
E = 8
FFN = 4096
CAP = 1024
SLOTS = E * CAP
M0 = 25.0
ROPE_BASE = 10000.0
KB = 1024


class Ctr:
    __slots__ = ("h", "n")

    def __init__(self, h):
        self.h = h
        self.n = 0


class Sched:
    def __init__(self):
        self.recs = []

    def op(self, eng, fn, waits=(), incs=()):
        w = [(c.h, v) for (c, v) in waits if v > 0]
        i = []
        for c, k in incs:
            c.n += k
            i.append((c.h, k))
        self.recs.append((eng, fn, w, i))

    def replay(self, name, eng):
        known = {}
        for e, fn, waits, incs in self.recs:
            if e != name:
                continue
            for sh, v in waits:
                if known.get(id(sh), -1) < v:
                    eng.wait_ge(sh, v)
                    known[id(sh)] = v
            ins = fn(eng)
            for sh, k in incs:
                ins.then_inc(sh, k)


def _perm64():
    return np.concatenate([np.arange(0, 64, 2), np.arange(1, 64, 2)])


def prep_shared(Wqkv, Wproj, router_w, w1, w2):
    p64 = _perm64()
    cols = np.empty((H, 192), dtype=np.int64)
    for h in range(H):
        cols[h] = np.concatenate(
            [64 * h + p64, 1024 + 64 * h + p64, 2048 + 64 * h + np.arange(64)]
        )
    wq = Wqkv.reshape(8, 128, 3072)
    wqkv_r = np.ascontiguousarray(
        wq[:, :, cols].transpose(2, 1, 0, 3).reshape(H, 128, 8 * 192)
    )
    wproj_r = np.ascontiguousarray(
        Wproj.reshape(8, 128, 1024).transpose(1, 0, 2).reshape(128, 8192)
    )
    rw_r = np.ascontiguousarray(
        router_w.reshape(8, 128, 8).transpose(1, 0, 2).reshape(128, 64)
    )
    w1v = w1.reshape(E, 8, 128, 2 * FFN)
    idx = np.empty((32, 256), dtype=np.int64)
    for c in range(32):
        idx[c, :128] = c * 128 + np.arange(128)
        idx[c, 128:] = FFN + c * 128 + np.arange(128)
    w1_r = np.ascontiguousarray(
        w1v[:, :, :, idx].transpose(0, 3, 2, 1, 4).reshape(E, 32, 128, 8 * 256)
    )
    w2_r = np.ascontiguousarray(
        w2.reshape(E, 32, 128, 1024).transpose(0, 2, 1, 3).reshape(E, 128, 32 * 1024)
    )
    return wqkv_r, wproj_r, rw_r, w1_r, w2_r


def prep_tables():
    t = np.arange(T, dtype=np.float32)
    inv_freq = (1.0 / ROPE_BASE ** (np.arange(0, HD, 2, dtype=np.float32) / HD)).astype(
        np.float32
    )
    tabs = np.empty((H, 128, 4096), dtype=np.float32)

    def blk(m):
        return m.reshape(NT, 128, 32).transpose(1, 0, 2).reshape(128, NT * 32)

    for h in range(H):
        fs = np.float32(ROPE_BASE ** (h / H))
        freqs = np.outer(t, (inv_freq * fs).astype(np.float32)).astype(np.float32)
        cos = np.cos(freqs).astype(np.float32)
        sin = np.sin(freqs).astype(np.float32)
        tabs[h, :, 0:1024] = blk(cos * np.float32(0.125))
        tabs[h, :, 1024:2048] = blk(sin * np.float32(0.125))
        tabs[h, :, 2048:3072] = blk(cos)
        tabs[h, :, 3072:4096] = blk(sin)
    return tabs


def prep_consts():
    ident = np.eye(128, dtype=np.float32)
    i = np.arange(128)[:, None]
    c = np.arange(896)[None, :]
    bigmask = (i <= c - 384).astype(np.float32)
    k = np.arange(128)[:, None]
    m = np.arange(128)[None, :]
    ltri = (k < m).astype(np.float32)
    erow = np.tile(np.arange(E, dtype=np.float32), (128, 1))
    return ident, bigmask, ltri, erow


def build_program(debug=(), stages=21, dbghead=0):
    nc = bass.Bass("TRN2", target_bir_lowering=False)

    x_d = nc.dram_tensor("x", [T, D], F32, kind="ExternalInput").ap()
    wqkv_d = nc.dram_tensor("wqkv", [H, 128, 1536], F32, kind="ExternalInput").ap()
    wproj_d = nc.dram_tensor("wproj", [128, 8192], F32, kind="ExternalInput").ap()
    rw_d = nc.dram_tensor("rw", [128, 64], F32, kind="ExternalInput").ap()
    rb_d = nc.dram_tensor("rb", [1, E], F32, kind="ExternalInput").ap()
    cw_d = nc.dram_tensor("cw", [1, H - 1], F32, kind="ExternalInput").ap()
    w1_d = nc.dram_tensor("w1r", [E, 32, 128, 2048], F32, kind="ExternalInput").ap()
    w2_d = nc.dram_tensor("w2r", [E, 128, 32 * 1024], F32, kind="ExternalInput").ap()
    tab_d = nc.dram_tensor("tabs", [H, 128, 4096], F32, kind="ExternalInput").ap()
    ident_d = nc.dram_tensor("identc", [128, 128], F32, kind="ExternalInput").ap()
    mask_d = nc.dram_tensor("bigmask", [128, 896], F32, kind="ExternalInput").ap()
    ltri_d = nc.dram_tensor("ltri", [128, 128], F32, kind="ExternalInput").ap()
    erow_d = nc.dram_tensor("erow", [128, E], F32, kind="ExternalInput").ap()
    out_d = nc.dram_tensor("out", [T, D], F32, kind="ExternalOutput").ap()

    x2acc_d = nc.dram_tensor("x2acc", [T, D], F32, kind="Internal").ap()
    x2_d = nc.dram_tensor("x2s", [T, D], F32, kind="Internal").ap()
    xn2_d = nc.dram_tensor("xn2s", [T, D], BF16, kind="Internal").ap()
    g_d = nc.dram_tensor("gbuf", [SLOTS, D], BF16, kind="Internal").ap()
    y_d = nc.dram_tensor("ybufd", [SLOTS, D], F32, kind="Internal").ap()
    w1b_d = nc.dram_tensor("w1b", [E, 32, 128, 2048], BF16, kind="Internal").ap()
    w2b_d = nc.dram_tensor("w2b", [E, 128, 32 * 1024], BF16, kind="Internal").ap()
    ttot_dram = nc.dram_tensor("ttotd", [1, 256], F32, kind="Internal").ap()
    tpre_dram = nc.dram_tensor("tpred", [1, 256], F32, kind="Internal").ap()
    brow_dram = nc.dram_tensor("browd", [1, 4096], BF16, kind="Internal").ap()

    dbg = {}
    for name, shape, dt in debug:
        dbg[name] = nc.dram_tensor(name, shape, dt, kind="ExternalOutput").ap()

    from contextlib import ExitStack

    es = ExitStack()

    def sbuf(name, shape, dt):
        return es.enter_context(nc.sbuf_tensor(name, shape, dt))

    def psum(name, shape, dt):
        return es.enter_context(nc.psum_tensor(name, shape, dt))

    with es:
        ident_b = sbuf("ident_b", [128, 128], BF16)
        ident_f = sbuf("ident_f", [128, 128], F32)
        mask_b = sbuf("mask_b", [128, 896], BF16)
        ltri_b = sbuf("ltri_b", [128, 128], BF16)
        erow_f = sbuf("erow_f", [128, E], F32)
        rw_b = sbuf("rw_b", [128, 64], BF16)
        rb_row = sbuf("rb_row", [1, E], F32)
        rbrep = sbuf("rbrep", [128, E], F32)
        ones_f = sbuf("ones_f", [1, 128], F32)
        ones_bf = sbuf("ones_bf", [1, 128], BF16)
        ones128 = sbuf("ones128", [128, 1], BF16)
        cws = sbuf("cws", [1, 16], F32)
        cwsc = sbuf("cwsc", [1, 16], F32)
        csts = sbuf("csts", [128, 4], F32)
        csts2 = sbuf("csts2", [128, 1], F32)
        stats = sbuf("stats", [128, 24], F32)
        bias_acc = sbuf("bias_acc", [128, 32], F32)
        brow_f = sbuf("brow_f", [32, 128], F32)
        brow_b = sbuf("brow_b", [32, 128], BF16)
        wqkv_b = sbuf("wqkv_b", [128, 2 * 1536], BF16)
        wproj_b = sbuf("wproj_b", [128, 2 * 1024], BF16)
        v_ext = sbuf("v_ext", [128, 2 * NT * 65], BF16)
        invrep = sbuf("invrep", [128, 512], BF16)
        invr_b = sbuf("invr_b", [1, 512], BF16)
        trash = sbuf("trash", [128, 512], BF16)
        trash2 = sbuf("trash2", [128, 512], BF16)
        silu_f = sbuf("silu_f", [128, 512], F32)
        invr_f2 = silu_f
        maxprob = sbuf("maxprob", [128, 32], F32)
        eidx = sbuf("eidx", [128, 32], F32)
        oh_all = sbuf("oh_all", [128, 32 * E], BF16)
        intile = sbuf("intile", [128, 32 * E], F32)
        tprer = sbuf("tprer", [128, 32 * E], F32)
        ttot_row = sbuf("ttot_row", [1, 256], F32)
        ttot_f = sbuf("ttot_f", [32, E], F32)
        ttot_b = sbuf("ttot_b", [32, E], BF16)
        tpre_f = sbuf("tpre_f", [32, E], F32)
        tpre_row = sbuf("tpre_row", [1, 256], F32)
        dest_f = sbuf("dest_f", [128, 32], F32)
        dest_i = sbuf("dest_i", [128, 32], I32)
        xn2t = sbuf("xn2t", [128, 1024], BF16)
        xn2t2 = sbuf("xn2t2", [128, 1024], BF16)
        xn2fmt = sbuf("xn2fmt", [128, 8 * 128], BF16)
        logits = sbuf("logits", [128, 32 * E], F32)
        bparts = logits

        nc.const_aps.register(F32, -M0, csts[:, 0:1])
        nc.const_aps.register(F32, 1e-5, csts[:, 1:2])
        nc.const_aps.register(F32, -0.5, csts[:, 2:3])
        nc.const_aps.register(F32, 0.0, csts[:, 3:4])
        nc.const_aps.register(F32, -1.0, csts2[:, 0:1])

        AR = 164 * KB
        arena = es.enter_context(nc.sbuf_tensor("arena", [128, AR], U8))
        ar_base = nc.lookup_mloc(arena).addr

        def at(name, off, shape, dt):
            return nc.alloc_sbuf_tensor_at(name, shape, dt, offset=ar_base + off)

        xn_fm = at("xn_fm", 0, [128, 8 * 4096], BF16)
        costab0 = at("costab", 64 * KB, [128, 4096], F32)
        qk_tm = at("qk_tm", 80 * KB, [128, NT * 128], BF16)
        qrot = at("qrot", 88 * KB, [128, NT * 64], BF16)
        krot = at("krot", 92 * KB, [128, NT * 64], BF16)
        qfm = at("qfm", 96 * KB, [128, 4096], BF16)
        kfm = at("kfm", 104 * KB, [128, 4096], BF16)
        pT = at("pT", 112 * KB, [128, 32 * 512], BF16)
        outcat2 = at("outcat2", 144 * KB, [128, 4096], BF16)
        projpart = at("projpart", 112 * KB, [128, 2 * 2048], F32)
        ln_scr = at("ln_scr", 128 * KB, [128, 1024], F32)
        w2_b = at("w2_b", 0, [128, 32 * 1024], BF16)
        t_b = at("t_b", 64 * KB, [128, 32 * 1024], BF16)
        g_fm = at("g_fm", 128 * KB, [128, 8 * 1024], BF16)
        w1img = at("w1img", 144 * KB, [128, 2 * 2048], BF16)
        ybuf = at("ybuf", 152 * KB, [128, 1024], F32)
        yg = at("yg", 156 * KB, [128, 1024], F32)
        x2t = at("x2t", 160 * KB, [128, 1024], F32)
        outt = at("outt", 152 * KB, [128, 1024], F32)
        x2b = at("x2b", 136 * KB, [128, 1024], F32)
        outt2 = at("outt2", 128 * KB, [128, 1024], F32)
        yg2 = at("yg2", 132 * KB, [128, 1024], F32)
        zrow = trash

        pt_ps = psum("pt_ps", [128, 1024], BF16)
        squad = psum("squad", [128, 2048], F32)
        pf5 = psum("pf5", [128, 512], F32)
        pf6 = psum("pf6", [128, 512], F32)
        pf7 = psum("pf7", [128, 512], F32)

        def newsem(name):
            return Ctr(es.enter_context(nc.semaphore(name)))

        sT = newsem("sT")
        sV = newsem("sV")
        sS = newsem("sS")
        sQ = newsem("sQ")
        syMisc = newsem("syMisc")
        syTab = newsem("syTab")
        syB = newsem("syB")
        syG = newsem("syG")
        syY = [newsem("syY0"), newsem("syY1")]
        syOut = newsem("syOut")
        sx = [newsem("sx0"), newsem("sx1")]
        sa = [newsem("sa0"), newsem("sa1")]
        sn = [newsem("sn0"), newsem("sn1")]
        sw = [newsem("sw0"), newsem("sw1")]
        sc_ = [newsem("sc0"), newsem("sc1")]
        gi = newsem("gi")
        gq = [newsem("gq0"), newsem("gq1")]
        gp = [newsem("gp0"), newsem("gp1")]
        gb = newsem("gb")
        gc = newsem("gc")
        sP = newsem("sP")
        gw = [newsem("gw0"), newsem("gw1")]
        gw2s = newsem("gw2s")
        ga = [newsem("ga0"), newsem("ga1")]
        gs = [newsem("gs0"), newsem("gs1")]
        gy = [newsem("gy0"), newsem("gy1")]

        sch = Sched()
        OP = sch.op

        def dma(queue, out_ap, in_ap, ctr, waits=(), accum=False):
            if accum:
                fn = lambda eng, o=out_ap, i=in_ap: eng.dma_start(
                    out=o, in_=i, accum_op=ALU.add
                )
            else:
                fn = lambda eng, o=out_ap, i=in_ap: eng.dma_start(out=o, in_=i)
            OP(queue, fn, waits=waits, incs=[(ctr, 16)])
            return ctr.n

        def mm(out_ap, lhsT, rhs, start, stop, waits=(), inc=False):
            fn = lambda eng, o=out_ap, l=lhsT, r=rhs, a=start, b=stop: eng.matmul(
                o, l, r, start=a, stop=b
            )
            OP("tensor", fn, waits=waits, incs=[(sT, 1)] if inc else ())
            return sT.n

        def tp(out_ap, in_ap, ident, waits=(), inc=False):
            fn = lambda eng, o=out_ap, i=in_ap, d=ident: eng.transpose(o, i, d)
            OP("tensor", fn, waits=waits, incs=[(sT, 1)] if inc else ())
            return sT.n

        def vop(fn, waits=(), inc=False):
            OP("vector", fn, waits=waits, incs=[(sV, 1)] if inc else ())
            return sV.n

        def sop(fn, waits=(), inc=False):
            OP("scalar", fn, waits=waits, incs=[(sS, 1)] if inc else ())
            return sS.n

        dma("sync", ident_f[:, :], ident_d[:, :], syMisc)
        dma("sync", erow_f[:, :], erow_d[:, :], syMisc)
        dma("sync", rb_row[0:1, :], rb_d[0:1, :], syMisc)
        t_misc4 = dma("sync", cws[0:1, 0:15], cw_d[0:1, :], syMisc)
        dma("gpsimd", ident_b[:, :], ident_d[:, :], gi)
        dma("gpsimd", mask_b[:, :], mask_d[:, :], gi)
        dma("gpsimd", ltri_b[:, :], ltri_d[:, :], gi)
        t_gi = dma("gpsimd", rw_b[:, :], rw_d[:, :], gi)

        vop(lambda v: v.memset(csts[:, 0:1], -M0))
        vop(lambda v: v.memset(csts[:, 1:2], 1e-5))
        vop(lambda v: v.memset(csts[:, 2:3], -0.5))
        vop(lambda v: v.memset(csts[:, 3:4], 0.0))
        vop(lambda v: v.memset(csts2[:, 0:1], -1.0))
        vop(lambda v: v.memset(ones_f[:, :], 1.0))
        vop(lambda v: v.memset(ones_bf[:, :], 1.0))
        vop(lambda v: v.memset(ones128[:, :], 1.0))
        vop(lambda v: v.memset(bias_acc[:, :], 0.0))
        vop(lambda v: v.memset(logits[:, :], 0.0))
        vop(lambda v: v.memset(qfm[64:65, :], 0.0))
        vop(lambda v: v.memset(kfm[64:65, :], 0.0))
        vext_v = v_ext[:].rearrange("p (s t c) -> p s t c", c=65, t=NT)
        vop(lambda v: v.memset(vext_v[:, :, :, 64:65], 1.0))
        t_zrow = vop(lambda v: v.memset(zrow[:, :], 0.0), inc=True)
        t_cwsc = vop(
            lambda v: v.tensor_scalar_mul(cwsc[0:1, 0:15], cws[0:1, 0:15], 1.0 / T),
            waits=[(syMisc, t_misc4)],
            inc=True,
        )
        t_initv = t_cwsc

        t_mm_rb = mm(
            pf7[0:128, 0:E], ones_f[0:1, :], rb_row[0:1, :], True, True,
            waits=[(syMisc, t_misc4), (sV, t_initv)], inc=True,
        )
        t_rbrep = vop(
            lambda v: v.tensor_copy(rbrep[:, :], pf7[0:128, 0:E]),
            waits=[(sT, t_mm_rb)], inc=True,
        )

        g_flat = g_d.rearrange("r (a f) -> (r a) f", a=2)
        for j in range(128):
            t_gzero = dma(
                "scalar", g_flat[j * 128:(j + 1) * 128, :], zrow[:, :], sQ,
                waits=[(sV, t_zrow)] if j == 0 else (),
            )

        t_wqkv = [0] * (H + 1)
        t_tab = [0] * (H + 1)
        t_wproj = [0] * 9
        t_wqkv[0] = dma("gpsimd", wqkv_b[:, 0:1536], wqkv_d[0, :, :], gq[0])
        t_tab[0] = dma("sync", costab0[:, :], tab_d[0, :, :], syTab)
        t_wproj[0] = dma("gpsimd", wproj_b[:, 0:1024], wproj_d[:, 0:1024], gp[0])

        xn_fm_v = xn_fm[:].rearrange("p (c t) -> p c t", t=4096)

        def ln_stats(xs, c0, t_load_tok, load_ctr):
            t_sum = vop(lambda v: v.reduce_sum(stats[:, c0:c0 + 1], xs, axis=AX),
                        waits=[(load_ctr, t_load_tok)], inc=True)
            t_sq = sop(lambda s: s.activation(ln_scr[:, :], xs, AF.Square,
                                              accum_out=stats[:, c0 + 1:c0 + 2]),
                       waits=[(load_ctr, t_load_tok)], inc=True)
            t_msq = vop(lambda v: v.scalar_tensor_tensor(
                stats[:, c0 + 4:c0 + 5], stats[:, c0:c0 + 1], 1.0 / (D * D),
                stats[:, c0:c0 + 1], ALU.mult, ALU.mult),
                waits=[(sV, t_sum)], inc=True)
            vop(lambda v: v.tensor_scalar_mul(
                stats[:, c0 + 2:c0 + 3], stats[:, c0:c0 + 1], 1.0 / D))
            t_var = vop(lambda v: v.scalar_tensor_tensor(
                stats[:, c0 + 5:c0 + 6], stats[:, c0 + 1:c0 + 2], 1.0 / D,
                stats[:, c0 + 4:c0 + 5], ALU.mult, ALU.subtract),
                waits=[(sS, t_sq), (sV, t_msq)], inc=True)
            t_ln = sop(lambda s: s.activation(
                stats[:, c0 + 6:c0 + 7], stats[:, c0 + 5:c0 + 6], AF.Ln,
                bias=1e-5), waits=[(sV, t_var)], inc=True)
            t_rstd = sop(lambda s: s.activation(
                stats[:, c0 + 7:c0 + 8], stats[:, c0 + 6:c0 + 7], AF.Exp,
                scale=-0.5), waits=[(sS, t_ln)], inc=True)
            t_negmr = vop(lambda v: v.scalar_tensor_tensor(
                stats[:, c0 + 8:c0 + 9], stats[:, c0 + 2:c0 + 3], -1.0,
                stats[:, c0 + 7:c0 + 8], ALU.mult, ALU.mult),
                waits=[(sS, t_rstd)], inc=True)
            return stats[:, c0 + 7:c0 + 8], stats[:, c0 + 8:c0 + 9], t_negmr, t_rstd

        t_xload = [0] * NT
        t_xuse = [0] * NT
        t_p0evac = [0] * NT
        t_p0tp = [0] * NT
        xslots0 = [x2t, x2b]
        lnrec = at("lnrec", 144 * KB, [128, NT * 4], F32)
        xnslots0 = [xn2t, xn2t2]
        for tt in range(NT):
            slot = tt % 2
            w = [(sS, t_xuse[tt - 2])] if tt >= 2 else ()
            t_xload[tt] = dma(
                "sync", xslots0[slot][:, :],
                x_d[tt * 128:(tt + 1) * 128, :], sx[slot], waits=w,
            )
            xs = xslots0[slot][:, :]
            c0 = slot * 10
            rstd, negmr, t_negmr, t_rstdtok = ln_stats(xs, c0, t_xload[tt], sx[slot])
            if "d_lnrec" in dbg:
                vop(lambda v, c0=c0, tt=tt: v.tensor_copy(
                    lnrec[:, tt * 4:tt * 4 + 4], stats[:, c0 + 2:c0 + 6]))
            xnsl0 = xnslots0[slot]
            w = [(sV, t_negmr), (sS, t_rstdtok)]
            if tt >= 2:
                w.append((sT, t_p0tp[tt - 2]))
            t_xn = sop(
                lambda s, xs=xs, rstd=rstd, negmr=negmr, xnsl0=xnsl0: s.activation(
                    xnsl0[:, :], xs, AF.Identity, scale=rstd, bias=negmr),
                waits=w, inc=True,
            )
            t_xuse[tt] = t_xn
            w = [(sS, t_xn), (gi, t_gi)]
            if tt >= 1:
                w.append((sV, t_p0evac[tt - 1]))
            for fc in range(8):
                tk = tp(
                    pt_ps[:, fc * 128:(fc + 1) * 128],
                    xnsl0[:, fc * 128:(fc + 1) * 128],
                    ident_b[:, :], waits=w if fc == 0 else (), inc=(fc == 7),
                )
            t_p0tp[tt] = tk
            t_p0evac[tt] = vop(
                lambda v, tt=tt: v.tensor_copy(
                    xn_fm_v[:, :, tt * 128:(tt + 1) * 128],
                    pt_ps[:].rearrange("p (c t) -> p c t", t=128),
                ),
                waits=[(sT, tk)], inc=True,
            )
        t_xnfm = t_p0evac[NT - 1]
        if "d_xnfm" in dbg:
            dma("sync", dbg["d_xnfm"][:, :], xn_fm[:, :], syMisc,
                waits=[(sV, t_xnfm)])
        if "d_lnrec" in dbg:
            dma("sync", dbg["d_lnrec"][:, :], lnrec[:, :], syMisc,
                waits=[(sV, t_xnfm)])

        HEADS = 0 if stages == 0 else min(stages, 16)
        DBGHEAD = dbghead
        NT2 = NT if stages >= 20 else 0
        P3PLUS = stages >= 21
        t_desti = 0
        t_ylast = 0
        t_scatall = []
        qk_v = qk_tm[:].rearrange("p (t c) -> p t c", c=128)
        qr_v = qrot[:].rearrange("p (t c) -> p t c", c=64)
        kr_v = krot[:].rearrange("p (t c) -> p t c", c=64)
        tmp_v = xn2t[:].rearrange("p (t c) -> p t c", c=32)

        t_exp_last = 0
        t_brow = [0] * (H + 1)
        t_qrow = [0] * (H + 1)
        t_rope = [0] * H
        t_qkvmm = [0] * H
        t_fmtp = [0] * H
        t_pv_lastnorm = 0
        t_pvlast = [0] * H
        t_subA_evac = 0
        t_lastprojdma = None
        t_projmm = [0] * 8
        t_projevac = [0] * 8
        t_projdma = [[0] * 16 for _ in range(8)]
        pt_lastread = [0] * 32
        t_exp_half = {0: 0, 1: 0}
        t_vext_last = 0

        def pl31(v):
            if isinstance(v, tuple):
                return (sV if v[0] == "V" else sP, v[1])
            return (sV, v)

        t_qkve_par = [0, 0]
        t_vext_done = [0] * (H + 1)
        t_fmdone = [0] * (H + 1)
        t_fmevac_roll = 0
        t_exp_bank = [0, 0]
        pair_ctr = [0]
        t_bev_last = [0]

        def emit_qkv_group(hh, tt):
            nonlocal t_fmevac_roll
            if hh == 0:
                bank = 2 + tt % 2
                pq = squad[:, bank * 512:bank * 512 + 192]
            else:
                pq = pf7[:, 0:192]
            wq = wqkv_b[:, (hh % 2) * 1536:(hh % 2 + 1) * 1536]
            sl = hh % 2
            w = [(gq[hh % 2], t_wqkv[hh]), (sV, t_xnfm)]
            if hh == 0:
                if t_qkve_par[tt % 2]:
                    w.append((sS, t_qkve_par[tt % 2]))
            else:
                for par in range(2):
                    if t_qkve_par[par]:
                        w.append((sS, t_qkve_par[par]))
                w.append((sV, t_rbrep))
                if t_bev_last[0]:
                    w.append((sV, t_bev_last[0]))
            if tt == 0 and hh >= 2:
                w.append((sT, t_pvlast[hh - 2]))
            for fc in range(8):
                tk = mm(pq, xn_fm_v[:, fc, tt * 128:(tt + 1) * 128],
                        wq[:, fc * 192:(fc + 1) * 192], fc == 0, fc == 7,
                        waits=w if fc == 0 else (), inc=(fc == 7))
            t_qkvmm[hh] = tk
            sop(lambda s, tt=tt, pq=pq: s.activation(
                qk_v[:, tt, :], pq[:, 0:128], AF.Copy), waits=[(sT, tk)])
            tok = sop(lambda s, tt=tt, pq=pq, sl=sl: s.activation(
                vext_v[:, sl, tt, 0:64], pq[:, 128:192], AF.Copy), inc=True)
            t_qkve_par[tt % 2] = tok
            if tt == NT - 1:
                t_vext_done[hh] = tok
            return tok

        def emit_rope(hh):
            x1q, x2q = qk_v[:, :, 0:32], qk_v[:, :, 32:64]
            x1k, x2k = qk_v[:, :, 64:96], qk_v[:, :, 96:128]
            ct = costab0[:].rearrange("p (b t i) -> p b t i", b=4, i=32)
            cq, sq_, ck, sk = ct[:, 0], ct[:, 1], ct[:, 2], ct[:, 3]
            q1, q2 = qr_v[:, :, 0:32], qr_v[:, :, 32:64]
            k1, k2 = kr_v[:, :, 0:32], kr_v[:, :, 32:64]
            w = [(syTab, t_tab[hh]), (sS, t_qkve_par[(NT - 1) % 2])]
            if hh >= 1:
                w.append((sT, t_fmtp[hh - 1]))
            vop(lambda v: v.tensor_mul(q1, x1q, cq), waits=w)
            vop(lambda v: v.tensor_mul(tmp_v, x2q, sq_))
            vop(lambda v: v.tensor_sub(q1, q1, tmp_v))
            vop(lambda v: v.tensor_mul(q2, x1q, sq_))
            vop(lambda v: v.tensor_mul(tmp_v, x2q, cq))
            vop(lambda v: v.tensor_add(q2, q2, tmp_v))
            vop(lambda v: v.tensor_mul(k1, x1k, ck))
            vop(lambda v: v.tensor_mul(tmp_v, x2k, sk))
            vop(lambda v: v.tensor_sub(k1, k1, tmp_v))
            vop(lambda v: v.tensor_mul(k2, x1k, sk))
            vop(lambda v: v.tensor_mul(tmp_v, x2k, ck))
            t_rope[hh] = vop(lambda v: v.tensor_add(k2, k2, tmp_v), inc=True)
            if hh + 1 < H:
                t_tab[hh + 1] = dma(
                    "sync", costab0[:, :], tab_d[hh + 1, :, :], syTab,
                    waits=[(sV, t_rope[hh])],
                )

        def emit_fmtrans(hh):
            nonlocal t_fmevac_roll
            t_fmevac = t_fmevac_roll
            for grp in range(8):
                srcbuf = qrot if grp < 4 else krot
                dstbuf = qfm if grp < 4 else kfm
                g4 = grp % 4
                w = [(sV, t_rope[hh])]
                if t_fmevac:
                    w.append((sV, t_fmevac))
                for j in range(8):
                    tile = g4 * 8 + j
                    tk = tp(
                        pt_ps[0:64, j * 128:(j + 1) * 128],
                        srcbuf[:, tile * 64:(tile + 1) * 64],
                        ident_b[:, :],
                        waits=w if j == 0 else (), inc=(j == 7),
                    )
                t_fmevac = vop(
                    lambda v, dstbuf=dstbuf, g4=g4: v.tensor_copy(
                        dstbuf[0:64, g4 * 1024:(g4 + 1) * 1024],
                        pt_ps[0:64, :]),
                    waits=[(sT, tk)], inc=True,
                )
            t_fmtp[hh] = tk
            t_fmevac_roll = t_fmevac
            t_fmdone[hh] = t_fmevac

        for h in range(HEADS):
            if h + 1 < H:
                wv = [(sT, t_qkvmm[h - 1])] if h >= 1 else ()
                t_wqkv[h + 1] = dma(
                    "gpsimd",
                    wqkv_b[:, ((h + 1) % 2) * 1536:((h + 1) % 2 + 1) * 1536],
                    wqkv_d[h + 1, :, :], gq[(h + 1) % 2], waits=wv,
                )
            if h % 2 == 0 and h // 2 + 1 < 8:
                p = h // 2 + 1
                wv = [(sT, t_projmm[p - 2])] if p >= 2 else ()
                t_wproj[p] = dma(
                    "gpsimd", wproj_b[:, (p % 2) * 1024:(p % 2 + 1) * 1024],
                    wproj_d[:, p * 1024:(p + 1) * 1024], gp[p % 2], waits=wv,
                )
            for piece in range(3):
                pidx = h * 3 + piece
                if pidx < 32:
                    e_, cg = divmod(pidx, 4)
                    dma("gpsimd",
                        w1b_d[e_, cg * 8:(cg + 1) * 8, :, :],
                        w1_d[e_, cg * 8:(cg + 1) * 8, :, :], gc)
                elif pidx < 48:
                    e_, jg = divmod(pidx - 32, 2)
                    dma("gpsimd",
                        w2b_d[e_, :, jg * 16384:(jg + 1) * 16384],
                        w2_d[e_, :, jg * 16384:(jg + 1) * 16384], gc)

            if h == 0:
                for tt in range(NT):
                    emit_qkv_group(0, tt)
                emit_rope(0)
                emit_fmtrans(0)
                t_exp_bank[1] = t_vext_done[0]

            if h == DBGHEAD and "d_qfm" in dbg:
                dma("sync", dbg["d_qfm"][:, :], qfm[:, :], syMisc,
                    waits=[(sV, t_fmdone[h])])
                dma("sync", dbg["d_kfm"][:, :], kfm[:, :], syMisc)
                dma("sync", dbg["d_vext"][:, :], v_ext[:, :], syMisc)
                dma("sync", dbg["d_qk"][:, :], qk_tm[:, :], syMisc)

            pend_tail = None
            qkv_tt = 0 if h + 1 < H else NT

            def emit_tail(qg, nsc, tk_pv, t_maskg_last):
                nonlocal t_pv_lastnorm
                t_lnr = sop(lambda s: s.activation(
                    invr_f2[0:1, :], pf5[64:65, :], AF.Ln),
                    waits=[(sT, tk_pv)], inc=True)
                def _iexp(s):
                    with nc.allow_low_precision(reason="1/r stored bf16 as matmul operand"):
                        return s.activation(invr_b[0:1, :], invr_f2[0:1, :],
                                            AF.Exp, scale=-1.0)
                t_invb = sop(_iexp, waits=[(sS, t_lnr)], inc=True)
                w = [(sS, t_invb)]
                if h % 2 == 0 and h >= 2:
                    w.append((sV, t_projevac[h // 2 - 1]))
                t_repmm = mm(
                    pf6[0:128, :], ones_bf[0:1, :], invr_b[0:1, :], True, True,
                    waits=w, inc=True,
                )
                t_rep = vop(
                    lambda v: v.tensor_copy(invrep[:, :], pf6[0:128, :]),
                    waits=[(sT, t_repmm)], inc=True,
                )
                r0 = 64 * (h % 2)
                t_pv_lastnorm = vop(
                    lambda v, r0=r0, qg=qg: v.tensor_mul(
                        outcat2[r0:r0 + 64, qg * 512:(qg + 1) * 512],
                        pf5[0:64, :], invrep[0:64, :]),
                    waits=[(sV, t_rep)], inc=True,
                )
                for sc in range(nsc):
                    pt_lastread[sc] = ("V", vop(
                        lambda v, sc=sc, qg=qg: v.scalar_tensor_tensor(
                            trash[:, :], pT[:, sc * 512:(sc + 1) * 512],
                            1.0, invrep[:, :], ALU.mult, ALU.mult,
                            accum_out=bparts[:, sc * 8 + qg:sc * 8 + qg + 1]),
                        waits=[(sV, t_rep), (sQ, t_gzero),
                               (sP, t_maskg_last)],
                        inc=True,
                    ))

            for qg in range(8):
                nsc = 4 * qg + 4
                qsl = qfm[0:65, qg * 512:(qg + 1) * 512]
                t_expg = {}
                scg_list = list(range(0, nsc, 2))
                if nsc > 4:
                    scg_list = scg_list[-2:] + scg_list[:-2]
                for si, scg in enumerate(scg_list):
                    if qkv_tt < NT:
                        emit_qkv_group(h + 1, qkv_tt)
                        qkv_tt += 1
                        if qkv_tt == NT:
                            emit_rope(h + 1)
                    bp = pair_ctr[0] & 1
                    pair_ctr[0] += 1
                    psl = squad[:, bp * 1024:(bp + 1) * 1024]
                    for j in range(2):
                        sc = scg + j
                        w = [(sV, t_fmdone[h]), (gb, t_brow[h]),
                             (sV, t_qrow[h]), (sS, t_exp_bank[bp])]
                        mm(
                            psl[:, j * 512:(j + 1) * 512],
                            kfm[0:65, sc * 128:(sc + 1) * 128],
                            qsl, True, True,
                            waits=w if j == 0 else (), inc=(j == 1),
                        )
                    tmk = sT.n
                    if si == 0 and pend_tail is not None:
                        pend_tail()
                        pend_tail = None
                    w = [(sT, tmk)]
                    if t_lastprojdma is not None:
                        w.extend(t_lastprojdma)
                    for sc in (scg, scg + 1):
                        if pt_lastread[sc]:
                            eng_, tok_ = pt_lastread[sc]
                            w.append((sV if eng_ == "V" else sP, tok_))
                    t_expg[scg] = sop(
                        lambda s, scg=scg, psl=psl: s.activation(
                            pT[:, scg * 512:(scg + 2) * 512], psl, AF.Exp,
                            bias=-M0),
                        waits=w, inc=True,
                    )
                    t_exp_bank[bp] = t_expg[scg]
                t_exp_last = t_expg[nsc - 2]
                t_maskg = 0
                for kk in range(4):
                    sc = 4 * qg + kk
                    OP("gpsimd",
                       lambda g, sc=sc, kk=kk: g.tensor_mul(
                           pT[:, sc * 512:(sc + 1) * 512],
                           pT[:, sc * 512:(sc + 1) * 512],
                           mask_b[:, 384 - 128 * kk:896 - 128 * kk]),
                       waits=[(sS.h, t_expg[sc - (sc % 2)])] if False else
                       [(sS, t_expg[sc - (sc % 2)])],
                       incs=[(sP, 1)])
                    t_maskg = sP.n
                for sc in range(nsc):
                    w = []
                    if sc == 0:
                        w.append((sV, t_pv_lastnorm))
                        w.append((sS, t_vext_done[h]))
                        if h % 2 == 0 and h >= 2:
                            w.append((sV, t_projevac[h // 2 - 1]))
                    w.append((sS, t_expg[sc - (sc % 2)]))
                    if sc >= 4 * qg:
                        w.append((sP, t_maskg))
                    tk = mm(
                        pf5[0:65, :],
                        v_ext[:, (h % 2) * NT * 65 + sc * 65:
                              (h % 2) * NT * 65 + (sc + 1) * 65],
                        pT[:, sc * 512:(sc + 1) * 512],
                        sc == 0, sc == nsc - 1,
                        waits=w, inc=(sc == nsc - 1),
                    )
                t_pvlast[h] = tk
                pend_tail = (lambda qg=qg, nsc=nsc, tk=tk, mg=t_maskg:
                             emit_tail(qg, nsc, tk, mg))
            pend_tail()
            pend_tail = None
            if h + 1 < H:
                emit_fmtrans(h + 1)

            if h == DBGHEAD and "d_outcat" in dbg:
                t_tmpred = vop(
                    lambda v: v.tensor_reduce(
                        bias_acc[:].rearrange("p (s o) -> p s o", o=1),
                        bparts[:].rearrange("p (s q) -> p s q", q=8),
                        AX, ALU.add),
                    waits=[pl31(pt_lastread[31]), (sP, sP.n)], inc=True,
                )
                dma("sync", dbg["d_outcat"][:, :], outcat2[:, :], syMisc,
                    waits=[(sV, t_pv_lastnorm)])
                if "d_bacc" in dbg:
                    dma("sync", dbg["d_bacc"][:, :], bias_acc[:, :], syMisc,
                        waits=[(sV, t_tmpred)])
                if "d_pt" in dbg:
                    dma("sync", dbg["d_pt"][:, :], pT[:, :], syMisc)

            if h + 1 < H:
                t_bred = vop(
                    lambda v: v.tensor_reduce(
                        bias_acc[:].rearrange("p (s o) -> p s o", o=1),
                        bparts[:].rearrange("p (s q) -> p s q", q=8),
                        AX, ALU.add),
                    waits=[pl31(pt_lastread[31]), (sP, sP.n)], inc=True,
                )
                t_bz = vop(lambda v: v.memset(bparts[:, :], 0.0),
                           waits=[(sV, t_bred)], inc=True)
                tk = tp(
                    pf7[0:32, 0:128], bias_acc[:, :], ident_f[:, :],
                    waits=[(sV, t_bred), (sS, t_vext_done[h + 1])], inc=True,
                )
                t_bev = vop(
                    lambda v: v.tensor_copy(brow_b[:, :], pf7[0:32, 0:128]),
                    waits=[(sT, tk)], inc=True,
                )
                t_bev_last[0] = t_bev
                t_bd1 = dma(
                    "sync",
                    brow_dram[0:1, :].rearrange("a (s i) -> (a s) i", i=128),
                    brow_b[:, :], gb, waits=[(sV, t_bev)])
                t_brow[h + 1] = dma(
                    "sync", kfm[64:65, :], brow_dram[0:1, :], gb,
                    waits=[(gb, t_bd1)],
                )
                t_qrow[h + 1] = vop(
                    lambda v, h=h: v.tensor_scalar(
                        qfm[64:65, :], qfm[64:65, :], 0.0,
                        cwsc[0:1, h:h + 1], ALU.mult, ALU.add),
                    waits=[(sV, t_cwsc), (sT, t_pvlast[h])], inc=True,
                )

            if h % 2 == 1:
                p = h // 2
                t_ev_hist = []
                for r in range(16):
                    rslot = r % 2
                    pp = projpart[:, rslot * 2048:(rslot + 1) * 2048]
                    for idx4 in range(4):
                        ti, ofh = divmod(idx4, 2)
                        tt = r * 2 + ti
                        pb = pf5 if idx4 % 2 == 0 else pf6
                        w = [(sV, t_pv_lastnorm), (gp[p % 2], t_wproj[p]),
                             pl31(pt_lastread[31]), (sP, sP.n)]
                        if len(t_ev_hist) >= 2:
                            w.append((sV, t_ev_hist[-2]))
                        tk = mm(
                            pb[:, :],
                            outcat2[:, tt * 128:(tt + 1) * 128],
                            wproj_b[:, (p % 2) * 1024 + ofh * 512:
                                    (p % 2) * 1024 + (ofh + 1) * 512],
                            True, True, waits=w, inc=True,
                        )
                        wv = [(sT, tk)]
                        if r >= 2 and idx4 == 0:
                            wv.append((ga[r % 2], t_projdma[p][r - 2]))
                        t_ev = vop(
                            lambda v, pp=pp, ti=ti, ofh=ofh, pb=pb: v.tensor_copy(
                                pp[:, ti * 1024 + ofh * 512:
                                   ti * 1024 + (ofh + 1) * 512], pb[:, :]),
                            waits=wv, inc=True,
                        )
                        t_ev_hist.append(t_ev)
                    w = [(sV, t_ev)]
                    if p >= 1:
                        w.append((ga[r % 2], t_projdma[p - 1][r]))
                    t_projdma[p][r] = dma(
                        "gpsimd",
                        x2acc_d[r * 256:(r + 1) * 256, :].rearrange(
                            "(a p) f -> p a f", p=128),
                        pp[:, :].rearrange("p (a f) -> p a f", f=1024),
                        ga[r % 2], waits=w, accum=(p > 0),
                    )
                t_projmm[p] = tk
                t_projevac[p] = t_ev
                t_lastprojdma = [(ga[0], t_projdma[p][14]),
                                 (ga[1], t_projdma[p][15])]

        t_projall = ([(ga[0], ga[0].n), (ga[1], ga[1].n)]
                     if HEADS == H else [])

        t_w1l = {}
        t_w2l = [0] * E
        t_castall = [(gc, gc.n)]
        if NT2 and P3PLUS:
            for c in range(2):
                t_w1l[(0, c)] = dma(
                    "scalar", w1img[:, (c % 2) * 2048:(c % 2 + 1) * 2048],
                    w1b_d[0, c, :, :], gw[c % 2],
                    waits=[(sT, t_projmm[7])] + t_castall,
                )
            for j in range(8):
                t_w2l[0] = dma(
                    "sync", w2_b[:, j * 4096:(j + 1) * 4096],
                    w2b_d[0, :, j * 4096:(j + 1) * 4096], gw2s,
                    waits=([(sT, t_qkvmm[15])] + t_castall) if j == 0 else (),
                )
        t_x2w = [0] * NT
        t_xn2w = [0] * NT
        t_lgev = [0] * NT
        t_add_v = [0] * NT
        t_xnuse = [0] * NT
        t_p2evac = 0
        t_totmm = 0
        x2slots = [yg, yg2]
        xslots2 = [x2t, x2b]
        xnslots = [xn2t, xn2t2]
        for tt in range(NT2):
            slot = tt % 2
            xsl = xslots2[slot][:, :]
            asl = x2slots[slot]
            w = list(t_projall)
            if tt >= 2:
                w.append((sS, t_xuse[tt - 2]))
                w.append((sc_[slot], t_x2w[tt - 2]))
            t_xl = dma("sync", xsl, x_d[tt * 128:(tt + 1) * 128, :], sx[slot],
                       waits=w)
            w2_ = ([(sV, t_add_v[tt - 2])] if tt >= 2 else list(t_projall))
            t_al = dma("sync", asl[:, :], x2acc_d[tt * 128:(tt + 1) * 128, :],
                       sa[slot], waits=w2_)
            t_add = vop(
                lambda v, xsl=xsl, asl=asl: v.tensor_add(xsl, xsl, asl[:, :]),
                waits=[(sa[slot], t_al), (sx[slot], t_xl)], inc=True,
            )
            t_add_v[tt] = t_add
            t_x2w[tt] = dma("sync", x2_d[tt * 128:(tt + 1) * 128, :], xsl,
                            sc_[slot], waits=[(sV, t_add)])
            c0 = slot * 10
            rstd, negmr, t_negmr, t_rstdtok = ln_stats(xsl, c0, t_add, sV)
            xnsl = xnslots[slot]
            w = [(sV, t_negmr), (sS, t_rstdtok)]
            if tt >= 2:
                w.append((sT, t_xnuse[tt - 2]))
                w.append((sw[slot], t_xn2w[tt - 2]))
            t_xn = sop(
                lambda s, xsl=xsl, rstd=rstd, negmr=negmr, xnsl=xnsl: s.activation(
                    xnsl[:, :], xsl, AF.Identity, scale=rstd, bias=negmr),
                waits=w, inc=True,
            )
            t_xuse[tt] = t_xn
            t_xn2w[tt] = dma("sync", xn2_d[tt * 128:(tt + 1) * 128, :],
                             xnsl[:, :], sw[slot], waits=[(sS, t_xn)])
            w = [(sS, t_xn)]
            if tt == 0:
                w.append((sV, t_subA_evac))
            if t_p2evac:
                w.append((sV, t_p2evac))
            for fc in range(8):
                tk = tp(
                    pt_ps[:, fc * 128:(fc + 1) * 128],
                    xnsl[:, fc * 128:(fc + 1) * 128],
                    ident_b[:, :], waits=w if fc == 0 else (), inc=(fc == 7),
                )
            t_xnuse[tt] = tk
            t_p2evac = vop(
                lambda v: v.tensor_copy(
                    xn2fmt[:].rearrange("p (c t) -> p c t", t=128),
                    pt_ps[:].rearrange("p (c t) -> p c t", t=128)),
                waits=[(sT, tk)], inc=True,
            )
            rslot = tt % 2
            w = [(sV, t_p2evac), (gi, t_gi)]
            if tt == 0 and t_bev_last[0]:
                w.append((sV, t_bev_last[0]))
            if tt >= 1:
                w.append((sV, t_lgev[tt - 1]))
            for fc in range(8):
                tk = mm(
                    pf7[:, rslot * 8:rslot * 8 + 8],
                    xn2fmt[:, fc * 128:(fc + 1) * 128],
                    rw_b[:, fc * 8:(fc + 1) * 8],
                    fc == 0, fc == 7, waits=w if fc == 0 else (), inc=(fc == 7),
                )
            t_lgev[tt] = vop(
                lambda v, tt=tt, rslot=rslot: v.tensor_add(
                    logits[:, tt * 8:(tt + 1) * 8],
                    pf7[:, rslot * 8:rslot * 8 + 8], rbrep[:, :]),
                waits=[(sT, tk), (sV, t_rbrep)], inc=True,
            )
            cnm = 9 if slot == 0 else 19
            cse = 20 if slot == 0 else 21
            t_nm = vop(lambda v, tt=tt, cnm=cnm: v.reduce_max(
                stats[:, cnm:cnm + 1], logits[:, tt * 8:(tt + 1) * 8],
                axis=AX, negate=True), waits=[(sV, t_lgev[tt])], inc=True)
            t_se = sop(
                lambda s, tt=tt, cnm=cnm, cse=cse: s.activation(
                    trash[:, 0:8], logits[:, tt * 8:(tt + 1) * 8], AF.Exp,
                    bias=stats[:, cnm:cnm + 1],
                    accum_out=stats[:, cse:cse + 1]),
                waits=[(sV, t_nm)], inc=True,
            )
            vop(lambda v, tt=tt, cse=cse: v.reciprocal(
                maxprob[:, tt:tt + 1], stats[:, cse:cse + 1]),
                waits=[(sS, t_se)])
            t_oh = vop(
                lambda v, tt=tt, cnm=cnm: v.tensor_scalar(
                    oh_all[:, tt * 8:(tt + 1) * 8],
                    logits[:, tt * 8:(tt + 1) * 8],
                    stats[:, cnm:cnm + 1], 0.0, ALU.add, ALU.is_equal),
                waits=[(sV, t_nm)], inc=True,
            )
            vop(
                lambda v, tt=tt: v.scalar_tensor_tensor(
                    silu_f[:, 0:8], erow_f[:, :], 1.0,
                    oh_all[:, tt * 8:(tt + 1) * 8],
                    ALU.mult, ALU.mult, accum_out=eidx[:, tt:tt + 1]),
                waits=[(sV, t_oh)],
            )
            w = [(sV, t_oh), (gi, t_gi)]
            mm(pf6[:, tt * 8:(tt + 1) * 8], ltri_b[:, :],
               oh_all[:, tt * 8:(tt + 1) * 8], True, True, waits=w, inc=True)
            t_totmm = mm(
                pf5[0:1, tt * 8:(tt + 1) * 8], ones128[:, 0:1],
                oh_all[:, tt * 8:(tt + 1) * 8], True, True, inc=True)

        if P3PLUS:
            t_it = vop(lambda v: v.tensor_copy(intile[:, :], pf6[:, 0:256]),
                       waits=[(sT, t_totmm)], inc=True)
            t_tr = vop(lambda v: v.tensor_copy(ttot_row[0:1, :], pf5[0:1, 0:256]),
                       inc=True)
            t_td1 = dma("sync", ttot_dram[0:1, :], ttot_row[0:1, :], syB,
                        waits=[(sV, t_tr)])
            t_td2 = dma("sync", ttot_f[:, :],
                        ttot_dram[0:1, :].rearrange("a (s e) -> (a s) e", e=8),
                        syB, waits=[(syB, t_td1)])
            t_tb = vop(lambda v: v.tensor_copy(ttot_b[:, :], ttot_f[:, :]),
                       waits=[(syB, t_td2)], inc=True)
            t_tpmm = mm(pf7[0:32, 0:8], ltri_b[0:32, 0:32], ttot_b[:, :], True,
                        True, waits=[(sV, t_tb)], inc=True)
            t_tp_ = vop(lambda v: v.tensor_copy(tpre_f[:, :], pf7[0:32, 0:8]),
                        waits=[(sT, t_tpmm)], inc=True)
            t_tpd1 = dma("sync",
                         tpre_dram[0:1, :].rearrange("a (s e) -> (a s) e", e=8),
                         tpre_f[:, :], syB, waits=[(sV, t_tp_)])
            t_tpd2 = dma("sync", tpre_row[0:1, :], tpre_dram[0:1, :], syB,
                         waits=[(syB, t_tpd1)])
            for tt in range(NT):
                w = [(syB, t_tpd2), (sV, t_it)]
                t_rp = mm(
                    pf6[:, tt * 8:(tt + 1) * 8], ones_f[0:1, :],
                    tpre_row[0:1, tt * 8:(tt + 1) * 8], True, True,
                    waits=w if tt == 0 else (), inc=(tt == NT - 1),
                )
            t_rpe = vop(lambda v: v.tensor_copy(tprer[:, :], pf6[:, 0:256]),
                        waits=[(sT, t_rp)], inc=True)
            for tt in range(NT):
                vop(
                    lambda v, tt=tt: v.tensor_add(
                        intile[:, tt * 8:(tt + 1) * 8],
                        intile[:, tt * 8:(tt + 1) * 8],
                        tprer[:, tt * 8:(tt + 1) * 8]),
                    waits=[(sV, t_rpe)],
                )
            for tt in range(NT):
                vop(
                    lambda v, tt=tt: v.scalar_tensor_tensor(
                        logits[:, tt * 8:(tt + 1) * 8],
                        intile[:, tt * 8:(tt + 1) * 8], 1.0,
                        oh_all[:, tt * 8:(tt + 1) * 8],
                        ALU.mult, ALU.mult,
                        accum_out=tprer[:, tt * 8:tt * 8 + 1]),
                )
            for tt in range(NT):
                t_df = vop(
                    lambda v, tt=tt: v.tensor_scalar(
                        dest_f[:, tt:tt + 1], eidx[:, tt:tt + 1], float(CAP),
                        tprer[:, tt * 8:tt * 8 + 1], ALU.mult, ALU.add),
                    inc=(tt == NT - 1),
                )
            t_desti = vop(lambda v: v.tensor_copy(dest_i[:, :], dest_f[:, :]),
                          waits=[(sV, t_df)], inc=True)
            t_scat_v = [0] * NT
            t_xnl = [0] * NT
            for tt in range(NT):
                slot = tt % 2
                w = [(sw[0], sw[0].n), (sw[1], sw[1].n)]
                if tt >= 2:
                    w.append((gs[slot], t_scat_v[tt - 2]))
                t_xnl[tt] = dma("sync", xnslots[slot][:, :],
                                xn2_d[tt * 128:(tt + 1) * 128, :], sn[slot],
                                waits=w)

                def mk(eng, tt=tt, slot=slot):
                    return eng.indirect_dma_start(
                        out=g_d[0:128, :],
                        out_offset=IndirectOffsetOnAxis(dest_i[:, tt:tt + 1], 0),
                        in_=xnslots[slot][:, :],
                        in_offset=None,
                    )
                OP("gpsimd", mk,
                   waits=[(sn[slot], t_xnl[tt]), (sV, t_desti), (sQ, t_gzero)],
                   incs=[(gs[slot], 16)])
                t_scat_v[tt] = gs[slot].n
            t_scatall = [(gs[0], gs[0].n), (gs[1], gs[1].n)]

            g_fm_v = g_fm[:].rearrange("p (c s) -> p c s", s=1024)
            t_gload = [0] * E
            t_gtrans = [0] * E
            t_mm1_done = [0] * E
            t_mm2_done = [0] * E
            t_ydma = {}
            t_silu_done = 0
            t_sil2 = [0, 0]
            t_yev_last = 0

            def sched_gload(e, waits):
                t_gload[e] = dma(
                    "sync", g_fm[:].rearrange("p (s f) -> p s f", f=1024),
                    g_d[e * CAP:(e + 1) * CAP, :].rearrange("(s p) f -> p s f", p=128),
                    syG, waits=waits,
                )

            def sched_gtrans(e):
                pairs = [(a, b) for a in range(8) for b in range(a, 8)]
                tev = 0
                batch = []
                for pi, (a, b) in enumerate(pairs):
                    batch.append((a, b))
                    if len(batch) == 4 or pi == len(pairs) - 1:
                        w = [(syG, t_gload[e])]
                        if tev:
                            w.append((sV, tev))
                        nops = sum(1 if aa == bb else 2 for aa, bb in batch)
                        cnt = 0
                        for j, (aa, bb) in enumerate(batch):
                            cnt += 1
                            tk = tp(
                                pt_ps[:, (2 * j) * 128:(2 * j + 1) * 128],
                                g_fm[:, aa * 1024 + bb * 128:aa * 1024 + (bb + 1) * 128],
                                ident_b[:, :],
                                waits=w if cnt == 1 else (), inc=(cnt == nops),
                            )
                            if aa != bb:
                                cnt += 1
                                tk = tp(
                                    pt_ps[:, (2 * j + 1) * 128:(2 * j + 2) * 128],
                                    g_fm[:, bb * 1024 + aa * 128:bb * 1024 + (aa + 1) * 128],
                                    ident_b[:, :], inc=(cnt == nops),
                                )
                        for j, (aa, bb) in enumerate(batch):
                            tev = vop(
                                lambda v, j=j, aa=aa, bb=bb: v.tensor_copy(
                                    g_fm[:, bb * 1024 + aa * 128:bb * 1024 + (aa + 1) * 128],
                                    pt_ps[:, (2 * j) * 128:(2 * j + 1) * 128]),
                                waits=[(sT, tk)], inc=True,
                            )
                            if aa != bb:
                                tev = vop(
                                    lambda v, j=j, aa=aa, bb=bb: v.tensor_copy(
                                        g_fm[:, aa * 1024 + bb * 128:aa * 1024 + (bb + 1) * 128],
                                        pt_ps[:, (2 * j + 1) * 128:(2 * j + 2) * 128]),
                                    inc=True,
                                )
                        batch = []
                t_gtrans[e] = tev

            sched_gload(0, list(t_scatall))
            sched_gtrans(0)

            for e in range(E):
                if e > 0:
                    for c in range(2):
                        wv = []
                        if c == 0:
                            wv.append((sT, t_mm1_done[e - 1]))
                        t_w1l[(e, c)] = dma(
                            "scalar", w1img[:, (c % 2) * 2048:(c % 2 + 1) * 2048],
                            w1b_d[e, c, :, :], gw[c % 2], waits=wv,
                        )
                for c in range(32):
                    w1sl = w1img[:, (c % 2) * 2048:(c % 2 + 1) * 2048]
                    t_lastmm_c = 0
                    par = c % 2
                    pa = squad[:, par * 1024:par * 1024 + 512]
                    pg = squad[:, par * 1024 + 512:par * 1024 + 1024]
                    ssl = silu_f if par == 0 else x2t[:, 0:512]
                    for sh in range(2):
                        w = [(gw[c % 2], t_w1l[(e, c)]), (sV, t_gtrans[e])]
                        if t_sil2[par]:
                            w.append((sV, t_sil2[par]))
                        if c == 1 and t_yev_last:
                            w.append((sV, t_yev_last))
                        for fc in range(8):
                            mm(pa, w1sl[:, fc * 256:fc * 256 + 128],
                               g_fm_v[:, fc, sh * 512:(sh + 1) * 512],
                               fc == 0, fc == 7, waits=w if fc == 0 else ())
                        for fc in range(8):
                            tk = mm(pg,
                                    w1sl[:, fc * 256 + 128:fc * 256 + 256],
                                    g_fm_v[:, fc, sh * 512:(sh + 1) * 512],
                                    fc == 0, fc == 7, inc=(fc == 7))
                        t_lastmm_c = tk
                        wsl = [(sT, tk)]
                        if t_sil2[par]:
                            wsl.append((sV, t_sil2[par]))
                        t_sil = sop(
                            lambda s, pa=pa, ssl=ssl: s.activation(
                                ssl[:, :], pa, AF.Silu),
                            waits=wsl, inc=True,
                        )
                        t_silu_done = vop(
                            lambda v, c=c, sh=sh, ssl=ssl, pg=pg: v.tensor_mul(
                                t_b[:, c * 1024 + sh * 512:c * 1024 + (sh + 1) * 512],
                                ssl[:, :], pg),
                            waits=[(sS, t_sil)], inc=True,
                        )
                        t_sil2[par] = t_silu_done
                    if c + 2 < 32:
                        t_w1l[(e, c + 2)] = dma(
                            "scalar", w1img[:, (c % 2) * 2048:(c % 2 + 1) * 2048],
                            w1b_d[e, c + 2, :, :], gw[c % 2],
                            waits=[(sT, t_lastmm_c)],
                        )
                    if e > 0 and 2 <= c < 10:
                        j = c - 2
                        t_w2l[e] = dma(
                            "sync", w2_b[:, j * 4096:(j + 1) * 4096],
                            w2b_d[e, :, j * 4096:(j + 1) * 4096], gw2s,
                            waits=[(sT, t_mm2_done[e - 1])] if j == 0 else (),
                        )
                t_mm1_done[e] = sT.n
                if e + 1 < E:
                    sched_gload(e + 1, [(sT, t_mm1_done[e])])
                t_yev = {}
                for sm in range(8):
                    if sm == 2 and e + 1 < E:
                        sched_gtrans(e + 1)
                    for dh in range(2):
                        k2_ = 2 * sm + dh
                        bank = 2 + k2_ % 2
                        pyp = squad[:, bank * 512:(bank + 1) * 512]
                        w = [(sV, t_silu_done), (gw2s, t_w2l[e])]
                        if k2_ >= 2:
                            w.append((sV, t_yev[k2_ - 2]))
                        elif t_yev_last:
                            w.append((sV, t_yev_last))
                        for c in range(32):
                            tk = mm(
                                pyp,
                                t_b[:, c * 1024 + sm * 128:c * 1024 + (sm + 1) * 128],
                                w2_b[:, c * 1024 + dh * 512:c * 1024 + (dh + 1) * 512],
                                c == 0, c == 31, waits=w if c == 0 else (),
                                inc=(c == 31),
                            )
                        ybsl = ybuf if sm % 2 == 0 else yg
                        wv = [(sT, tk)]
                        if sm > 1:
                            wv.append((syY[sm % 2], t_ydma[(e, sm - 2)]))
                        elif e > 0:
                            wv.append((syY[sm % 2], t_ydma[(e - 1, 6 + sm)]))
                        t_yev[k2_] = vop(
                            lambda v, dh=dh, pyp=pyp, ybsl=ybsl: v.tensor_copy(
                                ybsl[:, dh * 512:(dh + 1) * 512], pyp),
                            waits=wv, inc=True,
                        )
                    t_ydma[(e, sm)] = dma(
                        "sync", y_d[e * CAP + sm * 128:e * CAP + (sm + 1) * 128, :],
                        ybsl[:, :], syY[sm % 2], waits=[(sV, t_yev[2 * sm + 1])],
                    )
                t_yev_last = t_yev[15]
                t_mm2_done[e] = sT.n
            t_ylast = t_ydma[(E - 1, 7)]

            t_o_v = [0] * NT
            t_sc_v = [0] * NT
            t_outw = [0] * NT
            t_x2l_v = [0] * NT
            ygs = [yg, yg2]
            x2s = [x2t, x2b]
            outs_ = [outt, outt2]
            for tt in range(NT):
                sl = tt % 2
                w = [(syY[0], syY[0].n), (syY[1], syY[1].n), (sV, t_desti)]
                if tt >= 2:
                    w.append((sS, t_sc_v[tt - 2]))

                def mkg(eng, tt=tt, sl=sl):
                    return eng.indirect_dma_start(
                        out=ygs[sl][:, :], out_offset=None,
                        in_=y_d[0:128, :],
                        in_offset=IndirectOffsetOnAxis(dest_i[:, tt:tt + 1], 0),
                    )
                OP("gpsimd", mkg, waits=w, incs=[(gy[sl], 16)])
                t_yg = gy[sl].n
                w = [(sc_[0], sc_[0].n), (sc_[1], sc_[1].n)]
                if tt >= 2:
                    w.append((sV, t_o_v[tt - 2]))
                t_x2l_v[tt] = dma("sync", x2s[sl][:, :],
                                  x2_d[tt * 128:(tt + 1) * 128, :],
                                  sx[sl], waits=w)
                w = [(gy[sl], t_yg)]
                if tt >= 2:
                    w.append((syOut, t_outw[tt - 2]))
                t_sc_v[tt] = sop(
                    lambda s, tt=tt, sl=sl: s.activation(
                        outs_[sl][:, :], ygs[sl][:, :], AF.Copy,
                        scale=maxprob[:, tt:tt + 1]),
                    waits=w, inc=True,
                )
                t_o_v[tt] = vop(
                    lambda v, sl=sl: v.tensor_add(
                        outs_[sl][:, :], outs_[sl][:, :], x2s[sl][:, :]),
                    waits=[(sS, t_sc_v[tt]), (sx[sl], t_x2l_v[tt])], inc=True,
                )
                t_outw[tt] = dma("sync", out_d[tt * 128:(tt + 1) * 128, :],
                                 outs_[sl][:, :], syOut,
                                 waits=[(sV, t_o_v[tt])])

        if "d_oh" in dbg:
            dma("sync", dbg["d_oh"][:, :], oh_all[:, :], syMisc,
                waits=[(sV, sV.n)])
            dma("sync", dbg["d_stats"][:, :], stats[:, :], syMisc)
            dma("sync", dbg["d_eidx"][:, :], eidx[:, :], syMisc)
        if "d_logits" in dbg:
            dma("sync", dbg["d_logits"][:, :], logits[:, :], syMisc,
                waits=[(sV, sV.n)])
            dma("sync", dbg["d_rbrep"][:, :], rbrep[:, :], syMisc)
        if "d_x2acc" in dbg:
            dma("sync", dbg["d_x2acc"][:, :], x2acc_d[:, :], syMisc,
                waits=list(t_projall))
        if "d_x2" in dbg:
            dma("sync", dbg["d_x2"][:, :], x2_d[:, :], syMisc,
                waits=[(sc_[0], sc_[0].n), (sc_[1], sc_[1].n)])
        if "d_xn2" in dbg:
            dma("sync", dbg["d_xn2"][:, :], xn2_d[:, :], syMisc,
                waits=[(sw[0], sw[0].n), (sw[1], sw[1].n)])
        if "d_g" in dbg:
            dma("sync", dbg["d_g"][:, :], g_d[:, :], syMisc,
                waits=[(gs[0], gs[0].n), (gs[1], gs[1].n)])
        if "d_y" in dbg:
            dma("sync", dbg["d_y"][:, :], y_d[:, :], syMisc,
                waits=[(syY[0], syY[0].n), (syY[1], syY[1].n)])
        if "d_router" in dbg:
            dma("sync", dbg["d_router"][:, 0:32], eidx[:, :], syMisc,
                waits=[(sV, t_desti)])
            dma("sync", dbg["d_router"][:, 32:64], maxprob[:, :], syMisc)
            dma("sync", dbg["d_router"][:, 64:96], dest_f[:, :], syMisc)

        allsems = [sQ, syMisc, syTab, syB, syG, syY[0], syY[1], syOut, gi, gb, gc, gw2s,
                   sx[0], sx[1], sa[0], sa[1], sn[0], sn[1], sw[0], sw[1],
                   sc_[0], sc_[1], gq[0], gq[1], gp[0], gp[1],
                   gw[0], gw[1], ga[0], ga[1], gs[0], gs[1], gy[0], gy[1]]
        OP("sync", lambda e: e.nop(),
           waits=[(c, c.n) for c in allsems])

        block = es.enter_context(nc.Block())

        @block.sync
        def _(eng):
            sch.replay("sync", eng)

        @block.gpsimd
        def _(eng):
            sch.replay("gpsimd", eng)

        @block.tensor
        def _(eng):
            sch.replay("tensor", eng)

        @block.vector
        def _(eng):
            sch.replay("vector", eng)

        @block.scalar
        def _(eng):
            sch.replay("scalar", eng)

    return nc


def make_inputs(L0_attn_x, L0_attn_Wqkv, L0_attn_Wproj, L0_attn_cascade_w,
                L0_ffn_router_w, L0_ffn_router_b, L0_ffn_w1, L0_ffn_w2):
    x = np.ascontiguousarray(np.asarray(L0_attn_x, dtype=np.float32))
    wqkv_r, wproj_r, rw_r, w1_r, w2_r = prep_shared(
        np.asarray(L0_attn_Wqkv, np.float32),
        np.asarray(L0_attn_Wproj, np.float32),
        np.asarray(L0_ffn_router_w, np.float32),
        np.asarray(L0_ffn_w1, np.float32),
        np.asarray(L0_ffn_w2, np.float32),
    )
    tabs = prep_tables()
    ident, bigmask, ltri, erow = prep_consts()
    rb = np.ascontiguousarray(np.asarray(L0_ffn_router_b, np.float32).reshape(1, E))
    cw = np.ascontiguousarray(np.asarray(L0_attn_cascade_w, np.float32).reshape(1, H - 1))
    shared = dict(
        wqkv=wqkv_r, wproj=wproj_r, rw=rw_r, rb=rb, cw=cw,
        w1r=w1_r, w2r=w2_r, tabs=tabs, identc=ident, bigmask=bigmask,
        ltri=ltri, erow=erow,
    )
    return x, shared


def kernel(L0_attn_x, L0_attn_Wqkv, L0_attn_Wproj, L0_attn_cascade_w,
           L0_ffn_router_w, L0_ffn_router_b, L0_ffn_w1, L0_ffn_w2):
    x, shared = make_inputs(
        L0_attn_x, L0_attn_Wqkv, L0_attn_Wproj, L0_attn_cascade_w,
        L0_ffn_router_w, L0_ffn_router_b, L0_ffn_w1, L0_ffn_w2)
    nc = build_program()
    in_maps = [dict(shared, x=np.ascontiguousarray(x[c])) for c in range(8)]
    res = run_bass_kernel_spmd(nc, in_maps, core_ids=list(range(8)))
    out = np.stack([res.results[c]["out"] for c in range(8)], axis=0)
    return out.astype(np.float32)
